# version 02 (final)
# baseline (speedup 1.0000x reference)
import numpy as np
import ml_dtypes
from contextlib import ExitStack

import concourse.bass as bass
import concourse.mybir as mybir
from concourse.bass_utils import run_bass_kernel_spmd

F32 = mybir.dt.float32
BF16 = mybir.dt.bfloat16
U8 = mybir.dt.uint8
I32 = mybir.dt.int32
U32 = mybir.dt.uint32
AF = mybir.ActivationFunctionType
ALU = mybir.AluOpType
AX = mybir.AxisListType
BF = ml_dtypes.bfloat16

B, T, C, NH, HD, II = 4, 4096, 2048, 16, 128, 8192
TPC = 2048
NCH = C // 128
MYH = 8
MYC = MYH * HD
EPS = 1e-5
TT = 512
GROUPS = [[0, 1], [2, 3], [4, 5], [6, 7]]
SQS = 1.0 / np.sqrt(HD)
ATTN_C0 = 4.0



class SemC:

    def __init__(self, h):
        self.h = h
        self.n = 0


class Step:
    __slots__ = ("eng", "fn", "deps", "inc", "amt")

    def __init__(self, eng, fn, deps, inc, amt):
        self.eng, self.fn, self.deps, self.inc, self.amt = eng, fn, deps, inc, amt


class Prog:
    ENGS = ("gpsimd", "vector", "scalar", "tensor", "sync")

    def __init__(self):
        self.nc = bass.Bass("TRN2", target_bir_lowering=False)
        self.stack = ExitStack()
        self._semi = 0
        self._sbi = 0
        self.phase_builders = []

    def sem(self, tag="s"):
        if not hasattr(self, "_semcache"):
            self._semcache = {}
        if tag not in self._semcache:
            self._semi += 1
            self._semcache[tag] = SemC(self.stack.enter_context(
                self.nc.semaphore(f"{tag}{self._semi}")))
        return self._semcache[tag]

    def usem(self, tag="u"):
        self._semi += 1
        return SemC(self.stack.enter_context(
            self.nc.semaphore(f"{tag}{self._semi}")))

    def sb(self, tag, shape, dt, st=None):
        self._sbi += 1
        return (st if st is not None else self.stack).enter_context(
            self.nc.sbuf_tensor(f"{tag}{self._sbi}", shape, dt))

    def ps(self, tag, shape, dt=F32, st=None):
        self._sbi += 1
        return (st if st is not None else self.stack).enter_context(
            self.nc.psum_tensor(f"{tag}{self._sbi}", shape, dt))

    def dram(self, tag, shape, dt, kind="Internal"):
        return self.nc.dram_tensor(tag, shape, dt, kind=kind)

    def phase(self, name, fn):
        self.phase_builders.append(("ph", name, fn))

    def block_begin(self, fn_alloc):
        self.phase_builders.append(("begin", None, fn_alloc))

    def block_end(self):
        self.phase_builders.append(("end", None, None))

    def build(self):
        nc = self.nc
        bar = self.stack.enter_context(nc.semaphore("BAR"))
        with self.stack:
            fence_v = self.sb("fenceV", [1, 2], F32)
            fence_s = self.sb("fenceS", [1, 2], F32)
            with nc.Block() as blk:
                barn = 0
                bstk = []
                for kind, name, fn in self.phase_builders:
                    if kind == "begin":
                        bstk.append(ExitStack())
                        fn(bstk[-1])
                        continue
                    if kind == "end":
                        bstk.pop().close()
                        continue
                    with ExitStack() as pst:
                        steps = fn(pst)
                        for engname in Prog.ENGS:
                            fsteps = [s for s in steps if s.eng == engname]
                            self._emit(blk, engname, fsteps, bar, barn,
                                       fence_v, fence_s)
                    barn += 5
                while bstk:
                    bstk.pop().close()
        return nc

    def _emit(self, blk, engname, fsteps, bar, barn, fence_v, fence_s):
        def emitter(eng):
            if barn:
                eng.wait_ge(bar, barn)
            for s in fsteps:
                for (sc, v) in s.deps:
                    eng.wait_ge(sc.h, v)
                inst = s.fn(eng)
                if s.inc is not None:
                    inst.then_inc(s.inc.h, s.amt)
            if engname == "vector":
                eng.memset(fence_v[0:1, 0:1], 0.0).then_inc(bar, 1)
            elif engname == "scalar":
                eng.copy(fence_s[0:1, 0:1], fence_s[0:1, 1:2]).then_inc(bar, 1)
            else:
                eng.sem_inc(bar, 1)
        getattr(blk, engname)(emitter)


def step(steps, eng, fn, deps=(), inc=None, amt=1):
    if inc is not None:
        inc.n += amt
    steps.append(Step(eng, fn, list(deps), inc, amt))
    return None if inc is None else inc.n


def dstep(steps, eng, fn, deps=(), inc=None):
    return step(steps, eng, fn, deps, inc, 16)


def wstep(steps, eng, sem):
    steps.append(Step(eng, lambda e: e.nop(), [(sem, sem.n)], None, 0))




class LNCtx:

    def __init__(self, p, pst, Tt, tag="ln"):
        self.Tt = Tt
        self.ybf = p.sb(tag + "ybf", [128, NCH * Tt], BF16, pst)
        self.ysq = p.sb(tag + "ysq", [128, NCH * Tt], BF16, pst)
        self.row = p.sb(tag + "row", [1, 2 * Tt], F32, pst)
        self.rowt = p.sb(tag + "rowt", [1, Tt], F32, pst)
        self.tmpn = p.sb(tag + "tmpn", [128, Tt], F32, pst)
        self.ps_st = p.ps(tag + "st", [1, 2 * Tt], F32, pst)
        self.ps_bc = p.ps(tag + "bc", [128, 2 * Tt], F32, pst)
        self.sV = p.sem(tag + "V")
        self.sS = p.sem(tag + "S")
        self.sT = p.sem(tag + "T")


def emit_ln_stats(steps, cn, ln, ychunk, dep_y):
    Tt = ln.Tt
    reuse_T = [(ln.sT, ln.sT.n)]
    reuse_SV = [(ln.sS, ln.sS.n), (ln.sV, ln.sV.n)]
    for k in range(NCH):
        step(steps, "vector",
             lambda e, k=k: e.tensor_copy(ln.ybf[:, k * Tt:(k + 1) * Tt],
                                          ychunk(k)),
             deps=(list(dep_y) + reuse_T) if k == 0 else (),
             inc=ln.sV if k == NCH - 1 else None)
        step(steps, "scalar",
             lambda e, k=k: e.activation(ln.ysq[:, k * Tt:(k + 1) * Tt],
                                         ychunk(k), AF.Square),
             deps=(list(dep_y) + reuse_T) if k == 0 else (),
             inc=ln.sS if k == NCH - 1 else None)
    vS, sS = ln.sV.n, ln.sS.n
    for k in range(NCH):
        step(steps, "tensor",
             lambda e, k=k: e.matmul(ln.ps_st[0:1, 0:Tt], cn.mean_col,
                                     ln.ybf[:, k * Tt:(k + 1) * Tt],
                                     start=(k == 0), stop=(k == NCH - 1)),
             deps=([(ln.sV, vS)] + reuse_SV) if k == 0 else ())
    for k in range(NCH):
        step(steps, "tensor",
             lambda e, k=k: e.matmul(ln.ps_st[0:1, Tt:2 * Tt], cn.mean_col,
                                     ln.ysq[:, k * Tt:(k + 1) * Tt],
                                     start=(k == 0), stop=(k == NCH - 1)),
             deps=[(ln.sS, sS)] if k == 0 else (),
             inc=ln.sT if k == NCH - 1 else None)
    tS = ln.sT.n
    step(steps, "scalar", lambda e: e.copy(ln.row[:, :], ln.ps_st[:, :]),
         deps=[(ln.sT, tS)], inc=ln.sS)
    step(steps, "scalar", lambda e: e.activation(ln.rowt[:, :], ln.row[:, 0:Tt],
                                                 AF.Square), inc=ln.sS)
    sS2 = ln.sS.n
    step(steps, "vector", lambda e: e.tensor_sub(ln.row[:, Tt:2 * Tt],
                                                 ln.row[:, Tt:2 * Tt],
                                                 ln.rowt[:, :]),
         deps=[(ln.sS, sS2)], inc=ln.sV)
    vS2 = ln.sV.n
    step(steps, "scalar", lambda e: e.activation(ln.rowt[:, :],
                                                 ln.row[:, Tt:2 * Tt],
                                                 AF.Ln,
                                                 bias=cn.eps_col[0:1, 0:1]),
         deps=[(ln.sV, vS2)])
    step(steps, "scalar", lambda e: e.activation(ln.row[:, Tt:2 * Tt],
                                                 ln.rowt[:, :],
                                                 AF.Exp, scale=-0.5),
         inc=ln.sS)
    sS3 = ln.sS.n
    nspl = (2 * Tt + 511) // 512
    for i in range(nspl):
        c0, c1 = i * 512, min((i + 1) * 512, 2 * Tt)
        step(steps, "tensor",
             lambda e, c0=c0, c1=c1: e.matmul(ln.ps_bc[:, c0:c1],
                                              cn.one_row_f32[0:1, :],
                                              ln.row[0:1, c0:c1],
                                              start=True, stop=True),
             deps=([(ln.sS, sS3)] + reuse_SV) if i == 0 else (),
             inc=ln.sT if i == nspl - 1 else None)
    return ln.sT.n




class ConstSB:
    def __init__(self, p, need_wkv=False, need_caus=False):
        self.p = p
        self.need_wkv = need_wkv
        self.need_caus = need_caus
        self.d_ones = p.dram("c_ones", [128, 2], BF16, "ExternalInput")
        self.d_onesf = p.dram("c_onesf", [1, 128], F32, "ExternalInput")
        self.d_ident = p.dram("c_ident", [128, 128], BF16, "ExternalInput")
        if need_caus:
            self.d_mcaus = p.dram("c_mcaus", [128, 4 * 512], U8, "ExternalInput")
        if need_wkv:
            self.d_wkvp = p.dram("c_wkvp", [128, 8 * 128], F32, "ExternalInput")
            self.d_par = p.dram("c_par", [128, 2 * 128], F32, "ExternalInput")
            self.d_wmask = p.dram("c_wmask", [128, 5 * 128], U8, "ExternalInput")

    def alloc(self):
        p = self.p
        sb = p.sb("c_mean", [128, 2], BF16)
        self._csb = sb
        self.mean_col = sb[:, 0:1]
        self.ones_col = sb[:, 1:2]
        fc = p.sb("c_f", [128, 4], F32)
        self.eps_col = fc[:, 0:1]
        self.negc0_col = fc[:, 1:2]
        self._fc = fc
        self.one_row_f32 = p.sb("c_or", [1, 128], F32)
        self.ident = p.sb("c_id", [128, 128], BF16)
        if self.need_caus:
            self.mcaus = p.sb("c_mc", [128, 4 * 512], U8)
        if self.need_wkv:
            self.wkvp = p.sb("c_wp", [128, 8 * 128], F32)
            self.par = p.sb("c_pr", [128, 2 * 128], F32)
            self.wmask = p.sb("c_wm", [128, 5 * 128], U8)

    def load_steps(self, steps, sem):
        step(steps, "vector", lambda e: e.memset(self._fc[:, 0:1], EPS))
        step(steps, "vector", lambda e: e.memset(self._fc[:, 1:2], -ATTN_C0))
        dstep(steps, "sync", lambda e: e.dma_start(
            out=self._csb[:, :], in_=self.d_ones[:, :]), inc=sem)
        dstep(steps, "sync", lambda e: e.dma_start(
            out=self.one_row_f32[:, :], in_=self.d_onesf[:, :]), inc=sem)
        dstep(steps, "sync", lambda e: e.dma_start(
            out=self.ident[:, :], in_=self.d_ident[:, :]), inc=sem)
        if self.need_caus:
            dstep(steps, "sync", lambda e: e.dma_start(
                out=self.mcaus[:, :], in_=self.d_mcaus[:, :]), inc=sem)
        if self.need_wkv:
            dstep(steps, "sync", lambda e: e.dma_start(
                out=self.wkvp[:, :], in_=self.d_wkvp[:, :]), inc=sem)
            dstep(steps, "sync", lambda e: e.dma_start(
                out=self.par[:, :], in_=self.d_par[:, :]), inc=sem)
            dstep(steps, "sync", lambda e: e.dma_start(
                out=self.wmask[:, :], in_=self.d_wmask[:, :]), inc=sem)
        return sem.n

    @staticmethod
    def host_inputs(need_wkv=False, need_caus=False):
        ones = np.zeros((128, 2), dtype=BF)
        ones[:, 0] = BF(1.0 / C)
        ones[:, 1] = BF(1.0)
        d = {
            "c_ones": ones,
            "c_onesf": np.ones((1, 128), dtype=np.float32),
            "c_ident": np.eye(128, dtype=np.float32).astype(BF),
        }
        if need_caus:
            m = np.zeros((128, 4, 512), dtype=np.uint8)
            kl = np.arange(128)[:, None]
            ql = np.arange(512)[None, :]
            for j in range(4):
                m[:, j, :] = (128 * j + kl <= ql)
            d["c_mcaus"] = m.reshape(128, 4 * 512)
        if need_wkv:
            t = np.arange(128, dtype=np.float32)
            pats = [t % 8 - 4, 3 - t % 8, t % 8, 7 - t % 8,
                    t - 64, 63 - t, t, 127 - t]
            wp = np.stack([np.broadcast_to(pp, (128, 128)) for pp in pats], 1)
            d["c_wkvp"] = np.ascontiguousarray(
                wp.reshape(128, 8 * 128)).astype(np.float32)
            par = np.zeros((128, 2, 128), dtype=np.float32)
            par[:, 0, :] = (t % 2 == 1)
            par[:, 1, :] = (t % 2 == 0)
            d["c_par"] = par.reshape(128, 2 * 128)
            s = np.arange(128)[:, None]
            tt = np.arange(128)[None, :]
            same8 = (s // 8 == tt // 8)
            adj8 = (s // 8 == tt // 8 - 1)
            m0 = same8 & (tt - s >= 2) & (tt - s <= 7)
            m1 = same8 & (tt - s == 1)
            m2 = adj8
            m3 = (s < tt) & ~same8 & ~adj8
            m4 = (s == tt)
            wm = np.stack([m0, m1, m2, m3, m4], 1).astype(np.uint8)
            d["c_wmask"] = np.ascontiguousarray(wm.reshape(128, 5 * 128))
        return d




def emit_lnres_store(steps, p, cn, ln, ychunk, xold_chunk, dep_xold, xout, t0,
                     sST, stage, dep_y=()):
    Tt = ln.Tt
    tS = emit_ln_stats(steps, cn, ln, ychunk, dep_y)
    for k in range(NCH):
        kk = k
        step(steps, "vector",
             lambda e, kk=kk: e.tensor_sub(ln.tmpn[:, :], ychunk(kk),
                                           ln.ps_bc[:, 0:Tt]),
             deps=([(ln.sT, tS)] + list(dep_xold)) if kk == 0 else ())
        step(steps, "vector",
             lambda e, kk=kk: e.tensor_mul(ln.tmpn[:, :], ln.tmpn[:, :],
                                           ln.ps_bc[:, Tt:2 * Tt]))
        step(steps, "vector",
             lambda e, kk=kk: e.tensor_add(stage[:, kk * Tt:(kk + 1) * Tt],
                                           ln.tmpn[:, :], xold_chunk(kk)),
             deps=[(sST, max(0, sST.n - 16 * (NCH - 1)))], inc=ln.sV)
        vv = ln.sV.n
        dstep(steps, "sync",
              lambda e, kk=kk, t0=t0: e.dma_start(
                  out=xout[kk * 128:(kk + 1) * 128, t0:t0 + Tt],
                  in_=stage[:, kk * Tt:(kk + 1) * Tt]),
              deps=[(ln.sV, vv)], inc=sST)




def phases_l1ffn(p, cn, xin, xout, pfx="l1f", first=False, dbg=False):
    wgd = p.dram(pfx + "_wg", [C, II], F32, "ExternalInput")
    wud = p.dram(pfx + "_wu", [C, II], F32, "ExternalInput")
    wdd = p.dram(pfx + "_wd", [II, C], F32, "ExternalInput")
    hreg = p.dram(pfx + "_h", [II, TPC], BF16,
                  "ExternalOutput" if dbg else "Internal")
    hraw = p.dram(pfx + "_hraw", [II, TPC], BF16,
                  "ExternalOutput" if dbg else "Internal") if dbg else None
    blk = {}

    def balloc(bst):
        blk["xhat"] = p.sb("xhat", [128, NCH * TPC], BF16, bst)
        blk["ysb"] = p.sb("ysb", [128, NCH * 1024], F32, bst)
    p.block_begin(balloc)

    def ph_ln(pst):
        steps = []
        sLD = p.sem("ld")
        if first:
            cn.load_steps(steps, sLD)
        TL = 256
        ln = LNCtx(p, pst, TL)
        xf = p.sb("xf", [128, NCH * TL], F32, pst)
        xhat = blk["xhat"]
        for t0 in range(0, TPC, TL):
            reuse = [(ln.sV, ln.sV.n), (ln.sS, ln.sS.n)]
            for k in range(NCH):
                dstep(steps, "sync", lambda e, k=k, t0=t0: e.dma_start(
                    out=xf[:, k * TL:(k + 1) * TL],
                    in_=xin[k * 128:(k + 1) * 128, t0:t0 + TL]),
                    deps=reuse if k == 0 else (), inc=sLD)
            dep = [(sLD, sLD.n)]
            tS = emit_ln_stats(steps, cn, ln,
                               lambda k: xf[:, k * TL:(k + 1) * TL], dep)
            for k in range(NCH):
                kk = k
                step(steps, "vector",
                     lambda e, kk=kk: e.tensor_sub(ln.tmpn[:, :],
                                                   xf[:, kk * TL:(kk + 1) * TL],
                                                   ln.ps_bc[:, 0:TL]),
                     deps=[(ln.sT, tS)] if kk == 0 else ())
                step(steps, "vector",
                     lambda e, kk=kk, t0=t0: e.tensor_mul(
                         xhat[:, kk * TPC + t0:kk * TPC + t0 + TL],
                         ln.tmpn[:, :], ln.ps_bc[:, TL:2 * TL]),
                     inc=ln.sV)
        return steps

    def ph_gateup(pst):
        steps = []
        xhat = blk["xhat"]
        sW = p.sem("w")
        sMM = p.sem("mm")
        sACT = p.sem("act")
        sDVE = p.sem("dve")
        sST = p.sem("st")
        wg = p.sb("wg", [128, NCH * 512], BF16, pst)
        wu = p.sb("wu", [128, NCH * 512], BF16, pst)
        gps = [p.ps("gps", [128, TT], F32, pst) for _ in range(4)]
        ups = [p.ps("ups", [128, TT], F32, pst) for _ in range(4)]
        sil = [p.sb("sil", [128, TT], F32, pst) for _ in range(4)]
        hh = [p.sb("hh", [128, TT], F32, pst) for _ in range(4)]
        aa = [p.sb("aa", [128, TT], F32, pst) for _ in range(4)]
        t1 = p.sb("t1", [128, TT], F32, pst)
        t2 = p.sb("t2", [128, TT], F32, pst)
        t3 = p.sb("t3", [128, TT], F32, pst)
        t4 = p.sb("t4", [128, TT], F32, pst)
        tu8 = p.sb("tu8", [128, TT], U8, pst)
        ku8 = p.sb("ku8", [128, TT], U8, pst)
        zero = p.sb("zero", [128, TT], F32, pst)
        hbf = [p.sb("hbf", [128, TT], BF16, pst) for _ in range(4)]
        step(steps, "vector", lambda e: e.memset(zero[:, :], 0.0))

        NSUP = II // 512
        WS = NCH * 512

        for s in range(NSUP):
            dep_mm = [(sMM, sMM.n)] if s > 0 else ()
            for k in range(NCH):
                dstep(steps, "gpsimd", lambda e, k=k, s=s: e.dma_start(
                    out=wg[:, k * 512:(k + 1) * 512],
                    in_=wgd[k * 128:(k + 1) * 128, s * 512:(s + 1) * 512]),
                    deps=dep_mm if k == 0 else (), inc=sW)
                dstep(steps, "gpsimd", lambda e, k=k, s=s: e.dma_start(
                    out=wu[:, k * 512:(k + 1) * 512],
                    in_=wud[k * 128:(k + 1) * 128, s * 512:(s + 1) * 512]),
                    inc=sW)
            wv = sW.n
            for tt in range(TPC // TT):
                t0 = tt * TT
                for r in range(4):
                    for k in range(NCH):
                        dep0 = []
                        if k == 0:
                            dep0 = [(sACT, sACT.n), (sDVE, sDVE.n)]
                            if tt == 0 and r == 0:
                                dep0.append((sW, wv))
                        step(steps, "tensor",
                             lambda e, k=k, r=r, t0=t0: e.matmul(
                                 gps[r][:, :],
                                 bass.AP(wg, k * 512 + r, [[WS, 128], [4, 128]]),
                                 xhat[:, k * TPC + t0:k * TPC + t0 + TT],
                                 start=(k == 0), stop=(k == NCH - 1)),
                             deps=dep0)
                    for k in range(NCH):
                        step(steps, "tensor",
                             lambda e, k=k, r=r, t0=t0: e.matmul(
                                 ups[r][:, :],
                                 bass.AP(wu, k * 512 + r, [[WS, 128], [4, 128]]),
                                 xhat[:, k * TPC + t0:k * TPC + t0 + TT],
                                 start=(k == 0), stop=(k == NCH - 1)),
                             inc=sMM if k == NCH - 1 else None)
                    mmv = sMM.n
                    step(steps, "scalar",
                         lambda e, r=r: e.activation(sil[r][:, :], gps[r][:, :],
                                                     AF.Silu),
                         deps=[(sMM, mmv), (sDVE, sDVE.n)], inc=sACT)
                    av = sACT.n
                    step(steps, "vector",
                         lambda e, r=r: e.tensor_mul(hh[r][:, :], sil[r][:, :],
                                                     ups[r][:, :]),
                         deps=[(sACT, av)], inc=sDVE)
                    step(steps, "scalar",
                         lambda e, r=r: e.activation(aa[r][:, :], hh[r][:, :],
                                                     AF.Abs),
                         deps=[(sDVE, sDVE.n)],
                         inc=sACT if r == 3 else None)
                absv = sACT.n
                if dbg:
                    for r in range(4):
                        step(steps, "vector",
                             lambda e, r=r: e.tensor_copy(hbf[r][:, :],
                                                          hh[r][:, :]),
                             inc=sDVE)
                        dv = sDVE.n
                        dstep(steps, "sync",
                              lambda e, r=r, s=s, t0=t0: e.dma_start(
                                  out=hraw[r * 2048 + s * 128:
                                           r * 2048 + (s + 1) * 128,
                                           t0:t0 + TT],
                                  in_=hbf[r][:, :]),
                              deps=[(sDVE, dv)], inc=sST)
                dep = [(sACT, absv)]
                step(steps, "vector", lambda e: e.tensor_max(
                    t1[:, :], aa[0][:, :], aa[1][:, :]), deps=dep)
                step(steps, "vector", lambda e: e.tensor_tensor(
                    t2[:, :], aa[0][:, :], aa[1][:, :], ALU.min))
                step(steps, "vector", lambda e: e.tensor_max(
                    t3[:, :], aa[2][:, :], aa[3][:, :]))
                step(steps, "vector", lambda e: e.tensor_tensor(
                    aa[0][:, :], aa[2][:, :], aa[3][:, :], ALU.min))
                step(steps, "vector", lambda e: e.tensor_tensor(
                    tu8[:, :], t1[:, :], t3[:, :], ALU.is_ge))
                step(steps, "vector", lambda e: e.select(
                    t4[:, :], tu8[:, :], t2[:, :], aa[0][:, :]))
                step(steps, "vector", lambda e: e.tensor_tensor(
                    t1[:, :], t1[:, :], t3[:, :], ALU.min))
                step(steps, "vector", lambda e: e.tensor_max(
                    t1[:, :], t1[:, :], t4[:, :]), inc=sDVE)
                tauv = sDVE.n
                step(steps, "scalar", lambda e: e.activation(
                    aa[0][:, :], hh[0][:, :], AF.Abs),
                    deps=[(sDVE, tauv)], inc=sACT)
                a0v = sACT.n
                for r in range(4):
                    dd = [(sACT, a0v)] if r == 0 else ()
                    step(steps, "vector",
                         lambda e, r=r: e.tensor_tensor(ku8[:, :], aa[r][:, :],
                                                        t1[:, :], ALU.is_ge),
                         deps=dd)
                    step(steps, "vector",
                         lambda e, r=r: e.select(hbf[r][:, :], ku8[:, :],
                                                 hh[r][:, :], zero[:, :]),
                         deps=[(sST, sST.n)], inc=sDVE)
                    dv = sDVE.n
                    dstep(steps, "sync",
                          lambda e, r=r, s=s, t0=t0: e.dma_start(
                              out=hreg[r * 2048 + s * 128:
                                       r * 2048 + (s + 1) * 128, t0:t0 + TT],
                              in_=hbf[r][:, :]),
                          deps=[(sDVE, dv)], inc=sST)
        wstep(steps, "sync", sST)
        return steps

    def make_ph_down(tsup):
        def ph_down(pst):
            steps = []
            sW = p.sem("wd")
            sH = p.sem("hl")
            sMM = p.sem("mmd")
            sDVE = p.sem("dv")
            TSUP = 1024
            y_sb = blk["ysb"]
            NJ = 8
            wd = p.sb("wdW", [128, NJ * C], BF16, pst)
            ht = p.sb("ht", [128, NJ * 256], BF16, pst)
            yps = p.ps("yps", [128, 8 * 256], F32, pst)
            tbase = tsup * TSUP
            step(steps, "vector", lambda e: e.memset(y_sb[:, :], 0.0), inc=sDVE)
            for isup in range(II // (NJ * 128)):
                dep_mm = [(sMM, sMM.n)] if isup > 0 else ()
                for j in range(NJ):
                    row0 = isup * NJ * 128 + j * 128
                    dstep(steps, "gpsimd", lambda e, j=j, row0=row0:
                          e.dma_start(out=wd[:, j * C:(j + 1) * C],
                                      in_=wdd[row0:row0 + 128, :]),
                          deps=dep_mm if j == 0 else (), inc=sW)
                wv = sW.n
                for t4 in range(4):
                    ts = tbase + t4 * 256
                    dep_h = [(sMM, sMM.n)]
                    for j in range(NJ):
                        row0 = isup * NJ * 128 + j * 128
                        dstep(steps, "sync", lambda e, j=j, row0=row0, ts=ts:
                              e.dma_start(out=ht[:, j * 256:(j + 1) * 256],
                                          in_=hreg[row0:row0 + 128, ts:ts + 256]),
                              deps=dep_h if j == 0 else (), inc=sH)
                    hv = sH.n
                    for half in range(2):
                        for c8 in range(8):
                            for j in range(NJ):
                                cc = half * 8 + c8
                                dep = []
                                if j == 0 and c8 == 0:
                                    dep = [(sW, wv), (sH, hv), (sDVE, sDVE.n)]
                                step(steps, "tensor",
                                     lambda e, j=j, cc=cc, c8=c8: e.matmul(
                                         yps[:, c8 * 256:(c8 + 1) * 256],
                                         wd[:, j * C + cc * 128:
                                            j * C + (cc + 1) * 128],
                                         ht[:, j * 256:(j + 1) * 256],
                                         start=(j == 0), stop=(j == NJ - 1)),
                                     deps=dep,
                                     inc=sMM if (j == NJ - 1 and c8 == 7) else None)
                        mmv = sMM.n
                        off = half * 8 * TSUP + t4 * 256
                        step(steps, "vector",
                             lambda e, off=off: e.tensor_add(
                                 bass.AP(y_sb, off,
                                         [[NCH * TSUP, 128], [TSUP, 8], [1, 256]]),
                                 bass.AP(y_sb, off,
                                         [[NCH * TSUP, 128], [TSUP, 8], [1, 256]]),
                                 bass.AP(yps, 0,
                                         [[8 * 256, 128], [256, 8], [1, 256]])),
                             deps=[(sMM, mmv)], inc=sDVE)
            return steps
        return ph_down

    def make_ph_lnres(tsup):
        def ph_lnres(pst):
            steps = []
            TSUP = 1024
            TR = 256
            y_sb = blk["ysb"]
            ln = LNCtx(p, pst, TR)
            xf = p.sb("xo", [128, NCH * TR], F32, pst)
            stage = p.sb("stg", [128, NCH * TR], F32, pst)
            sLD = p.sem("lx")
            sST = p.sem("sx")
            for tt in range(TSUP // TR):
                t0 = tsup * TSUP + tt * TR
                tl = tt * TR
                reuse = [(ln.sV, ln.sV.n)]
                for k in range(NCH):
                    dstep(steps, "sync", lambda e, k=k, t0=t0: e.dma_start(
                        out=xf[:, k * TR:(k + 1) * TR],
                        in_=xin[k * 128:(k + 1) * 128, t0:t0 + TR]),
                        deps=reuse if k == 0 else (), inc=sLD)
                ldv = sLD.n
                emit_lnres_store(
                    steps, p, cn, ln,
                    lambda k, tl=tl: y_sb[:, k * TSUP + tl:k * TSUP + tl + TR],
                    lambda k: xf[:, k * TR:(k + 1) * TR],
                    [(sLD, ldv)], xout, t0, sST, stage)
            wstep(steps, "sync", sST)
            return steps
        return ph_lnres

    def ph_dump(pst):
        steps = []
        sST = p.sem("dmp")
        xhat = blk["xhat"]
        for k in range(NCH):
            dstep(steps, "sync", lambda e, k=k: e.dma_start(
                out=blk["dxhat"][k * 128:(k + 1) * 128, :],
                in_=xhat[:, k * TPC:(k + 1) * TPC]), inc=sST)
        for k in range(NCH):
            dstep(steps, "sync", lambda e, k=k: e.dma_start(
                out=blk["dysb"][k * 128:(k + 1) * 128, :],
                in_=blk["ysb"][:, k * 1024:(k + 1) * 1024]), inc=sST)
        wstep(steps, "sync", sST)
        return steps

    p.phase(pfx + "_ln", ph_ln)
    if dbg:
        blk["dxhat"] = p.dram(pfx + "_dxhat", [C, TPC], BF16, "ExternalOutput")
        blk["dysb"] = p.dram(pfx + "_dysb", [C, 1024], F32, "ExternalOutput")
        p.phase(pfx + "_dump0", ph_dump)
    p.phase(pfx + "_gu", ph_gateup)
    for tsup in range(2):
        p.phase(pfx + f"_dn{tsup}", make_ph_down(tsup))
        if dbg and tsup == 1:
            p.phase(pfx + "_dump1", ph_dump)
        p.phase(pfx + f"_lr{tsup}", make_ph_lnres(tsup))
    p.block_end()

    def host_inputs(inputs):
        wg = np.ascontiguousarray(inputs["L1_ffn_w_gate"], dtype=np.float32)
        wu = np.ascontiguousarray(inputs["L1_ffn_w_up"], dtype=np.float32)
        wd = np.asarray(inputs["L1_ffn_w_down"], dtype=np.float32)
        perm = (4 * (np.arange(II) % 2048) + np.arange(II) // 2048)
        wd = np.ascontiguousarray(wd[perm])
        return {pfx + "_wg": wg, pfx + "_wu": wu, pfx + "_wd": wd}

    return host_inputs






def _fm_shard(x_btc, c):
    b, half = c // 2, c % 2
    sl = np.asarray(x_btc[b, half * TPC:(half + 1) * TPC, :], dtype=np.float32)
    return np.ascontiguousarray(sl.T)


def _unshard(shards):
    out = np.empty((B, T, C), dtype=np.float32)
    for c in range(8):
        b, half = c // 2, c % 2
        out[b, half * TPC:(half + 1) * TPC, :] = shards[c].T
    return out


def build_prog_l1ffn(dbg=False):
    p = Prog()
    cn = ConstSB(p)
    cn.alloc()
    xin = p.dram("xin", [C, TPC], F32, "ExternalInput")
    xout = p.dram("xout", [C, TPC], F32, "ExternalOutput")
    hin = phases_l1ffn(p, cn, xin, xout, first=True, dbg=dbg)
    nc = p.build()
    return nc, hin


_prog_cache = {}


def _get_prog(name, builder):
    if name not in _prog_cache:
        _prog_cache[name] = builder()
    return _prog_cache[name]


def run_l1ffn(x_shards, inputs, dbg=False):
    nc, hin = _get_prog("l1ffn_dbg" if dbg else "l1ffn",
                        lambda: build_prog_l1ffn(dbg))
    wmap = hin(inputs)
    cmap = ConstSB.host_inputs()
    in_maps = []
    for c in range(8):
        m = {"xin": x_shards[c]}
        m.update(wmap)
        m.update(cmap)
        in_maps.append(m)
    res = run_bass_kernel_spmd(nc, in_maps, core_ids=list(range(8)))
    if dbg:
        return res
    return [res.results[c]["xout"] for c in range(8)]


def kernel(**inputs):
    x = [_fm_shard(np.asarray(inputs["L0_attn_x"]), c) for c in range(8)]
    x = run_l0attn(x, inputs)
    x = run_l0ffn(x, inputs)
    x = run_l1attn(x, inputs)
    x = run_l1ffn(x, inputs)
    return _unshard(x)




def phases_l0ffn(p, cn, xin, xout, pfx="l0f", first=False):
    wgd = p.dram(pfx + "_wg", [C, II], I32, "ExternalInput")
    wud = p.dram(pfx + "_wu", [C, II], I32, "ExternalInput")
    wdd = p.dram(pfx + "_wd", [II, C], I32, "ExternalInput")
    sgd = p.dram(pfx + "_sg", [128, II // 128], F32, "ExternalInput")
    sud = p.dram(pfx + "_su", [128, II // 128], F32, "ExternalInput")
    sdd = p.dram(pfx + "_sd", [128, NCH], F32, "ExternalInput")
    hreg = p.dram(pfx + "_h", [II, TPC], BF16)
    blk = {}

    def balloc(bst):
        blk["xhat"] = p.sb(pfx + "xhat", [128, NCH * TPC], BF16, bst)
        blk["ysb"] = p.sb(pfx + "ysb", [128, NCH * 1024], F32, bst)
        blk["hsum"] = p.sb(pfx + "hsum", [1, TPC], F32, bst)
        blk["scales"] = p.sb(pfx + "scl", [128, 2 * (II // 128) + NCH], F32, bst)
        blk["onesbf"] = p.sb(pfx + "ob", [1, 128], BF16, bst)
    p.block_begin(balloc)
    NIC = II // 128

    def ph_ln(pst):
        steps = []
        sLD = p.sem("ld")
        if first:
            cn.load_steps(steps, sLD)
        dstep(steps, "sync", lambda e: e.dma_start(
            out=blk["scales"][:, 0:NIC], in_=sgd[:, :]), inc=sLD)
        dstep(steps, "sync", lambda e: e.dma_start(
            out=blk["scales"][:, NIC:2 * NIC], in_=sud[:, :]), inc=sLD)
        dstep(steps, "sync", lambda e: e.dma_start(
            out=blk["scales"][:, 2 * NIC:2 * NIC + NCH], in_=sdd[:, :]), inc=sLD)
        step(steps, "vector", lambda e: e.memset(blk["onesbf"][:, :], 1.0))
        TL = 256
        ln = LNCtx(p, pst, TL)
        xf = p.sb("xf", [128, NCH * TL], F32, pst)
        xhat = blk["xhat"]
        for t0 in range(0, TPC, TL):
            reuse = [(ln.sV, ln.sV.n), (ln.sS, ln.sS.n)]
            for k in range(NCH):
                dstep(steps, "sync", lambda e, k=k, t0=t0: e.dma_start(
                    out=xf[:, k * TL:(k + 1) * TL],
                    in_=xin[k * 128:(k + 1) * 128, t0:t0 + TL]),
                    deps=reuse if k == 0 else (), inc=sLD)
            dep = [(sLD, sLD.n)]
            tS = emit_ln_stats(steps, cn, ln,
                               lambda k: xf[:, k * TL:(k + 1) * TL], dep)
            for k in range(NCH):
                kk = k
                step(steps, "vector",
                     lambda e, kk=kk: e.tensor_sub(ln.tmpn[:, :],
                                                   xf[:, kk * TL:(kk + 1) * TL],
                                                   ln.ps_bc[:, 0:TL]),
                     deps=[(ln.sT, tS)] if kk == 0 else ())
                step(steps, "vector",
                     lambda e, kk=kk, t0=t0: e.tensor_mul(
                         xhat[:, kk * TPC + t0:kk * TPC + t0 + TL],
                         ln.tmpn[:, :], ln.ps_bc[:, TL:2 * TL]),
                     inc=ln.sV)
        return steps

    def ph_gateup(pst):
        steps = []
        xhat = blk["xhat"]
        scl = blk["scales"]
        sMM = p.sem("mm")
        sACT = p.sem("act")
        sDVE = p.sem("dve")
        sST = p.sem("st")
        sW = p.sem("w")
        wg = p.sb("wg", [128, NCH * 512], BF16, pst)
        wu = p.sb("wu", [128, NCH * 512], BF16, pst)
        gps = p.ps("gps", [128, TT], F32, pst)
        ups = p.ps("ups", [128, TT], F32, pst)
        hs_ps = p.ps("hsps", [1, TPC], F32, pst)
        xs_ps = p.ps("xsps", [1, TT], F32, pst)
        sil = p.sb("sil", [128, TT], F32, pst)
        hbf = p.sb("hbf2", [128, TT], BF16, pst)
        xsum = p.sb("xsum", [1, 4 * TT], BF16, pst)
        WS = NCH * 512

        for tt in range(4):
            t0 = tt * TT
            for k in range(NCH):
                step(steps, "tensor",
                     lambda e, k=k, t0=t0: e.matmul(
                         xs_ps[0:1, :], cn.ones_col,
                         xhat[:, k * TPC + t0:k * TPC + t0 + TT],
                         start=(k == 0), stop=(k == NCH - 1)),
                     deps=[(sACT, sACT.n)] if k == 0 else (),
                     inc=sMM if k == NCH - 1 else None)
            mv = sMM.n
            step(steps, "scalar",
                 lambda e, tt=tt: e.activation(xsum[0:1, tt * TT:(tt + 1) * TT],
                                               xs_ps[0:1, :], AF.Copy,
                                               scale=-128.0),
                 deps=[(sMM, mv)], inc=sACT)

        NSUP = II // 512
        for s in range(NSUP):
            dep_mm = [(sMM, sMM.n)] if s > 0 else ()
            for k in range(NCH):
                dstep(steps, "gpsimd", lambda e, k=k, s=s: e.dma_start(
                    out=wg[:, k * 512:(k + 1) * 512],
                    in_=wgd[k * 128:(k + 1) * 128, s * 512:(s + 1) * 512]),
                    deps=dep_mm if k == 0 else (), inc=sW)
                dstep(steps, "gpsimd", lambda e, k=k, s=s: e.dma_start(
                    out=wu[:, k * 512:(k + 1) * 512],
                    in_=wud[k * 128:(k + 1) * 128, s * 512:(s + 1) * 512]),
                    inc=sW)
            wv = sW.n
            for tt in range(TPC // TT):
                t0 = tt * TT
                for ic in range(4):
                    icg = s * 4 + ic
                    for k in range(NCH):
                        dep0 = []
                        if k == 0:
                            dep0 = [(sACT, sACT.n), (sDVE, sDVE.n), (sW, wv)]
                        step(steps, "tensor",
                             lambda e, k=k, ic=ic, t0=t0: e.matmul(
                                 gps[:, :],
                                 wg[:, k * 512 + ic * 128:k * 512 + (ic + 1) * 128],
                                 xhat[:, k * TPC + t0:k * TPC + t0 + TT],
                                 start=(k == 0), stop=False),
                             deps=dep0)
                    step(steps, "tensor",
                         lambda e, tt=tt, t0=t0: e.matmul(
                             gps[:, :], blk["onesbf"][0:1, :],
                             xsum[0:1, t0:t0 + TT],
                             start=False, stop=True))
                    for k in range(NCH):
                        step(steps, "tensor",
                             lambda e, k=k, ic=ic, t0=t0: e.matmul(
                                 ups[:, :],
                                 wu[:, k * 512 + ic * 128:k * 512 + (ic + 1) * 128],
                                 xhat[:, k * TPC + t0:k * TPC + t0 + TT],
                                 start=(k == 0), stop=False))
                    step(steps, "tensor",
                         lambda e, tt=tt, t0=t0: e.matmul(
                             ups[:, :], blk["onesbf"][0:1, :],
                             xsum[0:1, t0:t0 + TT],
                             start=False, stop=True),
                         inc=sMM)
                    mmv = sMM.n
                    step(steps, "scalar",
                         lambda e, icg=icg: e.activation(
                             sil[:, :], gps[:, :], AF.Silu,
                             scale=scl[:, icg:icg + 1]),
                         deps=[(sMM, mmv), (sDVE, sDVE.n)], inc=sACT)
                    av = sACT.n
                    step(steps, "vector",
                         lambda e, icg=icg: e.scalar_tensor_tensor(
                             hbf[:, :], ups[:, :],
                             scl[:, NIC + icg:NIC + icg + 1], sil[:, :],
                             ALU.mult, ALU.mult),
                         deps=[(sACT, av), (sMM, sMM.n), (sST, sST.n)],
                         inc=sDVE)
                    dv = sDVE.n
                    step(steps, "tensor",
                         lambda e, t0=t0: e.matmul(
                             hs_ps[0:1, t0:t0 + TT], cn.ones_col, hbf[:, :],
                             start=(s == 0 and ic == 0),
                             stop=(s == NSUP - 1 and ic == 3)),
                         deps=[(sDVE, dv)],
                         inc=sMM)
                    dstep(steps, "sync",
                          lambda e, icg=icg, t0=t0: e.dma_start(
                              out=hreg[icg * 128:(icg + 1) * 128, t0:t0 + TT],
                              in_=hbf[:, :]),
                          deps=[(sDVE, dv)], inc=sST)
        mmv = sMM.n
        step(steps, "scalar",
             lambda e: e.activation(blk["hsum"][0:1, :], hs_ps[0:1, :],
                                    AF.Copy, scale=128.0),
             deps=[(sMM, mmv)], inc=sACT)
        wstep(steps, "sync", sST)
        return steps

    def make_ph_down(tsup):
        def ph_down(pst):
            steps = []
            sW = p.sem("wd")
            sH = p.sem("hl")
            sMM = p.sem("mmd")
            sDVE = p.sem("dv")
            TSUP = 1024
            y_sb = blk["ysb"]
            NJ = 8
            wd = p.sb("wdW", [128, NJ * C], BF16, pst)
            ht = p.sb("ht", [128, NJ * 256], BF16, pst)
            yps = p.ps("yps", [128, 8 * 256], F32, pst)
            tbase = tsup * TSUP
            step(steps, "vector", lambda e: e.memset(y_sb[:, :], 0.0), inc=sDVE)
            for isup in range(II // (NJ * 128)):
                dep_mm = [(sMM, sMM.n)] if isup > 0 else ()
                for j in range(NJ):
                    row0 = isup * NJ * 128 + j * 128
                    dstep(steps, "gpsimd", lambda e, j=j, row0=row0:
                          e.dma_start(out=wd[:, j * C:(j + 1) * C],
                                      in_=wdd[row0:row0 + 128, :]),
                          deps=dep_mm if j == 0 else (), inc=sW)
                wv = sW.n
                for t4 in range(4):
                    ts = tbase + t4 * 256
                    dep_h = [(sMM, sMM.n)]
                    for j in range(NJ):
                        row0 = isup * NJ * 128 + j * 128
                        dstep(steps, "sync", lambda e, j=j, row0=row0, ts=ts:
                              e.dma_start(out=ht[:, j * 256:(j + 1) * 256],
                                          in_=hreg[row0:row0 + 128, ts:ts + 256]),
                              deps=dep_h if j == 0 else (), inc=sH)
                    hv = sH.n
                    for half in range(2):
                        for c8 in range(8):
                            for j in range(NJ):
                                cc = half * 8 + c8
                                dep = []
                                if j == 0 and c8 == 0:
                                    dep = [(sW, wv), (sH, hv), (sDVE, sDVE.n)]
                                step(steps, "tensor",
                                     lambda e, j=j, cc=cc, c8=c8: e.matmul(
                                         yps[:, c8 * 256:(c8 + 1) * 256],
                                         wd[:, j * C + cc * 128:
                                            j * C + (cc + 1) * 128],
                                         ht[:, j * 256:(j + 1) * 256],
                                         start=(j == 0), stop=(j == NJ - 1)),
                                     deps=dep,
                                     inc=sMM if (j == NJ - 1 and c8 == 7) else None)
                        mmv = sMM.n
                        off = half * 8 * TSUP + t4 * 256
                        step(steps, "vector",
                             lambda e, off=off: e.tensor_add(
                                 bass.AP(y_sb, off,
                                         [[NCH * TSUP, 128], [TSUP, 8], [1, 256]]),
                                 bass.AP(y_sb, off,
                                         [[NCH * TSUP, 128], [TSUP, 8], [1, 256]]),
                                 bass.AP(yps, 0,
                                         [[8 * 256, 128], [256, 8], [1, 256]])),
                             deps=[(sMM, mmv)], inc=sDVE)
            return steps
        return ph_down

    def make_ph_lnres(tsup):
        def ph_lnres(pst):
            steps = []
            TSUP = 1024
            TR = 128
            y_sb = blk["ysb"]
            scl = blk["scales"]
            ln = LNCtx(p, pst, TR)
            xf = p.sb("xo", [128, NCH * TR], F32, pst)
            yc = p.sb("yc", [128, NCH * TR], F32, pst)
            stage = p.sb("stg", [128, NCH * TR], F32, pst)
            hs_bc = p.ps("hsbc", [128, TR], F32, pst)
            sLD = p.sem("lx")
            sST = p.sem("sx")
            sTb = p.sem("tb")
            sVc = p.sem("vc")
            for tt in range(TSUP // TR):
                t0 = tsup * TSUP + tt * TR
                tl = tt * TR
                reuse = [(ln.sV, ln.sV.n)]
                for k in range(NCH):
                    dstep(steps, "sync", lambda e, k=k, t0=t0: e.dma_start(
                        out=xf[:, k * TR:(k + 1) * TR],
                        in_=xin[k * 128:(k + 1) * 128, t0:t0 + TR]),
                        deps=reuse if k == 0 else (), inc=sLD)
                ldv = sLD.n
                step(steps, "tensor",
                     lambda e, t0=t0: e.matmul(
                         hs_bc[:, :], cn.one_row_f32[0:1, :],
                         blk["hsum"][0:1, t0:t0 + TR],
                         start=True, stop=True),
                     deps=[(sVc, sVc.n)], inc=sTb)
                tbv = sTb.n
                for k in range(NCH):
                    step(steps, "vector",
                         lambda e, k=k, tl=tl: e.tensor_sub(
                             yc[:, k * TR:(k + 1) * TR],
                             y_sb[:, k * TSUP + tl:k * TSUP + tl + TR],
                             hs_bc[:, :]),
                         deps=[(sTb, tbv)] if k == 0 else ())
                    step(steps, "vector",
                         lambda e, k=k: e.tensor_scalar_mul(
                             yc[:, k * TR:(k + 1) * TR],
                             yc[:, k * TR:(k + 1) * TR],
                             scl[:, 2 * NIC + k:2 * NIC + k + 1]),
                         inc=sVc)
                emit_lnres_store(
                    steps, p, cn, ln,
                    lambda k: yc[:, k * TR:(k + 1) * TR],
                    lambda k: xf[:, k * TR:(k + 1) * TR],
                    [(sLD, ldv)], xout, t0, sST, stage,
                    dep_y=[(sVc, sVc.n)])
            wstep(steps, "sync", sST)
            return steps
        return ph_lnres

    p.phase(pfx + "_ln", ph_ln)
    p.phase(pfx + "_gu", ph_gateup)
    for tsup in range(2):
        p.phase(pfx + f"_dn{tsup}", make_ph_down(tsup))
        p.phase(pfx + f"_lr{tsup}", make_ph_lnres(tsup))
    p.block_end()

    def host_inputs(inputs):
        wg = np.ascontiguousarray(
            np.asarray(inputs["L0_ffn_gate_w"], dtype=np.int32).T)
        wu = np.ascontiguousarray(
            np.asarray(inputs["L0_ffn_up_w"], dtype=np.int32).T)
        wd = np.ascontiguousarray(
            np.asarray(inputs["L0_ffn_down_w"], dtype=np.int32).T)
        sg = np.ascontiguousarray(np.asarray(
            inputs["L0_ffn_gate_scale"], np.float32).reshape(II // 128, 128).T)
        su = np.ascontiguousarray(np.asarray(
            inputs["L0_ffn_up_scale"], np.float32).reshape(II // 128, 128).T)
        sd = np.ascontiguousarray(np.asarray(
            inputs["L0_ffn_down_scale"], np.float32).reshape(NCH, 128).T)
        return {pfx + "_wg": wg, pfx + "_wu": wu, pfx + "_wd": wd,
                pfx + "_sg": sg, pfx + "_su": su, pfx + "_sd": sd}

    return host_inputs


def build_prog_l0ffn():
    p = Prog()
    cn = ConstSB(p)
    cn.alloc()
    xin = p.dram("xin", [C, TPC], F32, "ExternalInput")
    xout = p.dram("xout", [C, TPC], F32, "ExternalOutput")
    hin = phases_l0ffn(p, cn, xin, xout, first=True)
    nc = p.build()
    return nc, hin


def run_l0ffn(x_shards, inputs):
    nc, hin = _get_prog("l0ffn", build_prog_l0ffn)
    wmap = hin(inputs)
    cmap = ConstSB.host_inputs()
    in_maps = []
    for c in range(8):
        m = {"xin": x_shards[c]}
        m.update(wmap)
        m.update(cmap)
        in_maps.append(m)
    res = run_bass_kernel_spmd(nc, in_maps, core_ids=list(range(8)))
    return [res.results[c]["xout"] for c in range(8)]




def phases_l1attn(p, cn, xin, xout, pfx="l1a", first=False):
    wqkvd = p.dram(pfx + "_wqkv", [C, 3 * MYC], F32, "ExternalInput")
    bqkvd = p.dram(pfx + "_bqkv", [128, 24], F32, "ExternalInput")
    wod = p.dram(pfx + "_wo", [C, C], F32, "ExternalInput")
    bod = p.dram(pfx + "_bo", [128, NCH], F32, "ExternalInput")
    t0d = p.dram(pfx + "_t0", [1, 1], U32, "ExternalInput")
    ag_in = p.dram(pfx + "_agi", [C, TPC], BF16)
    ag_out = p.dram(pfx + "_ago", [2 * C, TPC], BF16)
    qkvreg = p.dram(pfx + "_qkv", [3 * MYC, T], BF16)
    ogreg = p.dram(pfx + "_og", [MYC, T], BF16)
    og_gath = p.dram(pfx + "_ogg", [2 * MYC, T], BF16)
    og_loc = p.dram(pfx + "_ogl", [C, TPC], BF16)

    blk = {}

    def balloc(bst):
        blk["bq"] = p.sb(pfx + "bq", [128, 24 + NCH], F32, bst)
        blk["t0s"] = p.sb(pfx + "t0s", [1, 1], U32, bst)
    p.block_begin(balloc)
    blk["r_t0"] = p.stack.enter_context(p.nc.gpsimd.register(pfx + "rt0"))
    blk["r_tmp"] = p.stack.enter_context(p.nc.gpsimd.register(pfx + "rtmp"))

    def ph_lng(pst):
        steps = []
        sLD = p.sem("ld")
        sST = p.sem("st")
        sCC = p.usem("cc")
        if first:
            cn.load_steps(steps, sLD)
        dstep(steps, "sync", lambda e: e.dma_start(
            out=blk["bq"][:, 0:24], in_=bqkvd[:, :]), inc=sLD)
        dstep(steps, "sync", lambda e: e.dma_start(
            out=blk["bq"][:, 24:24 + NCH], in_=bod[:, :]), inc=sLD)
        dstep(steps, "gpsimd", lambda e: e.dma_start(
            out=blk["t0s"][:, :], in_=t0d[:, :]), inc=sLD)
        ldt = sLD.n
        step(steps, "gpsimd",
             lambda e: e.reg_load(blk["r_t0"], blk["t0s"][0:1, 0:1]),
             deps=[(sLD, ldt)])
        TL = 256
        ln = LNCtx(p, pst, TL)
        xf = p.sb("xf", [128, NCH * TL], F32, pst)
        xbstage = p.sb("xbs", [128, NCH * TL], BF16, pst)
        for t0 in range(0, TPC, TL):
            reuse = [(ln.sV, ln.sV.n), (ln.sS, ln.sS.n), (sST, sST.n)]
            for k in range(NCH):
                dstep(steps, "sync", lambda e, k=k, t0=t0: e.dma_start(
                    out=xf[:, k * TL:(k + 1) * TL],
                    in_=xin[k * 128:(k + 1) * 128, t0:t0 + TL]),
                    deps=reuse if k == 0 else (), inc=sLD)
            dep = [(sLD, sLD.n)]
            tS = emit_ln_stats(steps, cn, ln,
                               lambda k: xf[:, k * TL:(k + 1) * TL], dep)
            for k in range(NCH):
                kk = k
                step(steps, "vector",
                     lambda e, kk=kk: e.tensor_sub(ln.tmpn[:, :],
                                                   xf[:, kk * TL:(kk + 1) * TL],
                                                   ln.ps_bc[:, 0:TL]),
                     deps=([(ln.sT, tS), (sST, sST.n)] if kk == 0 else ()))
                step(steps, "vector",
                     lambda e, kk=kk: e.tensor_mul(
                         xbstage[:, kk * TL:(kk + 1) * TL],
                         ln.tmpn[:, :], ln.ps_bc[:, TL:2 * TL]),
                     inc=ln.sV)
                vv = ln.sV.n
                dstep(steps, "sync",
                      lambda e, kk=kk, t0=t0: e.dma_start(
                          out=ag_in[kk * 128:(kk + 1) * 128, t0:t0 + TL],
                          in_=xbstage[:, kk * TL:(kk + 1) * TL]),
                      deps=[(ln.sV, vv)], inc=sST)
        wstep(steps, "sync", sST)
        step(steps, "gpsimd", lambda e: e.collective_compute(
            "AllGather", ALU.bypass, replica_groups=GROUPS,
            ins=[ag_in.ap().opt()], outs=[ag_out.ap().opt()]),
            deps=[(sST, sST.n)], inc=sCC)
        steps.append(Step("gpsimd", lambda e: e.nop(), [(sCC, 1)], None, 0))
        return steps

    def ph_qkv(pst):
        steps = []
        sW = p.sem("w")
        sX = p.sem("x")
        sMM = p.sem("mm")
        sACT = p.sem("act")
        sST = p.sem("st")
        wq = p.sb("wqkv", [128, NCH * 3 * MYC // 16 * 16], BF16, pst)
        xg = p.sb("xg", [128, NCH * TT], BF16, pst)
        og1 = p.sb("og1", [128, TT], BF16, pst)
        og2 = p.sb("og2", [128, TT], BF16, pst)
        pss = [p.ps("qps", [128, TT], F32, pst) for _ in range(2)]
        for k in range(NCH):
            dstep(steps, "gpsimd", lambda e, k=k: e.dma_start(
                out=wq[:, k * 3072:(k + 1) * 3072],
                in_=wqkvd[k * 128:(k + 1) * 128, :]), inc=sW)
        wv = sW.n
        stages = [og1, og2]
        for tt in range(T // TT):
            slab, tl = tt // 4, (tt % 4) * TT
            for k in range(NCH):
                dstep(steps, "sync", lambda e, k=k, slab=slab, tl=tl: e.dma_start(
                    out=xg[:, k * TT:(k + 1) * TT],
                    in_=ag_out[slab * C + k * 128:slab * C + (k + 1) * 128,
                               tl:tl + TT]),
                    deps=[(sMM, sMM.n)] if k == 0 else (), inc=sX)
            xv = sX.n
            for oc in range(24):
                pb = (tt * 24 + oc) % 2
                for k in range(NCH):
                    dep = []
                    if k == 0:
                        dep = [(sX, xv), (sACT, max(0, sACT.n - 1))]
                        if tt == 0 and oc == 0:
                            dep.append((sW, wv))
                    step(steps, "tensor",
                         lambda e, k=k, oc=oc, pb=pb: e.matmul(
                             pss[pb][:, :],
                             wq[:, k * 3072 + oc * 128:k * 3072 + (oc + 1) * 128],
                             xg[:, k * TT:(k + 1) * TT],
                             start=(k == 0), stop=(k == NCH - 1)),
                         deps=dep, inc=sMM if k == NCH - 1 else None)
                mmv = sMM.n
                step(steps, "vector",
                     lambda e, oc=oc, pb=pb: e.tensor_scalar_add(
                         stages[pb][:, :], pss[pb][:, :],
                         blk["bq"][:, oc:oc + 1]),
                     deps=[(sMM, mmv), (sST, max(0, sST.n - 16))], inc=sACT)
                av = sACT.n
                dstep(steps, "sync",
                      lambda e, oc=oc, tt=tt, pb=pb: e.dma_start(
                          out=qkvreg[oc * 128:(oc + 1) * 128,
                                     tt * TT:(tt + 1) * TT],
                          in_=stages[pb][:, :]),
                      deps=[(sACT, av)], inc=sST)
        wstep(steps, "sync", sST)
        return steps

    def ph_attn(pst):
        steps = []
        sL = p.sem("l")
        sMM = p.sem("mm")
        sACT = p.sem("act")
        sDVE = p.sem("dve")
        sST = p.sem("st")
        sTR = p.sem("tr")
        qf = p.sb("qf", [128, T], BF16, pst)
        kf = p.sb("kf", [128, T], BF16, pst)
        vf = p.sb("vf", [128, T], BF16, pst)
        vt = p.sb("vt", [128, T], BF16, pst)
        ogs = p.sb("ogs", [128, T], BF16, pst)
        pt = [p.sb("pt", [128, TT], BF16, pst) for _ in range(2)]
        ptm = p.sb("ptm", [128, TT], BF16, pst)
        zbf = p.sb("zbf", [128, TT], BF16, pst)
        srow = p.sb("srow", [1, TT], F32, pst)
        rrow = p.sb("rrow", [1, TT], F32, pst)
        rbsb = p.sb("rbsb", [128, TT], F32, pst)
        ps_s = [p.ps("sps", [128, TT], F32, pst) for _ in range(2)]
        ps_o = p.ps("ops", [128, TT], F32, pst)
        ps_sg = p.ps("sgps", [1, TT], F32, pst)
        ps_rb = p.ps("rbps", [128, TT], F32, pst)
        ps_tr = p.ps("trps", [128, 128], BF16, pst)
        step(steps, "vector", lambda e: e.memset(zbf[:, :], 0.0))
        for hl in range(MYH):
            dep_re = [(sMM, sMM.n), (sDVE, sDVE.n)]
            dstep(steps, "sync", lambda e, hl=hl: e.dma_start(
                out=qf[:, :], in_=qkvreg[hl * 128:(hl + 1) * 128, :]),
                deps=dep_re, inc=sL)
            dstep(steps, "sync", lambda e, hl=hl: e.dma_start(
                out=kf[:, :], in_=qkvreg[MYC + hl * 128:MYC + (hl + 1) * 128, :]),
                inc=sL)
            dstep(steps, "sync", lambda e, hl=hl: e.dma_start(
                out=vf[:, :],
                in_=qkvreg[2 * MYC + hl * 128:2 * MYC + (hl + 1) * 128, :]),
                inc=sL)
            lv = sL.n
            for w in range(T // 128):
                step(steps, "tensor",
                     lambda e, w=w: e.transpose(ps_tr[:, :],
                                                vf[:, w * 128:(w + 1) * 128],
                                                cn.ident[:, :]),
                     deps=([(sL, lv)] if w == 0 else []) +
                          [(sDVE, sDVE.n)],
                     inc=sTR)
                tv = sTR.n
                step(steps, "vector",
                     lambda e, w=w: e.tensor_copy(vt[:, w * 128:(w + 1) * 128],
                                                  ps_tr[:, :]),
                     deps=[(sTR, tv)], inc=sDVE)
            vtv = sDVE.n
            for qt in range(T // TT):
                kcmax = 4 * qt + 3
                for kc in range(kcmax + 1):
                    pb = kc % 2
                    diag = (kc >= 4 * qt)
                    dep = [(sACT, sACT.n)]
                    if kc == 0:
                        dep.append((sL, lv))
                    step(steps, "tensor",
                         lambda e, kc=kc, qt=qt, pb=pb: e.matmul(
                             ps_s[pb][:, :],
                             kf[:, kc * 128:(kc + 1) * 128],
                             qf[:, qt * TT:(qt + 1) * TT],
                             start=True, stop=True),
                         deps=dep, inc=sMM)
                    mmv = sMM.n
                    step(steps, "scalar",
                         lambda e, kc=kc, pb=pb: e.activation(
                             pt[pb][:, :], ps_s[pb][:, :], AF.Exp,
                             scale=SQS, bias=cn.negc0_col[:, 0:1]),
                         deps=[(sMM, mmv), (sDVE, sDVE.n),
                               (sTR, sTR.n)], inc=sACT)
                    av = sACT.n
                    buf = pt[pb]
                    if diag:
                        j = kc - 4 * qt
                        step(steps, "vector",
                             lambda e, j=j, pb=pb: e.select(
                                 ptm[:, :],
                                 cn.mcaus[:, j * 512:(j + 1) * 512],
                                 pt[pb][:, :], zbf[:, :]),
                             deps=[(sACT, av), (sMM, sMM.n)], inc=sDVE)
                        buf = ptm
                    bdep = [(sACT, av)] if not diag else [(sDVE, sDVE.n)]
                    if kc == 0:
                        bdep = bdep + [(sACT, sACT.n), (sDVE, sDVE.n)]
                    step(steps, "tensor",
                         lambda e, buf=buf, kc=kc, qt=qt: e.matmul(
                             ps_sg[0:1, :], cn.ones_col, buf[:, :],
                             start=(kc == 0), stop=(kc == kcmax)),
                         deps=bdep)
                    step(steps, "tensor",
                         lambda e, buf=buf, kc=kc, qt=qt: e.matmul(
                             ps_o[:, :], vt[:, kc * 128:(kc + 1) * 128],
                             buf[:, :],
                             start=(kc == 0), stop=(kc == kcmax)),
                         inc=sMM)
                mmv = sMM.n
                step(steps, "scalar",
                     lambda e: e.copy(srow[:, :], ps_sg[0:1, :]),
                     deps=[(sMM, mmv)])
                step(steps, "scalar",
                     lambda e: e.activation(rrow[:, :], srow[:, :], AF.Ln))
                step(steps, "scalar",
                     lambda e: e.activation(rrow[:, :], rrow[:, :], AF.Exp,
                                            scale=-1.0), inc=sACT)
                av = sACT.n
                step(steps, "tensor",
                     lambda e: e.matmul(ps_rb[:, :], cn.one_row_f32[0:1, :],
                                        rrow[0:1, :], start=True, stop=True),
                     deps=[(sACT, av), (sDVE, sDVE.n)], inc=sMM)
                mmv2 = sMM.n
                step(steps, "scalar",
                     lambda e: e.copy(rbsb[:, :], ps_rb[:, :]),
                     deps=[(sMM, mmv2)], inc=sACT)
                av2 = sACT.n
                step(steps, "vector",
                     lambda e, qt=qt: e.tensor_mul(
                         ogs[:, qt * TT:(qt + 1) * TT], ps_o[:, :],
                         rbsb[:, :]),
                     deps=[(sACT, av2), (sST, sST.n)], inc=sDVE)
            dv = sDVE.n
            dstep(steps, "sync", lambda e, hl=hl: e.dma_start(
                out=ogreg[hl * 128:(hl + 1) * 128, :], in_=ogs[:, :]),
                deps=[(sDVE, dv)], inc=sST)
        wstep(steps, "sync", sST)
        return steps

    def ph_oexch(pst):
        steps = []
        sCC = p.usem("cc")
        sCP = p.sem("cp")
        step(steps, "gpsimd", lambda e: e.collective_compute(
            "AllGather", ALU.bypass, replica_groups=GROUPS,
            ins=[ogreg.ap().opt()], outs=[og_gath.ap().opt()]),
            inc=sCC)
        steps.append(Step("gpsimd", lambda e: e.nop(), [(sCC, 1)], None, 0))
        for i in range(4):
            def cp(e, i=i):
                e.reg_mov(blk["r_tmp"], i * 512 * T)
                e.reg_add(blk["r_tmp"], blk["r_tmp"], blk["r_t0"])
                return e.dma_start(
                    out=og_loc[i * 512:(i + 1) * 512, :],
                    in_=bass.AP(og_gath, blk["r_tmp"], [[T, 512], [1, TPC]]))
            dstep(steps, "gpsimd", cp, inc=sCP)
        wstep(steps, "gpsimd", sCP)
        return steps

    def ph_out(pst):
        steps = []
        sW = p.sem("w")
        sX = p.sem("x")
        sMM = p.sem("mm")
        sACT = p.sem("act")
        sLD = p.sem("lx")
        sST = p.sem("sx")
        TR = 256
        wo = p.sb("wo", [128, NCH * C], BF16, pst)
        ogt = p.sb("ogt", [128, NCH * TR], BF16, pst)
        y_sb = p.sb("yat", [128, NCH * TR], F32, pst)
        xf = p.sb("xo", [128, NCH * TR], F32, pst)
        stage = p.sb("stg", [128, NCH * TR], F32, pst)
        ln = LNCtx(p, pst, TR)
        yps = [p.ps("yps", [128, TR], F32, pst) for _ in range(2)]
        for k in range(NCH):
            dstep(steps, "gpsimd", lambda e, k=k: e.dma_start(
                out=wo[:, k * C:(k + 1) * C],
                in_=wod[k * 128:(k + 1) * 128, :]), inc=sW)
        wv = sW.n
        for tt in range(TPC // TR):
            t0 = tt * TR
            reuse = [(sMM, sMM.n)]
            for k in range(NCH):
                dstep(steps, "sync", lambda e, k=k, t0=t0: e.dma_start(
                    out=ogt[:, k * TR:(k + 1) * TR],
                    in_=og_loc[k * 128:(k + 1) * 128, t0:t0 + TR]),
                    deps=reuse if k == 0 else (), inc=sX)
            xv = sX.n
            reuse2 = [(ln.sV, ln.sV.n)]
            for k in range(NCH):
                dstep(steps, "sync", lambda e, k=k, t0=t0: e.dma_start(
                    out=xf[:, k * TR:(k + 1) * TR],
                    in_=xin[k * 128:(k + 1) * 128, t0:t0 + TR]),
                    deps=reuse2 if k == 0 else (), inc=sLD)
            ldv = sLD.n
            for oc in range(NCH):
                pb = oc % 2
                for k in range(NCH):
                    dep = []
                    if k == 0:
                        dep = [(sX, xv), (sACT, max(0, sACT.n - 1))]
                        if tt == 0 and oc == 0:
                            dep.append((sW, wv))
                    step(steps, "tensor",
                         lambda e, k=k, oc=oc, pb=pb: e.matmul(
                             yps[pb][:, :],
                             wo[:, k * C + oc * 128:k * C + (oc + 1) * 128],
                             ogt[:, k * TR:(k + 1) * TR],
                             start=(k == 0), stop=(k == NCH - 1)),
                         deps=dep, inc=sMM if k == NCH - 1 else None)
                mmv = sMM.n
                step(steps, "vector",
                     lambda e, oc=oc, pb=pb: e.tensor_scalar_add(
                         y_sb[:, oc * TR:(oc + 1) * TR], yps[pb][:, :],
                         blk["bq"][:, 24 + oc:24 + oc + 1]),
                     deps=[(sMM, mmv), (ln.sV, ln.sV.n)], inc=sACT)
            av = sACT.n
            emit_lnres_store(
                steps, p, cn, ln,
                lambda k: y_sb[:, k * TR:(k + 1) * TR],
                lambda k: xf[:, k * TR:(k + 1) * TR],
                [(sLD, ldv)], xout, t0, sST, stage,
                dep_y=[(sACT, av)])
        wstep(steps, "sync", sST)
        return steps

    p.phase(pfx + "_lng", ph_lng)
    p.phase(pfx + "_qkv", ph_qkv)
    p.phase(pfx + "_att", ph_attn)
    p.phase(pfx + "_oex", ph_oexch)
    p.phase(pfx + "_out", ph_out)
    p.block_end()

    def host_inputs(inputs, core):
        half = core % 2
        wq = np.asarray(inputs["L1_attn_w_qkv"], dtype=np.float32)
        cols = np.concatenate([
            np.arange(half * MYC, (half + 1) * MYC),
            np.arange(C + half * MYC, C + (half + 1) * MYC),
            np.arange(2 * C + half * MYC, 2 * C + (half + 1) * MYC)])
        wqs = np.ascontiguousarray(wq[:, cols])
        bq = np.asarray(inputs["L1_attn_b_qkv"], dtype=np.float32)[cols]
        bqs = np.ascontiguousarray(bq.reshape(24, 128).T)
        wo = np.ascontiguousarray(np.asarray(inputs["L1_attn_w_out"],
                                             dtype=np.float32))
        bo = np.ascontiguousarray(np.asarray(
            inputs["L1_attn_b_out"], np.float32).reshape(NCH, 128).T)
        return {pfx + "_wqkv": wqs, pfx + "_bqkv": bqs, pfx + "_wo": wo,
                pfx + "_bo": bo,
                pfx + "_t0": np.array([[half * TPC]], dtype=np.uint32)}

    return host_inputs


def build_prog_l1attn():
    p = Prog()
    cn = ConstSB(p, need_caus=True)
    cn.alloc()
    xin = p.dram("xin", [C, TPC], F32, "ExternalInput")
    xout = p.dram("xout", [C, TPC], F32, "ExternalOutput")
    hin = phases_l1attn(p, cn, xin, xout, first=True)
    nc = p.build()
    return nc, hin


def run_l1attn(x_shards, inputs):
    nc, hin = _get_prog("l1attn", build_prog_l1attn)
    cmap = ConstSB.host_inputs(need_caus=True)
    in_maps = []
    for c in range(8):
        m = {"xin": x_shards[c]}
        m.update(hin(inputs, c))
        m.update(cmap)
        in_maps.append(m)
    res = run_bass_kernel_spmd(nc, in_maps, core_ids=list(range(8)))
    return [res.results[c]["xout"] for c in range(8)]




def phases_l0attn(p, cn, xin, xout, pfx="l0t", first=False):
    wrd = p.dram(pfx + "_wr", [C, MYC], F32, "ExternalInput")
    wkd = p.dram(pfx + "_wk", [C, MYC], F32, "ExternalInput")
    wvd = p.dram(pfx + "_wv", [C, MYC], F32, "ExternalInput")
    wgd = p.dram(pfx + "_wg", [C, MYC], F32, "ExternalInput")
    wod = p.dram(pfx + "_wo", [C, C], F32, "ExternalInput")
    tmd = p.dram(pfx + "_tm", [128, 4 * NCH], F32, "ExternalInput")
    lnwd = p.dram(pfx + "_lnw", [128, 2 * NCH], F32, "ExternalInput")
    tdd = p.dram(pfx + "_td", [128, MYH], F32, "ExternalInput")
    tfd = p.dram(pfx + "_tf", [128, MYH], F32, "ExternalInput")
    t0d = p.dram(pfx + "_t0", [1, 1], U32, "ExternalInput")
    ag_in = p.dram(pfx + "_agi", [C, TPC], BF16)
    ag_out = p.dram(pfx + "_ago", [2 * C, TPC], BF16)
    regs = {nm: p.dram(pfx + "_" + nm, [MYC, T], BF16)
            for nm in ("rreg", "kreg", "vreg", "greg", "ogreg")}
    og_gath = p.dram(pfx + "_ogg", [2 * MYC, T], BF16)
    og_loc = p.dram(pfx + "_ogl", [C, TPC], BF16)

    blk = {}

    def balloc(bst):
        blk["tm"] = p.sb(pfx + "tm", [128, 4 * NCH], F32, bst)
        blk["lnwb"] = p.sb(pfx + "lw", [128, 2 * NCH], F32, bst)
        blk["lam"] = p.sb(pfx + "lam", [128, MYH], F32, bst)
        blk["ucol"] = p.sb(pfx + "uc", [128, MYH], F32, bst)
        blk["msk"] = p.sb(pfx + "msk", [128, 3 * MYH], F32, bst)
        blk["t0s"] = p.sb(pfx + "t0s", [1, 1], U32, bst)
    p.block_begin(balloc)
    blk["r_t0"] = p.stack.enter_context(p.nc.gpsimd.register(pfx + "rt0"))
    blk["r_tmp"] = p.stack.enter_context(p.nc.gpsimd.register(pfx + "rtm"))

    def ph_lng(pst):
        steps = []
        sLD = p.sem("ld")
        sST = p.sem("st")
        sCC = p.usem("cc")
        sACT = p.sem("ac")
        if first:
            cn.load_steps(steps, sLD)
        dstep(steps, "sync", lambda e: e.dma_start(
            out=blk["tm"][:, :], in_=tmd[:, :]), inc=sLD)
        dstep(steps, "sync", lambda e: e.dma_start(
            out=blk["lnwb"][:, :], in_=lnwd[:, :]), inc=sLD)
        dstep(steps, "sync", lambda e: e.dma_start(
            out=blk["lam"][:, :], in_=tdd[:, :]), inc=sLD)
        dstep(steps, "sync", lambda e: e.dma_start(
            out=blk["ucol"][:, :], in_=tfd[:, :]), inc=sLD)
        dstep(steps, "gpsimd", lambda e: e.dma_start(
            out=blk["t0s"][:, :], in_=t0d[:, :]), inc=sLD)
        ldt = sLD.n
        step(steps, "gpsimd",
             lambda e: e.reg_load(blk["r_t0"], blk["t0s"][0:1, 0:1]),
             deps=[(sLD, ldt)])
        step(steps, "scalar", lambda e: e.activation(
            blk["lam"][:, :], blk["lam"][:, :], AF.Exp),
            deps=[(sLD, ldt)], inc=sACT)
        av = sACT.n
        step(steps, "vector", lambda e: e.tensor_single_scalar(
            blk["msk"][:, 0:MYH], blk["lam"][:, :], 26.0, ALU.is_le),
            deps=[(sACT, av)])
        step(steps, "vector", lambda e: e.tensor_single_scalar(
            blk["msk"][:, MYH:2 * MYH], blk["lam"][:, :], 1.25, ALU.is_le))
        step(steps, "vector", lambda e: e.tensor_single_scalar(
            blk["msk"][:, 2 * MYH:3 * MYH], blk["lam"][:, :], 26.0, ALU.is_gt))
        TL = 256
        ln = LNCtx(p, pst, TL)
        xf = p.sb("xf", [128, NCH * TL], F32, pst)
        xbstage = p.sb("xbs", [128, NCH * TL], BF16, pst)
        for t0 in range(0, TPC, TL):
            reuse = [(ln.sV, ln.sV.n), (ln.sS, ln.sS.n)]
            for k in range(NCH):
                dstep(steps, "sync", lambda e, k=k, t0=t0: e.dma_start(
                    out=xf[:, k * TL:(k + 1) * TL],
                    in_=xin[k * 128:(k + 1) * 128, t0:t0 + TL]),
                    deps=reuse if k == 0 else (), inc=sLD)
            dep = [(sLD, sLD.n)]
            tS = emit_ln_stats(steps, cn, ln,
                               lambda k: xf[:, k * TL:(k + 1) * TL], dep)
            for k in range(NCH):
                kk = k
                step(steps, "vector",
                     lambda e, kk=kk: e.tensor_sub(ln.tmpn[:, :],
                                                   xf[:, kk * TL:(kk + 1) * TL],
                                                   ln.ps_bc[:, 0:TL]),
                     deps=([(ln.sT, tS), (sST, sST.n)] if kk == 0 else ()))
                step(steps, "vector",
                     lambda e, kk=kk: e.tensor_mul(
                         xbstage[:, kk * TL:(kk + 1) * TL],
                         ln.tmpn[:, :], ln.ps_bc[:, TL:2 * TL]),
                     inc=ln.sV)
                vv = ln.sV.n
                dstep(steps, "sync",
                      lambda e, kk=kk, t0=t0: e.dma_start(
                          out=ag_in[kk * 128:(kk + 1) * 128, t0:t0 + TL],
                          in_=xbstage[:, kk * TL:(kk + 1) * TL]),
                      deps=[(ln.sV, vv)], inc=sST)
        wstep(steps, "sync", sST)
        step(steps, "gpsimd", lambda e: e.collective_compute(
            "AllGather", ALU.bypass, replica_groups=GROUPS,
            ins=[ag_in.ap().opt()], outs=[ag_out.ap().opt()]),
            deps=[(sST, sST.n)], inc=sCC)
        steps.append(Step("gpsimd", lambda e: e.nop(), [(sCC, 1)], None, 0))
        return steps

    def ph_mixproj(pst):
        steps = []
        sW = p.sem("w")
        sX = p.sem("x")
        sMM = p.sem("mm")
        sACT = p.sem("act")
        sDVE = p.sem("dve")
        sST = p.sem("st")
        TL = 256
        wsb = {}
        for nm, dh in (("wk", wkd), ("wv", wvd), ("wr", wrd), ("wg", wgd)):
            wsb[nm] = p.sb(nm, [128, NCH * MYC], BF16, pst)
            for k in range(NCH):
                dstep(steps, "gpsimd", lambda e, k=k, dh=dh, nm=nm: e.dma_start(
                    out=wsb[nm][:, k * MYC:(k + 1) * MYC],
                    in_=dh[k * 128:(k + 1) * 128, :]), inc=sW)
        wv_ = sW.n
        xg = p.sb("xg", [128, NCH * TL], BF16, pst)
        sh = p.sb("sh", [128, NCH * TL], BF16, pst)
        dd = p.sb("dd", [128, NCH * TL], BF16, pst)
        mix = p.sb("mix", [128, NCH * TL], BF16, pst)
        st1 = p.sb("st1", [128, TL], BF16, pst)
        st2 = p.sb("st2", [128, TL], BF16, pst)
        pss = [p.ps("mps", [128, TL], F32, pst) for _ in range(2)]
        stages = [st1, st2]
        worder = [("wk", 0, "kreg", False), ("wv", 1, "vreg", False),
                  ("wr", 2, "rreg", False), ("wg", 3, "greg", True)]
        for tt in range(T // TL):
            t0g = tt * TL
            slab, tl = t0g // TPC, t0g % TPC
            dep_x = [(sDVE, sDVE.n)]
            for k in range(NCH):
                dstep(steps, "sync", lambda e, k=k, slab=slab, tl=tl: e.dma_start(
                    out=xg[:, k * TL:(k + 1) * TL],
                    in_=ag_out[slab * C + k * 128:slab * C + (k + 1) * 128,
                               tl:tl + TL]),
                    deps=dep_x if k == 0 else (), inc=sX)
                if t0g == 0:
                    dstep(steps, "sync", lambda e, k=k: e.dma_start(
                        out=sh[:, k * TL + 1:(k + 1) * TL],
                        in_=ag_out[k * 128:(k + 1) * 128, 0:TL - 1]), inc=sX)
                    dstep(steps, "sync", lambda e, k=k: e.dma_start(
                        out=sh[:, k * TL:k * TL + 1],
                        in_=ag_out[k * 128:(k + 1) * 128, 1:2],
                        allow_slow_non_contiguous=True), inc=sX)
                elif tl == 0:
                    dstep(steps, "sync", lambda e, k=k, slab=slab: e.dma_start(
                        out=sh[:, k * TL + 1:(k + 1) * TL],
                        in_=ag_out[slab * C + k * 128:slab * C + (k + 1) * 128,
                                   0:TL - 1]), inc=sX)
                    dstep(steps, "sync", lambda e, k=k, slab=slab: e.dma_start(
                        out=sh[:, k * TL:k * TL + 1],
                        in_=ag_out[(slab - 1) * C + k * 128:
                                   (slab - 1) * C + (k + 1) * 128,
                                   TPC - 1:TPC],
                        allow_slow_non_contiguous=True), inc=sX)
                else:
                    dstep(steps, "sync", lambda e, k=k, slab=slab, tl=tl:
                          e.dma_start(
                              out=sh[:, k * TL:(k + 1) * TL],
                              in_=ag_out[slab * C + k * 128:
                                         slab * C + (k + 1) * 128,
                                         tl - 1:tl + TL - 1]), inc=sX)
            xv = sX.n
            for k in range(NCH):
                step(steps, "vector",
                     lambda e, k=k: e.tensor_sub(dd[:, k * TL:(k + 1) * TL],
                                                 sh[:, k * TL:(k + 1) * TL],
                                                 xg[:, k * TL:(k + 1) * TL]),
                     deps=[(sX, xv)] if k == 0 else (), inc=sDVE)
            for nm, tmi, reg, sig in worder:
                for k in range(NCH):
                    step(steps, "vector",
                         lambda e, k=k, tmi=tmi: e.scalar_tensor_tensor(
                             mix[:, k * TL:(k + 1) * TL],
                             dd[:, k * TL:(k + 1) * TL],
                             blk["tm"][:, tmi * NCH + k:tmi * NCH + k + 1],
                             xg[:, k * TL:(k + 1) * TL],
                             ALU.mult, ALU.add),
                         deps=[(sMM, sMM.n)] if k == 0 else (), inc=sDVE)
                dv = sDVE.n
                for oc in range(MYH):
                    pb = oc % 2
                    for k in range(NCH):
                        dep = []
                        if k == 0:
                            dep = [(sDVE, dv), (sACT, max(0, sACT.n - 1))]
                            if tt == 0 and tmi == 0 and oc == 0:
                                dep.append((sW, wv_))
                        step(steps, "tensor",
                             lambda e, k=k, oc=oc, pb=pb, nm=nm: e.matmul(
                                 pss[pb][:, :],
                                 wsb[nm][:, k * MYC + oc * 128:
                                         k * MYC + (oc + 1) * 128],
                                 mix[:, k * TL:(k + 1) * TL],
                                 start=(k == 0), stop=(k == NCH - 1)),
                             deps=dep, inc=sMM if k == NCH - 1 else None)
                    mmv = sMM.n
                    step(steps, "scalar",
                         lambda e, pb=pb, sig=sig: e.activation(
                             stages[pb][:, :], pss[pb][:, :],
                             AF.Sigmoid if sig else AF.Copy),
                         deps=[(sMM, mmv), (sST, max(0, sST.n - 16))],
                         inc=sACT)
                    av = sACT.n
                    dstep(steps, "sync",
                          lambda e, oc=oc, tt=tt, pb=pb, reg=reg: e.dma_start(
                              out=regs[reg][oc * 128:(oc + 1) * 128,
                                             tt * TL:(tt + 1) * TL],
                              in_=stages[pb][:, :]),
                          deps=[(sACT, av)], inc=sST)
        wstep(steps, "sync", sST)
        return steps

    def ph_wkv(pst):
        steps = []
        sL = p.sem("l")
        sMM = p.sem("mm")
        sACT = p.sem("act")
        sDVE = p.sem("dve")
        sST = p.sem("st")
        sTR = p.sem("tr")
        rf = p.sb("rf", [128, T], BF16, pst)
        kf = p.sb("kf", [128, T], BF16, pst)
        vf = p.sb("vf", [128, T], BF16, pst)
        gf = p.sb("gf", [128, T], BF16, pst)
        vt = p.sb("vt", [128, T], BF16, pst)
        ogs = p.sb("ogs", [128, T], BF16, pst)
        ktf = p.sb("ktf", [128, T], BF16, pst)
        sc_names = ("r8c", "rn8", "r64", "rt", "rDo",
                    "k8c", "kn8", "k64", "kDe", "ku")
        sct = {nm: p.sb("sc" + nm, [128, TT], BF16, pst) for nm in sc_names}
        tabs = p.sb("tabs", [128, 8 * 128], BF16, pst)
        tpar = p.sb("tpar", [128, 2 * 128], BF16, pst)
        fcol = p.sb("fcol", [128, 8], F32, pst)
        ATc = p.sb("ATc", [128, 128], BF16, pst)
        ATp = p.sb("ATp", [128, 128], BF16, pst)
        pp = [p.ps("pp", [128, 128], F32, pst) for _ in range(2)]
        ps_o = p.ps("po", [128, 128], F32, pst)
        ps_tr = p.ps("ptr", [128, 128], BF16, pst)

        TAB = {"r8c": 0, "k8c": 1, "rn8": 2, "kn8": 3,
               "r64": 4, "k64": 5, "rt": 6, "kt": 7}

        for hl in range(MYH):
            dep_pre = [(sACT, sACT.n), (sMM, sMM.n)]
            step(steps, "vector", lambda e: e.memset(fcol[:, 0:1], 0.0),
                 deps=dep_pre)
            step(steps, "vector", lambda e, hl=hl: e.tensor_scalar(
                fcol[:, 1:2], blk["lam"][:, hl:hl + 1],
                blk["msk"][:, hl:hl + 1], -1.0, ALU.mult, ALU.mult))
            step(steps, "vector", lambda e, hl=hl: e.tensor_scalar(
                fcol[:, 2:3], blk["lam"][:, hl:hl + 1],
                blk["msk"][:, MYH + hl:MYH + hl + 1], -1.0,
                ALU.mult, ALU.mult))
            step(steps, "vector", lambda e, hl=hl: e.tensor_scalar_mul(
                fcol[:, 3:4], blk["lam"][:, hl:hl + 1], -1.0))
            step(steps, "vector", lambda e, hl=hl: e.tensor_scalar(
                fcol[:, 4:5], blk["msk"][:, hl:hl + 1], -1.0, 200.0,
                ALU.add, ALU.mult))
            step(steps, "vector", lambda e, hl=hl: e.tensor_scalar(
                fcol[:, 5:6], blk["msk"][:, MYH + hl:MYH + hl + 1],
                -1.0, 200.0, ALU.add, ALU.mult), inc=sDVE)
            fv = sDVE.n
            tspec = [("r8c", 1, 4), ("k8c", 1, 4), ("rn8", 3, 0), ("kn8", 3, 0),
                     ("r64", 2, 5), ("k64", 2, 5), ("rt", 3, 0), ("kt", 3, 0)]
            for nm, sci, bi in tspec:
                j = TAB[nm]
                step(steps, "scalar",
                     lambda e, j=j, sci=sci, bi=bi: e.activation(
                         tabs[:, j * 128:(j + 1) * 128],
                         cn.wkvp[:, j * 128:(j + 1) * 128], AF.Exp,
                         scale=fcol[:, sci:sci + 1], bias=fcol[:, bi:bi + 1]),
                     deps=[(sDVE, fv)], inc=sACT)
            step(steps, "vector", lambda e, hl=hl: e.tensor_scalar_mul(
                tpar[:, 0:128], cn.par[:, 0:128],
                blk["msk"][:, 2 * MYH + hl:2 * MYH + hl + 1]),
                deps=[(sACT, sACT.n)])
            step(steps, "vector", lambda e, hl=hl: e.tensor_scalar_mul(
                tpar[:, 128:256], cn.par[:, 128:256],
                blk["msk"][:, 2 * MYH + hl:2 * MYH + hl + 1]), inc=sDVE)
            tabv = sACT.n
            dep_re = [(sMM, sMM.n), (sDVE, sDVE.n)]
            for nm, reg in (("rf", "rreg"), ("kf", "kreg"),
                            ("vf", "vreg"), ("gf", "greg")):
                tgt = {"rf": rf, "kf": kf, "vf": vf, "gf": gf}[nm]
                dstep(steps, "sync", lambda e, tgt=tgt, reg=reg, hl=hl:
                      e.dma_start(out=tgt[:, :],
                                  in_=regs[reg][hl * 128:(hl + 1) * 128, :]),
                      deps=dep_re if nm == "rf" else (), inc=sL)
            lv = sL.n
            step(steps, "vector", lambda e: e.tensor_mul(
                ktf[:, :], kf[:, :],
                bass.AP(tabs, TAB["kt"] * 128,
                        [[8 * 128, 128], [0, T // 128], [1, 128]])),
                deps=[(sL, lv), (sACT, tabv), (sMM, sMM.n)], inc=sDVE)
            for w in range(T // 128):
                step(steps, "tensor",
                     lambda e, w=w: e.transpose(ps_tr[:, :],
                                                vf[:, w * 128:(w + 1) * 128],
                                                cn.ident[:, :]),
                     deps=([(sL, lv)] if w == 0 else []) + [(sDVE, sDVE.n)],
                     inc=sTR)
                tv = sTR.n
                step(steps, "vector",
                     lambda e, w=w: e.tensor_copy(vt[:, w * 128:(w + 1) * 128],
                                                  ps_tr[:, :]),
                     deps=[(sTR, tv)], inc=sDVE)
            for ts4 in range(T // TT):
                tb = ts4 * TT
                sspec = [("r8c", rf, "r8c"), ("rn8", rf, "rn8"),
                         ("r64", rf, "r64"), ("rt", rf, "rt"),
                         ("k8c", kf, "k8c"), ("kn8", kf, "kn8"),
                         ("k64", kf, "k64")]
                dep_sc = [(sMM, sMM.n)]
                for nm, srcb, tb_nm in sspec:
                    j = TAB[tb_nm]
                    step(steps, "vector",
                         lambda e, nm=nm, srcb=srcb, j=j, tb=tb: e.tensor_mul(
                             sct[nm][:, :], srcb[:, tb:tb + TT],
                             bass.AP(tabs, j * 128,
                                     [[8 * 128, 128], [0, 4], [1, 128]])),
                         deps=dep_sc if nm == "r8c" else ())
                step(steps, "vector",
                     lambda e, tb=tb: e.tensor_mul(
                         sct["rDo"][:, :], rf[:, tb:tb + TT],
                         bass.AP(tpar, 0, [[2 * 128, 128], [0, 4], [1, 128]])))
                step(steps, "vector",
                     lambda e, tb=tb: e.tensor_mul(
                         sct["kDe"][:, :], kf[:, tb:tb + TT],
                         bass.AP(tpar, 128, [[2 * 128, 128], [0, 4], [1, 128]])))
                step(steps, "vector",
                     lambda e, tb=tb, hl=hl: e.tensor_scalar_mul(
                         sct["ku"][:, :], kf[:, tb:tb + TT],
                         blk["ucol"][:, hl:hl + 1]), inc=sDVE)
                scv = sDVE.n
                for wl in range(4):
                    w = ts4 * 4 + wl
                    c0, c1 = wl * 128, (wl + 1) * 128
                    step(steps, "vector", lambda e: e.memset(ATc[:, :], 0.0),
                         deps=[(sMM, sMM.n)], inc=sDVE)
                    step(steps, "tensor",
                         lambda e, c0=c0, c1=c1: e.matmul(
                             pp[0][:, :], sct["k8c"][:, c0:c1],
                             sct["r8c"][:, c0:c1], start=True, stop=True),
                         deps=[(sDVE, sDVE.n)], inc=sMM)
                    mv = sMM.n
                    step(steps, "vector", lambda e: e.select(
                        ATc[:, :], cn.wmask[:, 0:128], pp[0][:, :], ATc[:, :]),
                        deps=[(sMM, mv)], inc=sDVE)
                    dv = sDVE.n
                    step(steps, "tensor",
                         lambda e, c0=c0, c1=c1: e.matmul(
                             pp[0][:, :], sct["kDe"][:, c0:c1],
                             sct["rDo"][:, c0:c1], start=False, stop=True),
                         deps=[(sDVE, dv)], inc=sMM)
                    mv = sMM.n
                    step(steps, "vector", lambda e: e.select(
                        ATc[:, :], cn.wmask[:, 128:256], pp[0][:, :], ATc[:, :]),
                        deps=[(sMM, mv)], inc=sDVE)
                    step(steps, "tensor",
                         lambda e, c0=c0, c1=c1: e.matmul(
                             pp[1][:, :], sct["kn8"][:, c0:c1],
                             sct["rn8"][:, c0:c1], start=True, stop=True),
                         deps=[(sDVE, sDVE.n)], inc=sMM)
                    mv = sMM.n
                    step(steps, "vector", lambda e: e.select(
                        ATc[:, :], cn.wmask[:, 256:384], pp[1][:, :], ATc[:, :]),
                        deps=[(sMM, mv)], inc=sDVE)
                    step(steps, "tensor",
                         lambda e, c0=c0, c1=c1: e.matmul(
                             pp[0][:, :], sct["k64"][:, c0:c1],
                             sct["r64"][:, c0:c1], start=True, stop=True),
                         deps=[(sDVE, sDVE.n)], inc=sMM)
                    mv = sMM.n
                    step(steps, "vector", lambda e: e.select(
                        ATc[:, :], cn.wmask[:, 384:512], pp[0][:, :], ATc[:, :]),
                        deps=[(sMM, mv)], inc=sDVE)
                    step(steps, "tensor",
                         lambda e, c0=c0, c1=c1, tb=tb: e.matmul(
                             pp[1][:, :], sct["ku"][:, c0:c1],
                             rf[:, tb + c0:tb + c1], start=True, stop=True),
                         deps=[(sDVE, sDVE.n)], inc=sMM)
                    mv = sMM.n
                    step(steps, "vector", lambda e: e.select(
                        ATc[:, :], cn.wmask[:, 512:640], pp[1][:, :], ATc[:, :]),
                        deps=[(sMM, mv)], inc=sDVE)
                    atv = sDVE.n
                    if w > 0:
                        step(steps, "tensor",
                             lambda e, w=w, c0=c0, c1=c1: e.matmul(
                                 pp[0][:, :],
                                 ktf[:, (w - 1) * 128:w * 128],
                                 sct["rt"][:, c0:c1], start=True, stop=True),
                             deps=[(sDVE, sDVE.n)], inc=sMM)
                        mv = sMM.n
                        step(steps, "vector", lambda e: e.tensor_copy(
                            ATp[:, :], pp[0][:, :]),
                            deps=[(sMM, mv)], inc=sDVE)
                        apv = sDVE.n
                    step(steps, "tensor",
                         lambda e, w=w: e.matmul(
                             ps_o[:, :], vt[:, w * 128:(w + 1) * 128],
                             ATc[:, :], start=True, stop=(w == 0)),
                         deps=[(sDVE, sDVE.n)],
                         inc=sMM if w == 0 else None)
                    if w > 0:
                        step(steps, "tensor",
                             lambda e, w=w: e.matmul(
                                 ps_o[:, :], vt[:, (w - 1) * 128:w * 128],
                                 ATp[:, :], start=False, stop=True),
                             inc=sMM)
                    mv = sMM.n
                    step(steps, "vector",
                         lambda e, w=w, tb=tb, c0=c0, c1=c1: e.tensor_mul(
                             ogs[:, w * 128:(w + 1) * 128], ps_o[:, :],
                             gf[:, tb + c0:tb + c1]),
                         deps=[(sMM, mv), (sST, sST.n)], inc=sDVE)
            dv = sDVE.n
            dstep(steps, "sync", lambda e, hl=hl: e.dma_start(
                out=regs["ogreg"][hl * 128:(hl + 1) * 128, :], in_=ogs[:, :]),
                deps=[(sDVE, dv)], inc=sST)
        wstep(steps, "sync", sST)
        return steps

    def ph_oexch(pst):
        steps = []
        sCC = p.usem("cc")
        sCP = p.sem("cp")
        step(steps, "gpsimd", lambda e: e.collective_compute(
            "AllGather", ALU.bypass, replica_groups=GROUPS,
            ins=[regs["ogreg"].ap().opt()], outs=[og_gath.ap().opt()]),
            inc=sCC)
        steps.append(Step("gpsimd", lambda e: e.nop(), [(sCC, 1)], None, 0))
        for i in range(4):
            def cp(e, i=i):
                e.reg_mov(blk["r_tmp"], i * 512 * T)
                e.reg_add(blk["r_tmp"], blk["r_tmp"], blk["r_t0"])
                return e.dma_start(
                    out=og_loc[i * 512:(i + 1) * 512, :],
                    in_=bass.AP(og_gath, blk["r_tmp"], [[T, 512], [1, TPC]]))
            dstep(steps, "gpsimd", cp, inc=sCP)
        wstep(steps, "gpsimd", sCP)
        return steps

    def ph_out(pst):
        steps = []
        sW = p.sem("w")
        sX = p.sem("x")
        sMM = p.sem("mm")
        sACT = p.sem("act")
        sLD = p.sem("lx")
        sST = p.sem("sx")
        sZ = p.sem("z")
        TR = 256
        wo = p.sb("wo", [128, NCH * C], BF16, pst)
        ogt = p.sb("ogt", [128, NCH * TR], BF16, pst)
        y_sb = p.sb("yat", [128, NCH * TR], F32, pst)
        z_sb = p.sb("zat", [128, NCH * TR], F32, pst)
        xf = p.sb("xo", [128, NCH * TR], F32, pst)
        stage = p.sb("stg", [128, NCH * TR], F32, pst)
        ln = LNCtx(p, pst, TR)
        yps = [p.ps("yps", [128, TR], F32, pst) for _ in range(2)]
        for k in range(NCH):
            dstep(steps, "gpsimd", lambda e, k=k: e.dma_start(
                out=wo[:, k * C:(k + 1) * C],
                in_=wod[k * 128:(k + 1) * 128, :]), inc=sW)
        wv = sW.n
        for tt in range(TPC // TR):
            t0 = tt * TR
            reuse = [(sMM, sMM.n)]
            for k in range(NCH):
                dstep(steps, "sync", lambda e, k=k, t0=t0: e.dma_start(
                    out=ogt[:, k * TR:(k + 1) * TR],
                    in_=og_loc[k * 128:(k + 1) * 128, t0:t0 + TR]),
                    deps=reuse if k == 0 else (), inc=sX)
            xv = sX.n
            reuse2 = [(ln.sV, ln.sV.n)]
            for k in range(NCH):
                dstep(steps, "sync", lambda e, k=k, t0=t0: e.dma_start(
                    out=xf[:, k * TR:(k + 1) * TR],
                    in_=xin[k * 128:(k + 1) * 128, t0:t0 + TR]),
                    deps=reuse2 if k == 0 else (), inc=sLD)
            ldv = sLD.n
            for oc in range(NCH):
                pb = oc % 2
                for k in range(NCH):
                    dep = []
                    if k == 0:
                        dep = [(sX, xv), (sACT, max(0, sACT.n - 1))]
                        if tt == 0 and oc == 0:
                            dep.append((sW, wv))
                    step(steps, "tensor",
                         lambda e, k=k, oc=oc, pb=pb: e.matmul(
                             yps[pb][:, :],
                             wo[:, k * C + oc * 128:k * C + (oc + 1) * 128],
                             ogt[:, k * TR:(k + 1) * TR],
                             start=(k == 0), stop=(k == NCH - 1)),
                         deps=dep, inc=sMM if k == NCH - 1 else None)
                mmv = sMM.n
                step(steps, "scalar",
                     lambda e, oc=oc, pb=pb: e.copy(
                         y_sb[:, oc * TR:(oc + 1) * TR], yps[pb][:, :]),
                     deps=[(sMM, mmv), (sZ, sZ.n)], inc=sACT)
            av = sACT.n
            tS = emit_ln_stats(steps, cn, ln,
                               lambda k: y_sb[:, k * TR:(k + 1) * TR],
                               [(sACT, av)])
            for k in range(NCH):
                kk = k
                step(steps, "vector",
                     lambda e, kk=kk: e.tensor_sub(ln.tmpn[:, :],
                                                   y_sb[:, kk * TR:(kk + 1) * TR],
                                                   ln.ps_bc[:, 0:TR]),
                     deps=[(ln.sT, tS)] if kk == 0 else ())
                step(steps, "vector",
                     lambda e, kk=kk: e.tensor_mul(ln.tmpn[:, :], ln.tmpn[:, :],
                                                   ln.ps_bc[:, TR:2 * TR]))
                step(steps, "vector",
                     lambda e, kk=kk: e.tensor_scalar(
                         z_sb[:, kk * TR:(kk + 1) * TR], ln.tmpn[:, :],
                         blk["lnwb"][:, kk:kk + 1],
                         blk["lnwb"][:, NCH + kk:NCH + kk + 1],
                         ALU.mult, ALU.add),
                     inc=sZ)
            zv = sZ.n
            emit_lnres_store(
                steps, p, cn, ln,
                lambda k: z_sb[:, k * TR:(k + 1) * TR],
                lambda k: xf[:, k * TR:(k + 1) * TR],
                [(sLD, ldv)], xout, t0, sST, stage,
                dep_y=[(sZ, zv)])
        wstep(steps, "sync", sST)
        return steps

    p.phase(pfx + "_lng", ph_lng)
    p.phase(pfx + "_mix", ph_mixproj)
    p.phase(pfx + "_wkv", ph_wkv)
    p.phase(pfx + "_oex", ph_oexch)
    p.phase(pfx + "_out", ph_out)
    p.block_end()

    def host_inputs(inputs, core):
        half = core % 2
        cols = np.arange(half * MYC, (half + 1) * MYC)
        out = {}
        for nm, key in (("_wr", "L0_attn_Wr"), ("_wk", "L0_attn_Wk"),
                        ("_wv", "L0_attn_Wv"), ("_wg", "L0_attn_Wg")):
            w = np.asarray(inputs[key], dtype=np.float32)
            out[pfx + nm] = np.ascontiguousarray(w.T[:, cols])
        out[pfx + "_wo"] = np.ascontiguousarray(
            np.asarray(inputs["L0_attn_Wo"], dtype=np.float32).T)
        tm = [np.asarray(inputs[k], np.float32).reshape(NCH, 128).T
              for k in ("L0_attn_tm_k", "L0_attn_tm_v",
                        "L0_attn_tm_r", "L0_attn_tm_g")]
        out[pfx + "_tm"] = np.ascontiguousarray(np.concatenate(tm, axis=1))
        lnw = np.asarray(inputs["L0_attn_ln_w"], np.float32).reshape(NCH, 128).T
        lnb = np.asarray(inputs["L0_attn_ln_b"], np.float32).reshape(NCH, 128).T
        out[pfx + "_lnw"] = np.ascontiguousarray(
            np.concatenate([lnw, lnb], axis=1))
        td = np.asarray(inputs["L0_attn_time_decay"], np.float32)
        tf = np.asarray(inputs["L0_attn_time_first"], np.float32)
        hs = slice(half * MYH, (half + 1) * MYH)
        out[pfx + "_td"] = np.ascontiguousarray(td[hs].T)
        out[pfx + "_tf"] = np.ascontiguousarray(tf[hs].T)
        out[pfx + "_t0"] = np.array([[half * TPC]], dtype=np.uint32)
        return out

    return host_inputs


def build_prog_l0attn():
    p = Prog()
    cn = ConstSB(p, need_wkv=True)
    cn.alloc()
    xin = p.dram("xin", [C, TPC], F32, "ExternalInput")
    xout = p.dram("xout", [C, TPC], F32, "ExternalOutput")
    hin = phases_l0attn(p, cn, xin, xout, first=True)
    nc = p.build()
    return nc, hin


def run_l0attn(x_shards, inputs):
    nc, hin = _get_prog("l0attn", build_prog_l0attn)
    cmap = ConstSB.host_inputs(need_wkv=True)
    in_maps = []
    for c in range(8):
        m = {"xin": x_shards[c]}
        m.update(hin(inputs, c))
        m.update(cmap)
        in_maps.append(m)
    res = run_bass_kernel_spmd(nc, in_maps, core_ids=list(range(8)))
    return [res.results[c]["xout"] for c in range(8)]




def build_prog_full():
    p = Prog()
    cn = ConstSB(p, need_wkv=True, need_caus=True)
    cn.alloc()
    x0 = p.dram("xin", [C, TPC], F32, "ExternalInput")
    x1 = p.dram("x1", [C, TPC], F32)
    x2 = p.dram("x2", [C, TPC], F32)
    x3 = p.dram("x3", [C, TPC], F32)
    x4 = p.dram("xout", [C, TPC], F32, "ExternalOutput")
    h0 = phases_l0attn(p, cn, x0, x1, first=True)
    h1 = phases_l0ffn(p, cn, x1, x2)
    h2 = phases_l1attn(p, cn, x2, x3)
    h3 = phases_l1ffn(p, cn, x3, x4)
    nc = p.build()

    def host_inputs(inputs, core):
        m = {}
        m.update(h0(inputs, core))
        m.update(h1(inputs))
        m.update(h2(inputs, core))
        m.update(h3(inputs))
        return m
    return nc, host_inputs


FUSED = True


def kernel(**inputs):
    x = [_fm_shard(np.asarray(inputs["L0_attn_x"]), c) for c in range(8)]
    if FUSED:
        nc, hin = _get_prog("full", build_prog_full)
        cmap = ConstSB.host_inputs(need_wkv=True, need_caus=True)
        in_maps = []
        for c in range(8):
            m = {"xin": x[c]}
            m.update(hin(inputs, c))
            m.update(cmap)
            in_maps.append(m)
        res = run_bass_kernel_spmd(nc, in_maps, core_ids=list(range(8)))
        x = [res.results[c]["xout"] for c in range(8)]
    else:
        x = run_l0attn(x, inputs)
        x = run_l0ffn(x, inputs)
        x = run_l1attn(x, inputs)
        x = run_l1ffn(x, inputs)
    return _unshard(x)


# version 04
# speedup vs baseline: 1.0728x; 1.0728x over previous
import numpy as np
import ml_dtypes
from contextlib import ExitStack

import concourse.bass as bass
import concourse.mybir as mybir
from concourse.bass_utils import run_bass_kernel_spmd

F32 = mybir.dt.float32
BF16 = mybir.dt.bfloat16
U8 = mybir.dt.uint8
I32 = mybir.dt.int32
U32 = mybir.dt.uint32
AF = mybir.ActivationFunctionType
ALU = mybir.AluOpType
AX = mybir.AxisListType
BF = ml_dtypes.bfloat16

B, T, C, NH, HD, II = 4, 4096, 2048, 16, 128, 8192
TPC = 2048
NCH = C // 128
MYH = 8
MYC = MYH * HD
EPS = 1e-5
TT = 512
GROUPS = [[0, 1], [2, 3], [4, 5], [6, 7]]
SQS = 1.0 / np.sqrt(HD)
ATTN_C0 = 4.0



class SemC:

    def __init__(self, h):
        self.h = h
        self.n = 0


class Step:
    __slots__ = ("eng", "fn", "deps", "inc", "amt")

    def __init__(self, eng, fn, deps, inc, amt):
        self.eng, self.fn, self.deps, self.inc, self.amt = eng, fn, deps, inc, amt


class Prog:
    ENGS = ("gpsimd", "vector", "scalar", "tensor", "sync")

    def __init__(self):
        self.nc = bass.Bass("TRN2", target_bir_lowering=False)
        self.stack = ExitStack()
        self._semi = 0
        self._sbi = 0
        self.phase_builders = []

    def sem(self, tag="s"):
        if not hasattr(self, "_semcache"):
            self._semcache = {}
        if tag not in self._semcache:
            self._semi += 1
            self._semcache[tag] = SemC(self.stack.enter_context(
                self.nc.semaphore(f"{tag}{self._semi}")))
        return self._semcache[tag]

    def usem(self, tag="u"):
        self._semi += 1
        return SemC(self.stack.enter_context(
            self.nc.semaphore(f"{tag}{self._semi}")))

    def sb(self, tag, shape, dt, st=None):
        self._sbi += 1
        return (st if st is not None else self.stack).enter_context(
            self.nc.sbuf_tensor(f"{tag}{self._sbi}", shape, dt))

    def ps(self, tag, shape, dt=F32, st=None):
        self._sbi += 1
        return (st if st is not None else self.stack).enter_context(
            self.nc.psum_tensor(f"{tag}{self._sbi}", shape, dt))

    def dram(self, tag, shape, dt, kind="Internal"):
        return self.nc.dram_tensor(tag, shape, dt, kind=kind)

    def phase(self, name, fn):
        self.phase_builders.append(("ph", name, fn))

    def block_begin(self, fn_alloc):
        self.phase_builders.append(("begin", None, fn_alloc))

    def block_end(self):
        self.phase_builders.append(("end", None, None))

    def build(self):
        nc = self.nc
        bar = self.stack.enter_context(nc.semaphore("BAR"))
        with self.stack:
            fence_v = self.sb("fenceV", [1, 2], F32)
            fence_s = self.sb("fenceS", [1, 2], F32)
            with nc.Block() as blk:
                barn = 0
                bstk = []
                for kind, name, fn in self.phase_builders:
                    if kind == "begin":
                        bstk.append(ExitStack())
                        fn(bstk[-1])
                        continue
                    if kind == "end":
                        bstk.pop().close()
                        continue
                    with ExitStack() as pst:
                        steps = fn(pst)
                        with nc.named_scope(name):
                            for engname in Prog.ENGS:
                                fsteps = [s for s in steps if s.eng == engname]
                                self._emit(blk, engname, fsteps, bar, barn,
                                           fence_v, fence_s)
                    barn += 5
                while bstk:
                    bstk.pop().close()
        return nc

    def _emit(self, blk, engname, fsteps, bar, barn, fence_v, fence_s):
        def emitter(eng):
            if barn:
                eng.wait_ge(bar, barn)
            for s in fsteps:
                for (sc, v) in s.deps:
                    eng.wait_ge(sc.h, v)
                inst = s.fn(eng)
                if s.inc is not None:
                    inst.then_inc(s.inc.h, s.amt)
            if engname == "vector":
                eng.memset(fence_v[0:1, 0:1], 0.0).then_inc(bar, 1)
            elif engname == "scalar":
                eng.copy(fence_s[0:1, 0:1], fence_s[0:1, 1:2]).then_inc(bar, 1)
            else:
                eng.sem_inc(bar, 1)
        getattr(blk, engname)(emitter)


def step(steps, eng, fn, deps=(), inc=None, amt=1):
    if inc is not None:
        inc.n += amt
    steps.append(Step(eng, fn, list(deps), inc, amt))
    return None if inc is None else inc.n


def dstep(steps, eng, fn, deps=(), inc=None):
    return step(steps, eng, fn, deps, inc, 16)


def wstep(steps, eng, sem):
    steps.append(Step(eng, lambda e: e.nop(), [(sem, sem.n)], None, 0))




class LNCtx:

    def __init__(self, p, pst, Tt, tag="ln"):
        self.Tt = Tt
        self.ybf = p.sb(tag + "ybf", [128, NCH * Tt], BF16, pst)
        self.ysq = p.sb(tag + "ysq", [128, NCH * Tt], BF16, pst)
        self.row = p.sb(tag + "row", [1, 2 * Tt], F32, pst)
        self.rowt = p.sb(tag + "rowt", [1, Tt], F32, pst)
        self.tmpn = p.sb(tag + "tmpn", [128, Tt], F32, pst)
        self.ps_st = p.ps(tag + "st", [1, 2 * Tt], F32, pst)
        self.ps_bc = p.ps(tag + "bc", [128, 2 * Tt], F32, pst)
        self.sV = p.sem(tag + "V")
        self.sS = p.sem(tag + "S")
        self.sT = p.sem(tag + "T")


def emit_ln_stats(steps, cn, ln, ychunk, dep_y):
    Tt = ln.Tt
    reuse_T = [(ln.sT, ln.sT.n)]
    reuse_SV = [(ln.sS, ln.sS.n), (ln.sV, ln.sV.n)]
    for k in range(NCH):
        step(steps, "vector",
             lambda e, k=k: e.tensor_copy(ln.ybf[:, k * Tt:(k + 1) * Tt],
                                          ychunk(k)),
             deps=(list(dep_y) + reuse_T) if k == 0 else (),
             inc=ln.sV if k == NCH - 1 else None)
        step(steps, "scalar",
             lambda e, k=k: e.activation(ln.ysq[:, k * Tt:(k + 1) * Tt],
                                         ychunk(k), AF.Square),
             deps=(list(dep_y) + reuse_T) if k == 0 else (),
             inc=ln.sS if k == NCH - 1 else None)
    vS, sS = ln.sV.n, ln.sS.n
    for k in range(NCH):
        step(steps, "tensor",
             lambda e, k=k: e.matmul(ln.ps_st[0:1, 0:Tt], cn.mean_col,
                                     ln.ybf[:, k * Tt:(k + 1) * Tt],
                                     start=(k == 0), stop=(k == NCH - 1)),
             deps=([(ln.sV, vS)] + reuse_SV) if k == 0 else ())
    for k in range(NCH):
        step(steps, "tensor",
             lambda e, k=k: e.matmul(ln.ps_st[0:1, Tt:2 * Tt], cn.mean_col,
                                     ln.ysq[:, k * Tt:(k + 1) * Tt],
                                     start=(k == 0), stop=(k == NCH - 1)),
             deps=[(ln.sS, sS)] if k == 0 else (),
             inc=ln.sT if k == NCH - 1 else None)
    tS = ln.sT.n
    step(steps, "scalar", lambda e: e.copy(ln.row[:, :], ln.ps_st[:, :]),
         deps=[(ln.sT, tS)], inc=ln.sS)
    step(steps, "scalar", lambda e: e.activation(ln.rowt[:, :], ln.row[:, 0:Tt],
                                                 AF.Square), inc=ln.sS)
    sS2 = ln.sS.n
    step(steps, "vector", lambda e: e.tensor_sub(ln.row[:, Tt:2 * Tt],
                                                 ln.row[:, Tt:2 * Tt],
                                                 ln.rowt[:, :]),
         deps=[(ln.sS, sS2)], inc=ln.sV)
    vS2 = ln.sV.n
    step(steps, "scalar", lambda e: e.activation(ln.rowt[:, :],
                                                 ln.row[:, Tt:2 * Tt],
                                                 AF.Ln,
                                                 bias=cn.eps_col[0:1, 0:1]),
         deps=[(ln.sV, vS2)])
    step(steps, "scalar", lambda e: e.activation(ln.row[:, Tt:2 * Tt],
                                                 ln.rowt[:, :],
                                                 AF.Exp, scale=-0.5),
         inc=ln.sS)
    sS3 = ln.sS.n
    nspl = (2 * Tt + 511) // 512
    for i in range(nspl):
        c0, c1 = i * 512, min((i + 1) * 512, 2 * Tt)
        step(steps, "tensor",
             lambda e, c0=c0, c1=c1: e.matmul(ln.ps_bc[:, c0:c1],
                                              cn.one_row_f32[0:1, :],
                                              ln.row[0:1, c0:c1],
                                              start=True, stop=True),
             deps=([(ln.sS, sS3)] + reuse_SV) if i == 0 else (),
             inc=ln.sT if i == nspl - 1 else None)
    return ln.sT.n




class ConstSB:
    def __init__(self, p, need_wkv=False, need_caus=False):
        self.p = p
        self.need_wkv = need_wkv
        self.need_caus = need_caus
        self.d_ones = p.dram("c_ones", [128, 2], BF16, "ExternalInput")
        self.d_onesf = p.dram("c_onesf", [1, 128], F32, "ExternalInput")
        self.d_ident = p.dram("c_ident", [128, 128], BF16, "ExternalInput")
        if need_caus:
            self.d_mcaus = p.dram("c_mcaus", [128, 4 * 512], U8, "ExternalInput")
        if need_wkv:
            self.d_wkvp = p.dram("c_wkvp", [128, 8 * 128], F32, "ExternalInput")
            self.d_par = p.dram("c_par", [128, 2 * 128], F32, "ExternalInput")
            self.d_wmask = p.dram("c_wmask", [128, 5 * 128], U8, "ExternalInput")

    def alloc(self):
        p = self.p
        sb = p.sb("c_mean", [128, 2], BF16)
        self._csb = sb
        self.mean_col = sb[:, 0:1]
        self.ones_col = sb[:, 1:2]
        fc = p.sb("c_f", [128, 4], F32)
        self.eps_col = fc[:, 0:1]
        self.negc0_col = fc[:, 1:2]
        self._fc = fc
        self.one_row_f32 = p.sb("c_or", [1, 128], F32)
        self.ident = p.sb("c_id", [128, 128], BF16)
        if self.need_caus:
            self.mcaus = p.sb("c_mc", [128, 4 * 512], U8)
        if self.need_wkv:
            self.wkvp = p.sb("c_wp", [128, 8 * 128], F32)
            self.par = p.sb("c_pr", [128, 2 * 128], F32)
            self.wmask = p.sb("c_wm", [128, 5 * 128], U8)

    def load_steps(self, steps, sem):
        step(steps, "vector", lambda e: e.memset(self._fc[:, 0:1], EPS))
        step(steps, "vector", lambda e: e.memset(self._fc[:, 1:2], -ATTN_C0))
        dstep(steps, "sync", lambda e: e.dma_start(
            out=self._csb[:, :], in_=self.d_ones[:, :]), inc=sem)
        dstep(steps, "sync", lambda e: e.dma_start(
            out=self.one_row_f32[:, :], in_=self.d_onesf[:, :]), inc=sem)
        dstep(steps, "sync", lambda e: e.dma_start(
            out=self.ident[:, :], in_=self.d_ident[:, :]), inc=sem)
        if self.need_caus:
            dstep(steps, "sync", lambda e: e.dma_start(
                out=self.mcaus[:, :], in_=self.d_mcaus[:, :]), inc=sem)
        if self.need_wkv:
            dstep(steps, "sync", lambda e: e.dma_start(
                out=self.wkvp[:, :], in_=self.d_wkvp[:, :]), inc=sem)
            dstep(steps, "sync", lambda e: e.dma_start(
                out=self.par[:, :], in_=self.d_par[:, :]), inc=sem)
            dstep(steps, "sync", lambda e: e.dma_start(
                out=self.wmask[:, :], in_=self.d_wmask[:, :]), inc=sem)
        return sem.n

    @staticmethod
    def host_inputs(need_wkv=False, need_caus=False):
        ones = np.zeros((128, 2), dtype=BF)
        ones[:, 0] = BF(1.0 / C)
        ones[:, 1] = BF(1.0)
        d = {
            "c_ones": ones,
            "c_onesf": np.ones((1, 128), dtype=np.float32),
            "c_ident": np.eye(128, dtype=np.float32).astype(BF),
        }
        if need_caus:
            m = np.zeros((128, 4, 512), dtype=np.uint8)
            kl = np.arange(128)[:, None]
            ql = np.arange(512)[None, :]
            for j in range(4):
                m[:, j, :] = (128 * j + kl <= ql)
            d["c_mcaus"] = m.reshape(128, 4 * 512)
        if need_wkv:
            t = np.arange(128, dtype=np.float32)
            pats = [t % 8 - 4, 3 - t % 8, t % 8, 7 - t % 8,
                    t - 64, 63 - t, t, 127 - t]
            wp = np.stack([np.broadcast_to(pp, (128, 128)) for pp in pats], 1)
            d["c_wkvp"] = np.ascontiguousarray(
                wp.reshape(128, 8 * 128)).astype(np.float32)
            par = np.zeros((128, 2, 128), dtype=np.float32)
            par[:, 0, :] = (t % 2 == 1)
            par[:, 1, :] = (t % 2 == 0)
            d["c_par"] = par.reshape(128, 2 * 128)
            s = np.arange(128)[:, None]
            tt = np.arange(128)[None, :]
            same8 = (s // 8 == tt // 8)
            adj8 = (s // 8 == tt // 8 - 1)
            m0 = same8 & (tt - s >= 2) & (tt - s <= 7)
            m1 = same8 & (tt - s == 1)
            m2 = adj8
            m3 = (s < tt) & ~same8 & ~adj8
            m4 = (s == tt)
            wm = np.stack([m0, m1, m2, m3, m4], 1).astype(np.uint8)
            d["c_wmask"] = np.ascontiguousarray(wm.reshape(128, 5 * 128))
        return d




def emit_lnres_store(steps, p, cn, ln, ychunk, xold_chunk, dep_xold, xout, t0,
                     sST, stage, dep_y=()):
    Tt = ln.Tt
    tS = emit_ln_stats(steps, cn, ln, ychunk, dep_y)
    for k in range(NCH):
        kk = k
        step(steps, "vector",
             lambda e, kk=kk: e.tensor_sub(ln.tmpn[:, :], ychunk(kk),
                                           ln.ps_bc[:, 0:Tt]),
             deps=([(ln.sT, tS)] + list(dep_xold)) if kk == 0 else ())
        step(steps, "vector",
             lambda e, kk=kk: e.tensor_mul(ln.tmpn[:, :], ln.tmpn[:, :],
                                           ln.ps_bc[:, Tt:2 * Tt]))
        step(steps, "vector",
             lambda e, kk=kk: e.tensor_add(stage[:, kk * Tt:(kk + 1) * Tt],
                                           ln.tmpn[:, :], xold_chunk(kk)),
             deps=[(sST, max(0, sST.n - 16 * (NCH - 1)))], inc=ln.sV)
        vv = ln.sV.n
        dstep(steps, "sync",
              lambda e, kk=kk, t0=t0: e.dma_start(
                  out=xout[kk * 128:(kk + 1) * 128, t0:t0 + Tt],
                  in_=stage[:, kk * Tt:(kk + 1) * Tt]),
              deps=[(ln.sV, vv)], inc=sST)




def phases_l1ffn(p, cn, xin, xout, pfx="l1f", first=False, dbg=False):
    wgd = p.dram(pfx + "_wg", [C, II], F32, "ExternalInput")
    wud = p.dram(pfx + "_wu", [C, II], F32, "ExternalInput")
    wdd = p.dram(pfx + "_wd", [II, C], F32, "ExternalInput")
    hreg = p.dram(pfx + "_h", [II, TPC], BF16,
                  "ExternalOutput" if dbg else "Internal")
    hraw = p.dram(pfx + "_hraw", [II, TPC], BF16,
                  "ExternalOutput" if dbg else "Internal") if dbg else None
    blk = {}

    def balloc(bst):
        blk["xhat"] = p.sb("xhat", [128, NCH * TPC], BF16, bst)
    p.block_begin(balloc)

    def ph_ln(pst):
        steps = []
        sLD = p.sem("ld")
        if first:
            cn.load_steps(steps, sLD)
        TL = 256
        ln = LNCtx(p, pst, TL)
        xf = [p.sb("xf", [128, NCH * TL], F32, pst) for _ in range(2)]
        lastC = {0: (0, 0), 1: (0, 0)}
        xhat = blk["xhat"]
        for ti, t0 in enumerate(range(0, TPC, TL)):
            par = ti % 2
            xb = xf[par]
            reuse = [(ln.sV, lastC[par][0]), (ln.sS, lastC[par][1])]
            for k in range(NCH):
                dstep(steps, "sync", lambda e, k=k, t0=t0, xb=xb: e.dma_start(
                    out=xb[:, k * TL:(k + 1) * TL],
                    in_=xin[k * 128:(k + 1) * 128, t0:t0 + TL]),
                    deps=reuse if k == 0 else (), inc=sLD)
            dep = [(sLD, sLD.n)]
            tS = emit_ln_stats(steps, cn, ln,
                               lambda k, xb=xb: xb[:, k * TL:(k + 1) * TL], dep)
            for k in range(NCH):
                kk = k
                step(steps, "vector",
                     lambda e, kk=kk, xb=xb: e.tensor_sub(
                         ln.tmpn[:, :], xb[:, kk * TL:(kk + 1) * TL],
                         ln.ps_bc[:, 0:TL]),
                     deps=[(ln.sT, tS)] if kk == 0 else ())
                step(steps, "vector",
                     lambda e, kk=kk, t0=t0: e.tensor_mul(
                         xhat[:, kk * TPC + t0:kk * TPC + t0 + TL],
                         ln.tmpn[:, :], ln.ps_bc[:, TL:2 * TL]),
                     inc=ln.sV)
            lastC[par] = (ln.sV.n, ln.sS.n)
        return steps

    def ph_gateup(pst):
        steps = []
        xhat = blk["xhat"]
        sW = p.sem("w")
        sMM = p.sem("mm")
        sACT = p.sem("act")
        sDVE = p.sem("dve")
        sST = p.sem("st")
        wg = [p.sb("wg", [128, NCH * 512], BF16, pst) for _ in range(2)]
        wu = [p.sb("wu", [128, NCH * 512], BF16, pst) for _ in range(2)]
        gps = [p.ps("gps", [128, TT], F32, pst) for _ in range(4)]
        ups = [p.ps("ups", [128, TT], F32, pst) for _ in range(4)]
        sil = [p.sb("sil", [128, TT], F32, pst) for _ in range(4)]
        hh = [p.sb("hh", [128, TT], F32, pst) for _ in range(4)]
        aa = [p.sb("aa", [128, TT], F32, pst) for _ in range(4)]
        t1 = p.sb("t1", [128, TT], F32, pst)
        t2 = p.sb("t2", [128, TT], F32, pst)
        t3 = p.sb("t3", [128, TT], F32, pst)
        t4 = p.sb("t4", [128, TT], F32, pst)
        tu8 = p.sb("tu8", [128, TT], U8, pst)
        ku8 = p.sb("ku8", [128, TT], U8, pst)
        zero = p.sb("zero", [128, TT], F32, pst)
        hbf = [p.sb("hbf", [128, TT], BF16, pst) for _ in range(4)]
        step(steps, "vector", lambda e: e.memset(zero[:, :], 0.0))

        NSUP = II // 512
        WS = NCH * 512
        lastA = {r: 0 for r in range(4)}
        lastD = {r: 0 for r in range(4)}
        last_mm_super = {}
        wready = {}

        def load_super(s, buf):
            dep_mm = [(sMM, last_mm_super[s - 2])] if s >= 2 else ()
            for k in range(NCH):
                dstep(steps, "gpsimd", lambda e, k=k, s=s, buf=buf: e.dma_start(
                    out=wg[buf][:, k * 512:(k + 1) * 512],
                    in_=wgd[k * 128:(k + 1) * 128, s * 512:(s + 1) * 512]),
                    deps=dep_mm if k == 0 else (), inc=sW)
                dstep(steps, "gpsimd", lambda e, k=k, s=s, buf=buf: e.dma_start(
                    out=wu[buf][:, k * 512:(k + 1) * 512],
                    in_=wud[k * 128:(k + 1) * 128, s * 512:(s + 1) * 512]),
                    inc=sW)
            wready[s] = sW.n

        load_super(0, 0)
        for s in range(NSUP):
            buf = s % 2
            if s + 1 < NSUP:
                load_super(s + 1, (s + 1) % 2)
            wv = wready[s]
            for tt in range(TPC // TT):
                t0 = tt * TT
                for r in range(4):
                    for k in range(NCH):
                        dep0 = []
                        if k == 0:
                            dep0 = [(sACT, lastA[r]), (sDVE, lastD[r])]
                            if tt == 0:
                                dep0.append((sW, wv))
                        step(steps, "tensor",
                             lambda e, k=k, r=r, t0=t0, buf=buf: e.matmul(
                                 gps[r][:, :],
                                 bass.AP(wg[buf], k * 512 + r,
                                         [[WS, 128], [4, 128]]),
                                 xhat[:, k * TPC + t0:k * TPC + t0 + TT],
                                 start=(k == 0), stop=(k == NCH - 1)),
                             deps=dep0)
                    for k in range(NCH):
                        step(steps, "tensor",
                             lambda e, k=k, r=r, t0=t0, buf=buf: e.matmul(
                                 ups[r][:, :],
                                 bass.AP(wu[buf], k * 512 + r,
                                         [[WS, 128], [4, 128]]),
                                 xhat[:, k * TPC + t0:k * TPC + t0 + TT],
                                 start=(k == 0), stop=(k == NCH - 1)),
                             inc=sMM if k == NCH - 1 else None)
                    mmv = sMM.n
                    last_mm_super[s] = mmv
                    step(steps, "scalar",
                         lambda e, r=r: e.activation(sil[r][:, :], gps[r][:, :],
                                                     AF.Silu),
                         deps=[(sMM, mmv), (sDVE, lastD[r])], inc=sACT)
                    av = sACT.n
                    lastA[r] = av
                    step(steps, "vector",
                         lambda e, r=r: e.tensor_mul(hh[r][:, :], sil[r][:, :],
                                                     ups[r][:, :]),
                         deps=[(sACT, av)], inc=sDVE)
                    lastD[r] = sDVE.n
                    step(steps, "scalar",
                         lambda e, r=r: e.activation(aa[r][:, :], hh[r][:, :],
                                                     AF.Abs),
                         deps=[(sDVE, sDVE.n)],
                         inc=sACT if r == 3 else None)
                absv = sACT.n
                if dbg:
                    for r in range(4):
                        step(steps, "vector",
                             lambda e, r=r: e.tensor_copy(hbf[r][:, :],
                                                          hh[r][:, :]),
                             inc=sDVE)
                        dv = sDVE.n
                        dstep(steps, "sync",
                              lambda e, r=r, s=s, t0=t0: e.dma_start(
                                  out=hraw[r * 2048 + s * 128:
                                           r * 2048 + (s + 1) * 128,
                                           t0:t0 + TT],
                                  in_=hbf[r][:, :]),
                              deps=[(sDVE, dv)], inc=sST)
                dep = [(sACT, absv)]
                step(steps, "vector", lambda e: e.tensor_max(
                    t1[:, :], aa[0][:, :], aa[1][:, :]), deps=dep)
                step(steps, "vector", lambda e: e.tensor_tensor(
                    t2[:, :], aa[0][:, :], aa[1][:, :], ALU.min))
                step(steps, "vector", lambda e: e.tensor_max(
                    t3[:, :], aa[2][:, :], aa[3][:, :]))
                step(steps, "vector", lambda e: e.tensor_tensor(
                    aa[0][:, :], aa[2][:, :], aa[3][:, :], ALU.min))
                step(steps, "vector", lambda e: e.tensor_tensor(
                    tu8[:, :], t1[:, :], t3[:, :], ALU.is_ge))
                step(steps, "vector", lambda e: e.select(
                    t4[:, :], tu8[:, :], t2[:, :], aa[0][:, :]))
                step(steps, "vector", lambda e: e.tensor_tensor(
                    t1[:, :], t1[:, :], t3[:, :], ALU.min))
                step(steps, "vector", lambda e: e.tensor_max(
                    t1[:, :], t1[:, :], t4[:, :]), inc=sDVE)
                tauv = sDVE.n
                step(steps, "scalar", lambda e: e.activation(
                    aa[0][:, :], hh[0][:, :], AF.Abs),
                    deps=[(sDVE, tauv)], inc=sACT)
                a0v = sACT.n
                for r in range(4):
                    dd = [(sACT, a0v)] if r == 0 else ()
                    step(steps, "vector",
                         lambda e, r=r: e.tensor_tensor(ku8[:, :], aa[r][:, :],
                                                        t1[:, :], ALU.is_ge),
                         deps=dd)
                    step(steps, "vector",
                         lambda e, r=r: e.select(hbf[r][:, :], ku8[:, :],
                                                 hh[r][:, :], zero[:, :]),
                         deps=[(sST, sST.n)], inc=sDVE)
                    dv = sDVE.n
                    dstep(steps, "sync",
                          lambda e, r=r, s=s, t0=t0: e.dma_start(
                              out=hreg[r * 2048 + s * 128:
                                       r * 2048 + (s + 1) * 128, t0:t0 + TT],
                              in_=hbf[r][:, :]),
                          deps=[(sDVE, dv)], inc=sST)
        wstep(steps, "sync", sST)
        return steps

    def make_ph_down(tsup):
        def ph_down(pst):
            steps = []
            sW = p.sem("wd")
            sH = p.sem("hl")
            sMM = p.sem("mmd")
            sDVE = p.sem("dv")
            TSUP = 1024
            y_sb = blk["ysb"]
            NJ = 8
            wd = p.sb("wdW", [128, NJ * C], BF16, pst)
            ht = p.sb("ht", [128, NJ * 256], BF16, pst)
            yps = p.ps("yps", [128, 8 * 256], F32, pst)
            tbase = tsup * TSUP
            step(steps, "vector", lambda e: e.memset(y_sb[:, :], 0.0), inc=sDVE)
            for isup in range(II // (NJ * 128)):
                dep_mm = [(sMM, sMM.n)] if isup > 0 else ()
                for j in range(NJ):
                    row0 = isup * NJ * 128 + j * 128
                    dstep(steps, "gpsimd", lambda e, j=j, row0=row0:
                          e.dma_start(out=wd[:, j * C:(j + 1) * C],
                                      in_=wdd[row0:row0 + 128, :]),
                          deps=dep_mm if j == 0 else (), inc=sW)
                wv = sW.n
                for t4 in range(4):
                    ts = tbase + t4 * 256
                    dep_h = [(sMM, sMM.n)]
                    for j in range(NJ):
                        row0 = isup * NJ * 128 + j * 128
                        dstep(steps, "sync", lambda e, j=j, row0=row0, ts=ts:
                              e.dma_start(out=ht[:, j * 256:(j + 1) * 256],
                                          in_=hreg[row0:row0 + 128, ts:ts + 256]),
                              deps=dep_h if j == 0 else (), inc=sH)
                    hv = sH.n
                    for half in range(2):
                        for c8 in range(8):
                            for j in range(NJ):
                                cc = half * 8 + c8
                                dep = []
                                if j == 0 and c8 == 0:
                                    dep = [(sW, wv), (sH, hv), (sDVE, sDVE.n)]
                                step(steps, "tensor",
                                     lambda e, j=j, cc=cc, c8=c8: e.matmul(
                                         yps[:, c8 * 256:(c8 + 1) * 256],
                                         wd[:, j * C + cc * 128:
                                            j * C + (cc + 1) * 128],
                                         ht[:, j * 256:(j + 1) * 256],
                                         start=(j == 0), stop=(j == NJ - 1)),
                                     deps=dep,
                                     inc=sMM if (j == NJ - 1 and c8 == 7) else None)
                        mmv = sMM.n
                        off = half * 8 * TSUP + t4 * 256
                        step(steps, "vector",
                             lambda e, off=off: e.tensor_add(
                                 bass.AP(y_sb, off,
                                         [[NCH * TSUP, 128], [TSUP, 8], [1, 256]]),
                                 bass.AP(y_sb, off,
                                         [[NCH * TSUP, 128], [TSUP, 8], [1, 256]]),
                                 bass.AP(yps, 0,
                                         [[8 * 256, 128], [256, 8], [1, 256]])),
                             deps=[(sMM, mmv)], inc=sDVE)
            return steps
        return ph_down

    def make_ph_lnres(tsup):
        def ph_lnres(pst):
            steps = []
            TSUP = 1024
            TR = 256
            y_sb = blk["ysb"]
            ln = LNCtx(p, pst, TR)
            xf = p.sb("xo", [128, NCH * TR], F32, pst)
            stage = p.sb("stg", [128, NCH * TR], F32, pst)
            sLD = p.sem("lx")
            sST = p.sem("sx")
            for tt in range(TSUP // TR):
                t0 = tsup * TSUP + tt * TR
                tl = tt * TR
                reuse = [(ln.sV, ln.sV.n)]
                for k in range(NCH):
                    dstep(steps, "sync", lambda e, k=k, t0=t0: e.dma_start(
                        out=xf[:, k * TR:(k + 1) * TR],
                        in_=xin[k * 128:(k + 1) * 128, t0:t0 + TR]),
                        deps=reuse if k == 0 else (), inc=sLD)
                ldv = sLD.n
                emit_lnres_store(
                    steps, p, cn, ln,
                    lambda k, tl=tl: y_sb[:, k * TSUP + tl:k * TSUP + tl + TR],
                    lambda k: xf[:, k * TR:(k + 1) * TR],
                    [(sLD, ldv)], xout, t0, sST, stage)
            wstep(steps, "sync", sST)
            return steps
        return ph_lnres

    def ph_dump(pst):
        steps = []
        sST = p.sem("dmp")
        xhat = blk["xhat"]
        for k in range(NCH):
            dstep(steps, "sync", lambda e, k=k: e.dma_start(
                out=blk["dxhat"][k * 128:(k + 1) * 128, :],
                in_=xhat[:, k * TPC:(k + 1) * TPC]), inc=sST)
        for k in range(NCH):
            dstep(steps, "sync", lambda e, k=k: e.dma_start(
                out=blk["dysb"][k * 128:(k + 1) * 128, :],
                in_=blk["ysb"][:, k * 1024:(k + 1) * 1024]), inc=sST)
        wstep(steps, "sync", sST)
        return steps

    p.phase(pfx + "_ln", ph_ln)
    if dbg:
        blk["dxhat"] = p.dram(pfx + "_dxhat", [C, TPC], BF16, "ExternalOutput")
        blk["dysb"] = p.dram(pfx + "_dysb", [C, 1024], F32, "ExternalOutput")
        p.phase(pfx + "_dump0", ph_dump)
    p.phase(pfx + "_gu", ph_gateup)
    def balloc2(bst):
        blk["ysb"] = p.sb("ysb", [128, NCH * 1024], F32, bst)
    p.block_begin(balloc2)
    for tsup in range(2):
        p.phase(pfx + f"_dn{tsup}", make_ph_down(tsup))
        if dbg and tsup == 1:
            p.phase(pfx + "_dump1", ph_dump)
        p.phase(pfx + f"_lr{tsup}", make_ph_lnres(tsup))
    p.block_end()
    p.block_end()

    def host_inputs(inputs):
        wg = np.ascontiguousarray(inputs["L1_ffn_w_gate"], dtype=np.float32)
        wu = np.ascontiguousarray(inputs["L1_ffn_w_up"], dtype=np.float32)
        wd = np.asarray(inputs["L1_ffn_w_down"], dtype=np.float32)
        perm = (4 * (np.arange(II) % 2048) + np.arange(II) // 2048)
        wd = np.ascontiguousarray(wd[perm])
        return {pfx + "_wg": wg, pfx + "_wu": wu, pfx + "_wd": wd}

    return host_inputs






def _fm_shard(x_btc, c):
    b, half = c // 2, c % 2
    sl = np.asarray(x_btc[b, half * TPC:(half + 1) * TPC, :], dtype=np.float32)
    return np.ascontiguousarray(sl.T)


def _unshard(shards):
    out = np.empty((B, T, C), dtype=np.float32)
    for c in range(8):
        b, half = c // 2, c % 2
        out[b, half * TPC:(half + 1) * TPC, :] = shards[c].T
    return out


def build_prog_l1ffn(dbg=False):
    p = Prog()
    cn = ConstSB(p)
    cn.alloc()
    xin = p.dram("xin", [C, TPC], F32, "ExternalInput")
    xout = p.dram("xout", [C, TPC], F32, "ExternalOutput")
    hin = phases_l1ffn(p, cn, xin, xout, first=True, dbg=dbg)
    nc = p.build()
    return nc, hin


_prog_cache = {}


def _get_prog(name, builder):
    if name not in _prog_cache:
        _prog_cache[name] = builder()
    return _prog_cache[name]


def run_l1ffn(x_shards, inputs, dbg=False):
    nc, hin = _get_prog("l1ffn_dbg" if dbg else "l1ffn",
                        lambda: build_prog_l1ffn(dbg))
    wmap = hin(inputs)
    cmap = ConstSB.host_inputs()
    in_maps = []
    for c in range(8):
        m = {"xin": x_shards[c]}
        m.update(wmap)
        m.update(cmap)
        in_maps.append(m)
    res = run_bass_kernel_spmd(nc, in_maps, core_ids=list(range(8)))
    if dbg:
        return res
    return [res.results[c]["xout"] for c in range(8)]


def kernel(**inputs):
    x = [_fm_shard(np.asarray(inputs["L0_attn_x"]), c) for c in range(8)]
    x = run_l0attn(x, inputs)
    x = run_l0ffn(x, inputs)
    x = run_l1attn(x, inputs)
    x = run_l1ffn(x, inputs)
    return _unshard(x)




def phases_l0ffn(p, cn, xin, xout, pfx="l0f", first=False):
    wgd = p.dram(pfx + "_wg", [C, II], I32, "ExternalInput")
    wud = p.dram(pfx + "_wu", [C, II], I32, "ExternalInput")
    wdd = p.dram(pfx + "_wd", [II, C], I32, "ExternalInput")
    sgd = p.dram(pfx + "_sg", [128, II // 128], F32, "ExternalInput")
    sud = p.dram(pfx + "_su", [128, II // 128], F32, "ExternalInput")
    sdd = p.dram(pfx + "_sd", [128, NCH], F32, "ExternalInput")
    hreg = p.dram(pfx + "_h", [II, TPC], BF16)
    blk = {}

    def balloc(bst):
        blk["xhat"] = p.sb(pfx + "xhat", [128, NCH * TPC], BF16, bst)
        blk["hsum"] = p.sb(pfx + "hsum", [1, TPC], F32, bst)
        blk["scales"] = p.sb(pfx + "scl", [128, 2 * (II // 128) + NCH], F32, bst)
        blk["onesbf"] = p.sb(pfx + "ob", [1, 128], BF16, bst)
    p.block_begin(balloc)
    NIC = II // 128

    def ph_ln(pst):
        steps = []
        sLD = p.sem("ld")
        if first:
            cn.load_steps(steps, sLD)
        dstep(steps, "sync", lambda e: e.dma_start(
            out=blk["scales"][:, 0:NIC], in_=sgd[:, :]), inc=sLD)
        dstep(steps, "sync", lambda e: e.dma_start(
            out=blk["scales"][:, NIC:2 * NIC], in_=sud[:, :]), inc=sLD)
        dstep(steps, "sync", lambda e: e.dma_start(
            out=blk["scales"][:, 2 * NIC:2 * NIC + NCH], in_=sdd[:, :]), inc=sLD)
        step(steps, "vector", lambda e: e.memset(blk["onesbf"][:, :], 1.0))
        TL = 256
        ln = LNCtx(p, pst, TL)
        xf = [p.sb("xf", [128, NCH * TL], F32, pst) for _ in range(2)]
        lastC = {0: (0, 0), 1: (0, 0)}
        xhat = blk["xhat"]
        for ti, t0 in enumerate(range(0, TPC, TL)):
            par = ti % 2
            xb = xf[par]
            reuse = [(ln.sV, lastC[par][0]), (ln.sS, lastC[par][1])]
            for k in range(NCH):
                dstep(steps, "sync", lambda e, k=k, t0=t0, xb=xb: e.dma_start(
                    out=xb[:, k * TL:(k + 1) * TL],
                    in_=xin[k * 128:(k + 1) * 128, t0:t0 + TL]),
                    deps=reuse if k == 0 else (), inc=sLD)
            dep = [(sLD, sLD.n)]
            tS = emit_ln_stats(steps, cn, ln,
                               lambda k, xb=xb: xb[:, k * TL:(k + 1) * TL], dep)
            for k in range(NCH):
                kk = k
                step(steps, "vector",
                     lambda e, kk=kk, xb=xb: e.tensor_sub(
                         ln.tmpn[:, :], xb[:, kk * TL:(kk + 1) * TL],
                         ln.ps_bc[:, 0:TL]),
                     deps=[(ln.sT, tS)] if kk == 0 else ())
                step(steps, "vector",
                     lambda e, kk=kk, t0=t0: e.tensor_mul(
                         xhat[:, kk * TPC + t0:kk * TPC + t0 + TL],
                         ln.tmpn[:, :], ln.ps_bc[:, TL:2 * TL]),
                     inc=ln.sV)
            lastC[par] = (ln.sV.n, ln.sS.n)
        return steps

    def ph_gateup(pst):
        steps = []
        xhat = blk["xhat"]
        scl = blk["scales"]
        sMM = p.sem("mm")
        sACT = p.sem("act")
        sDVE = p.sem("dve")
        sST = p.sem("st")
        sW = p.sem("w")
        wg = [p.sb("wg", [128, NCH * 512], BF16, pst) for _ in range(2)]
        wu = [p.sb("wu", [128, NCH * 512], BF16, pst) for _ in range(2)]
        gps = p.ps("gps", [128, TT], F32, pst)
        ups = p.ps("ups", [128, TT], F32, pst)
        hs_ps = p.ps("hsps", [1, TPC], F32, pst)
        xs_ps = p.ps("xsps", [1, TT], F32, pst)
        sil = p.sb("sil", [128, TT], F32, pst)
        hbf = p.sb("hbf2", [128, TT], BF16, pst)
        xsum = p.sb("xsum", [1, 4 * TT], BF16, pst)
        WS = NCH * 512

        for tt in range(4):
            t0 = tt * TT
            for k in range(NCH):
                step(steps, "tensor",
                     lambda e, k=k, t0=t0: e.matmul(
                         xs_ps[0:1, :], cn.ones_col,
                         xhat[:, k * TPC + t0:k * TPC + t0 + TT],
                         start=(k == 0), stop=(k == NCH - 1)),
                     deps=[(sACT, sACT.n)] if k == 0 else (),
                     inc=sMM if k == NCH - 1 else None)
            mv = sMM.n
            step(steps, "scalar",
                 lambda e, tt=tt: e.activation(xsum[0:1, tt * TT:(tt + 1) * TT],
                                               xs_ps[0:1, :], AF.Copy,
                                               scale=-128.0),
                 deps=[(sMM, mv)], inc=sACT)

        NSUP = II // 512
        last_mm_super = {}
        wready = {}

        def load_super(s, buf):
            dep_mm = [(sMM, last_mm_super[s - 2])] if s >= 2 else ()
            for k in range(NCH):
                dstep(steps, "gpsimd", lambda e, k=k, s=s, buf=buf: e.dma_start(
                    out=wg[buf][:, k * 512:(k + 1) * 512],
                    in_=wgd[k * 128:(k + 1) * 128, s * 512:(s + 1) * 512]),
                    deps=dep_mm if k == 0 else (), inc=sW)
                dstep(steps, "gpsimd", lambda e, k=k, s=s, buf=buf: e.dma_start(
                    out=wu[buf][:, k * 512:(k + 1) * 512],
                    in_=wud[k * 128:(k + 1) * 128, s * 512:(s + 1) * 512]),
                    inc=sW)
            wready[s] = sW.n

        load_super(0, 0)
        for s in range(NSUP):
            buf = s % 2
            if s + 1 < NSUP:
                load_super(s + 1, (s + 1) % 2)
            wv = wready[s]
            for tt in range(TPC // TT):
                t0 = tt * TT
                for ic in range(4):
                    icg = s * 4 + ic
                    for k in range(NCH):
                        dep0 = []
                        if k == 0:
                            dep0 = [(sACT, sACT.n), (sDVE, sDVE.n), (sW, wv)]
                        step(steps, "tensor",
                             lambda e, k=k, ic=ic, t0=t0, buf=buf: e.matmul(
                                 gps[:, :],
                                 wg[buf][:, k * 512 + ic * 128:
                                         k * 512 + (ic + 1) * 128],
                                 xhat[:, k * TPC + t0:k * TPC + t0 + TT],
                                 start=(k == 0), stop=False),
                             deps=dep0)
                    step(steps, "tensor",
                         lambda e, tt=tt, t0=t0: e.matmul(
                             gps[:, :], blk["onesbf"][0:1, :],
                             xsum[0:1, t0:t0 + TT],
                             start=False, stop=True))
                    for k in range(NCH):
                        step(steps, "tensor",
                             lambda e, k=k, ic=ic, t0=t0, buf=buf: e.matmul(
                                 ups[:, :],
                                 wu[buf][:, k * 512 + ic * 128:
                                         k * 512 + (ic + 1) * 128],
                                 xhat[:, k * TPC + t0:k * TPC + t0 + TT],
                                 start=(k == 0), stop=False))
                    step(steps, "tensor",
                         lambda e, tt=tt, t0=t0: e.matmul(
                             ups[:, :], blk["onesbf"][0:1, :],
                             xsum[0:1, t0:t0 + TT],
                             start=False, stop=True),
                         inc=sMM)
                    mmv = sMM.n
                    step(steps, "scalar",
                         lambda e, icg=icg: e.activation(
                             sil[:, :], gps[:, :], AF.Silu,
                             scale=scl[:, icg:icg + 1]),
                         deps=[(sMM, mmv), (sDVE, sDVE.n)], inc=sACT)
                    av = sACT.n
                    step(steps, "vector",
                         lambda e, icg=icg: e.scalar_tensor_tensor(
                             hbf[:, :], ups[:, :],
                             scl[:, NIC + icg:NIC + icg + 1], sil[:, :],
                             ALU.mult, ALU.mult),
                         deps=[(sACT, av), (sMM, sMM.n), (sST, sST.n)],
                         inc=sDVE)
                    dv = sDVE.n
                    step(steps, "tensor",
                         lambda e, t0=t0: e.matmul(
                             hs_ps[0:1, t0:t0 + TT], cn.ones_col, hbf[:, :],
                             start=(s == 0 and ic == 0),
                             stop=(s == NSUP - 1 and ic == 3)),
                         deps=[(sDVE, dv)],
                         inc=sMM)
                    last_mm_super[s] = sMM.n
                    dstep(steps, "sync",
                          lambda e, icg=icg, t0=t0: e.dma_start(
                              out=hreg[icg * 128:(icg + 1) * 128, t0:t0 + TT],
                              in_=hbf[:, :]),
                          deps=[(sDVE, dv)], inc=sST)
        mmv = sMM.n
        step(steps, "scalar",
             lambda e: e.activation(blk["hsum"][0:1, :], hs_ps[0:1, :],
                                    AF.Copy, scale=128.0),
             deps=[(sMM, mmv)], inc=sACT)
        wstep(steps, "sync", sST)
        return steps

    def make_ph_down(tsup):
        def ph_down(pst):
            steps = []
            sW = p.sem("wd")
            sH = p.sem("hl")
            sMM = p.sem("mmd")
            sDVE = p.sem("dv")
            TSUP = 1024
            y_sb = blk["ysb"]
            NJ = 8
            wd = p.sb("wdW", [128, NJ * C], BF16, pst)
            ht = p.sb("ht", [128, NJ * 256], BF16, pst)
            yps = p.ps("yps", [128, 8 * 256], F32, pst)
            tbase = tsup * TSUP
            step(steps, "vector", lambda e: e.memset(y_sb[:, :], 0.0), inc=sDVE)
            for isup in range(II // (NJ * 128)):
                dep_mm = [(sMM, sMM.n)] if isup > 0 else ()
                for j in range(NJ):
                    row0 = isup * NJ * 128 + j * 128
                    dstep(steps, "gpsimd", lambda e, j=j, row0=row0:
                          e.dma_start(out=wd[:, j * C:(j + 1) * C],
                                      in_=wdd[row0:row0 + 128, :]),
                          deps=dep_mm if j == 0 else (), inc=sW)
                wv = sW.n
                for t4 in range(4):
                    ts = tbase + t4 * 256
                    dep_h = [(sMM, sMM.n)]
                    for j in range(NJ):
                        row0 = isup * NJ * 128 + j * 128
                        dstep(steps, "sync", lambda e, j=j, row0=row0, ts=ts:
                              e.dma_start(out=ht[:, j * 256:(j + 1) * 256],
                                          in_=hreg[row0:row0 + 128, ts:ts + 256]),
                              deps=dep_h if j == 0 else (), inc=sH)
                    hv = sH.n
                    for half in range(2):
                        for c8 in range(8):
                            for j in range(NJ):
                                cc = half * 8 + c8
                                dep = []
                                if j == 0 and c8 == 0:
                                    dep = [(sW, wv), (sH, hv), (sDVE, sDVE.n)]
                                step(steps, "tensor",
                                     lambda e, j=j, cc=cc, c8=c8: e.matmul(
                                         yps[:, c8 * 256:(c8 + 1) * 256],
                                         wd[:, j * C + cc * 128:
                                            j * C + (cc + 1) * 128],
                                         ht[:, j * 256:(j + 1) * 256],
                                         start=(j == 0), stop=(j == NJ - 1)),
                                     deps=dep,
                                     inc=sMM if (j == NJ - 1 and c8 == 7) else None)
                        mmv = sMM.n
                        off = half * 8 * TSUP + t4 * 256
                        step(steps, "vector",
                             lambda e, off=off: e.tensor_add(
                                 bass.AP(y_sb, off,
                                         [[NCH * TSUP, 128], [TSUP, 8], [1, 256]]),
                                 bass.AP(y_sb, off,
                                         [[NCH * TSUP, 128], [TSUP, 8], [1, 256]]),
                                 bass.AP(yps, 0,
                                         [[8 * 256, 128], [256, 8], [1, 256]])),
                             deps=[(sMM, mmv)], inc=sDVE)
            return steps
        return ph_down

    def make_ph_lnres(tsup):
        def ph_lnres(pst):
            steps = []
            TSUP = 1024
            TR = 128
            y_sb = blk["ysb"]
            scl = blk["scales"]
            ln = LNCtx(p, pst, TR)
            xf = p.sb("xo", [128, NCH * TR], F32, pst)
            yc = p.sb("yc", [128, NCH * TR], F32, pst)
            stage = p.sb("stg", [128, NCH * TR], F32, pst)
            hs_bc = p.ps("hsbc", [128, TR], F32, pst)
            sLD = p.sem("lx")
            sST = p.sem("sx")
            sTb = p.sem("tb")
            sVc = p.sem("vc")
            for tt in range(TSUP // TR):
                t0 = tsup * TSUP + tt * TR
                tl = tt * TR
                reuse = [(ln.sV, ln.sV.n)]
                for k in range(NCH):
                    dstep(steps, "sync", lambda e, k=k, t0=t0: e.dma_start(
                        out=xf[:, k * TR:(k + 1) * TR],
                        in_=xin[k * 128:(k + 1) * 128, t0:t0 + TR]),
                        deps=reuse if k == 0 else (), inc=sLD)
                ldv = sLD.n
                step(steps, "tensor",
                     lambda e, t0=t0: e.matmul(
                         hs_bc[:, :], cn.one_row_f32[0:1, :],
                         blk["hsum"][0:1, t0:t0 + TR],
                         start=True, stop=True),
                     deps=[(sVc, sVc.n)], inc=sTb)
                tbv = sTb.n
                for k in range(NCH):
                    step(steps, "vector",
                         lambda e, k=k, tl=tl: e.tensor_sub(
                             yc[:, k * TR:(k + 1) * TR],
                             y_sb[:, k * TSUP + tl:k * TSUP + tl + TR],
                             hs_bc[:, :]),
                         deps=[(sTb, tbv)] if k == 0 else ())
                    step(steps, "vector",
                         lambda e, k=k: e.tensor_scalar_mul(
                             yc[:, k * TR:(k + 1) * TR],
                             yc[:, k * TR:(k + 1) * TR],
                             scl[:, 2 * NIC + k:2 * NIC + k + 1]),
                         inc=sVc)
                emit_lnres_store(
                    steps, p, cn, ln,
                    lambda k: yc[:, k * TR:(k + 1) * TR],
                    lambda k: xf[:, k * TR:(k + 1) * TR],
                    [(sLD, ldv)], xout, t0, sST, stage,
                    dep_y=[(sVc, sVc.n)])
            wstep(steps, "sync", sST)
            return steps
        return ph_lnres

    p.phase(pfx + "_ln", ph_ln)
    p.phase(pfx + "_gu", ph_gateup)

    def balloc2(bst):
        blk["ysb"] = p.sb(pfx + "ysb", [128, NCH * 1024], F32, bst)
    p.block_begin(balloc2)
    for tsup in range(2):
        p.phase(pfx + f"_dn{tsup}", make_ph_down(tsup))
        p.phase(pfx + f"_lr{tsup}", make_ph_lnres(tsup))
    p.block_end()
    p.block_end()

    def host_inputs(inputs):
        wg = np.ascontiguousarray(
            np.asarray(inputs["L0_ffn_gate_w"], dtype=np.int32).T)
        wu = np.ascontiguousarray(
            np.asarray(inputs["L0_ffn_up_w"], dtype=np.int32).T)
        wd = np.ascontiguousarray(
            np.asarray(inputs["L0_ffn_down_w"], dtype=np.int32).T)
        sg = np.ascontiguousarray(np.asarray(
            inputs["L0_ffn_gate_scale"], np.float32).reshape(II // 128, 128).T)
        su = np.ascontiguousarray(np.asarray(
            inputs["L0_ffn_up_scale"], np.float32).reshape(II // 128, 128).T)
        sd = np.ascontiguousarray(np.asarray(
            inputs["L0_ffn_down_scale"], np.float32).reshape(NCH, 128).T)
        return {pfx + "_wg": wg, pfx + "_wu": wu, pfx + "_wd": wd,
                pfx + "_sg": sg, pfx + "_su": su, pfx + "_sd": sd}

    return host_inputs


def build_prog_l0ffn():
    p = Prog()
    cn = ConstSB(p)
    cn.alloc()
    xin = p.dram("xin", [C, TPC], F32, "ExternalInput")
    xout = p.dram("xout", [C, TPC], F32, "ExternalOutput")
    hin = phases_l0ffn(p, cn, xin, xout, first=True)
    nc = p.build()
    return nc, hin


def run_l0ffn(x_shards, inputs):
    nc, hin = _get_prog("l0ffn", build_prog_l0ffn)
    wmap = hin(inputs)
    cmap = ConstSB.host_inputs()
    in_maps = []
    for c in range(8):
        m = {"xin": x_shards[c]}
        m.update(wmap)
        m.update(cmap)
        in_maps.append(m)
    res = run_bass_kernel_spmd(nc, in_maps, core_ids=list(range(8)))
    return [res.results[c]["xout"] for c in range(8)]




def phases_l1attn(p, cn, xin, xout, pfx="l1a", first=False):
    wqkvd = p.dram(pfx + "_wqkv", [C, 3 * MYC], F32, "ExternalInput")
    bqkvd = p.dram(pfx + "_bqkv", [128, 24], F32, "ExternalInput")
    wod = p.dram(pfx + "_wo", [C, C], F32, "ExternalInput")
    bod = p.dram(pfx + "_bo", [128, NCH], F32, "ExternalInput")
    t0d = p.dram(pfx + "_t0", [1, 1], U32, "ExternalInput")
    ag_in = p.dram(pfx + "_agi", [C, TPC], BF16)
    ag_out = p.dram(pfx + "_ago", [2 * C, TPC], BF16)
    qkvreg = p.dram(pfx + "_qkv", [3 * MYC, T], BF16)
    ogreg = p.dram(pfx + "_og", [MYC, T], BF16)
    og_gath = p.dram(pfx + "_ogg", [2 * MYC, T], BF16)
    og_loc = p.dram(pfx + "_ogl", [C, TPC], BF16)

    blk = {}

    def balloc(bst):
        blk["bq"] = p.sb(pfx + "bq", [128, 24 + NCH], F32, bst)
        blk["t0s"] = p.sb(pfx + "t0s", [1, 1], U32, bst)
    p.block_begin(balloc)
    blk["r_t0"] = p.stack.enter_context(p.nc.gpsimd.register(pfx + "rt0"))
    blk["r_tmp"] = p.stack.enter_context(p.nc.gpsimd.register(pfx + "rtmp"))
    blk["r_sy"] = p.stack.enter_context(p.nc.sync.register(pfx + "rsyA"))
    blk["r_sy2"] = p.stack.enter_context(p.nc.sync.register(pfx + "rsyB"))

    def ph_lng(pst):
        steps = []
        sLD = p.sem("ld")
        sST = p.sem("st")
        sCC = p.usem("cc")
        if first:
            cn.load_steps(steps, sLD)
        dstep(steps, "sync", lambda e: e.dma_start(
            out=blk["bq"][:, 0:24], in_=bqkvd[:, :]), inc=sLD)
        dstep(steps, "sync", lambda e: e.dma_start(
            out=blk["bq"][:, 24:24 + NCH], in_=bod[:, :]), inc=sLD)
        dstep(steps, "gpsimd", lambda e: e.dma_start(
            out=blk["t0s"][:, :], in_=t0d[:, :]), inc=sLD)
        ldt = sLD.n
        step(steps, "gpsimd",
             lambda e: e.reg_load(blk["r_t0"], blk["t0s"][0:1, 0:1]),
             deps=[(sLD, ldt)])
        TL = 256
        ln = LNCtx(p, pst, TL)
        xf = p.sb("xf", [128, NCH * TL], F32, pst)
        xbstage = p.sb("xbs", [128, NCH * TL], BF16, pst)
        xf2 = [xf, p.sb("xfb", [128, NCH * TL], F32, pst)]
        xs2 = [xbstage, p.sb("xbsb", [128, NCH * TL], BF16, pst)]
        lastC = {0: (0, 0, 0), 1: (0, 0, 0)}
        for ti, t0 in enumerate(range(0, TPC, TL)):
            par = ti % 2
            xb, xs = xf2[par], xs2[par]
            reuse = [(ln.sV, lastC[par][0]), (ln.sS, lastC[par][1])]
            for k in range(NCH):
                dstep(steps, "sync", lambda e, k=k, t0=t0, xb=xb: e.dma_start(
                    out=xb[:, k * TL:(k + 1) * TL],
                    in_=xin[k * 128:(k + 1) * 128, t0:t0 + TL]),
                    deps=reuse if k == 0 else (), inc=sLD)
            dep = [(sLD, sLD.n)]
            tS = emit_ln_stats(steps, cn, ln,
                               lambda k, xb=xb: xb[:, k * TL:(k + 1) * TL], dep)
            for k in range(NCH):
                kk = k
                step(steps, "vector",
                     lambda e, kk=kk, xb=xb: e.tensor_sub(
                         ln.tmpn[:, :], xb[:, kk * TL:(kk + 1) * TL],
                         ln.ps_bc[:, 0:TL]),
                     deps=([(ln.sT, tS), (sST, lastC[par][2])]
                           if kk == 0 else ()))
                step(steps, "vector",
                     lambda e, kk=kk, xs=xs: e.tensor_mul(
                         xs[:, kk * TL:(kk + 1) * TL],
                         ln.tmpn[:, :], ln.ps_bc[:, TL:2 * TL]),
                     inc=ln.sV)
                vv = ln.sV.n
                dstep(steps, "sync",
                      lambda e, kk=kk, t0=t0, xs=xs: e.dma_start(
                          out=ag_in[kk * 128:(kk + 1) * 128, t0:t0 + TL],
                          in_=xs[:, kk * TL:(kk + 1) * TL]),
                      deps=[(ln.sV, vv)], inc=sST)
            lastC[par] = (ln.sV.n, ln.sS.n, sST.n)
        wstep(steps, "sync", sST)
        step(steps, "gpsimd", lambda e: e.collective_compute(
            "AllGather", ALU.bypass, replica_groups=GROUPS,
            ins=[ag_in.ap().opt()], outs=[ag_out.ap().opt()]),
            deps=[(sST, sST.n)], inc=sCC)
        steps.append(Step("gpsimd", lambda e: e.nop(), [(sCC, 1)], None, 0))
        return steps

    def ph_qkv(pst):
        steps = []
        sW = p.sem("w")
        sX = p.sem("x")
        sMM = p.sem("mm")
        sACT = p.sem("act")
        sST = p.sem("st")
        wq = p.sb("wqkv", [128, NCH * 3 * MYC // 16 * 16], BF16, pst)
        xg = p.sb("xg", [128, NCH * TT], BF16, pst)
        og1 = p.sb("og1", [128, TT], BF16, pst)
        og2 = p.sb("og2", [128, TT], BF16, pst)
        pss = [p.ps("qps", [128, TT], F32, pst) for _ in range(2)]
        for k in range(NCH):
            dstep(steps, "gpsimd", lambda e, k=k: e.dma_start(
                out=wq[:, k * 3072:(k + 1) * 3072],
                in_=wqkvd[k * 128:(k + 1) * 128, :]), inc=sW)
        wv = sW.n
        stages = [og1, og2]
        for tt in range(T // TT):
            slab, tl = tt // 4, (tt % 4) * TT
            for k in range(NCH):
                dstep(steps, "sync", lambda e, k=k, slab=slab, tl=tl: e.dma_start(
                    out=xg[:, k * TT:(k + 1) * TT],
                    in_=ag_out[slab * C + k * 128:slab * C + (k + 1) * 128,
                               tl:tl + TT]),
                    deps=[(sMM, sMM.n)] if k == 0 else (), inc=sX)
            xv = sX.n
            for oc in range(24):
                pb = (tt * 24 + oc) % 2
                for k in range(NCH):
                    dep = []
                    if k == 0:
                        dep = [(sX, xv), (sACT, max(0, sACT.n - 1))]
                        if tt == 0 and oc == 0:
                            dep.append((sW, wv))
                    step(steps, "tensor",
                         lambda e, k=k, oc=oc, pb=pb: e.matmul(
                             pss[pb][:, :],
                             wq[:, k * 3072 + oc * 128:k * 3072 + (oc + 1) * 128],
                             xg[:, k * TT:(k + 1) * TT],
                             start=(k == 0), stop=(k == NCH - 1)),
                         deps=dep, inc=sMM if k == NCH - 1 else None)
                mmv = sMM.n
                step(steps, "vector",
                     lambda e, oc=oc, pb=pb: e.tensor_scalar_add(
                         stages[pb][:, :], pss[pb][:, :],
                         blk["bq"][:, oc:oc + 1]),
                     deps=[(sMM, mmv), (sST, max(0, sST.n - 16))], inc=sACT)
                av = sACT.n
                dstep(steps, "sync",
                      lambda e, oc=oc, tt=tt, pb=pb: e.dma_start(
                          out=qkvreg[oc * 128:(oc + 1) * 128,
                                     tt * TT:(tt + 1) * TT],
                          in_=stages[pb][:, :]),
                      deps=[(sACT, av)], inc=sST)
        wstep(steps, "sync", sST)
        return steps

    def ph_attn(pst):
        steps = []
        sL = p.sem("l")
        sMM = p.sem("mm")
        sACT = p.sem("act")
        sDVE = p.sem("dve")
        sST = p.sem("st")
        sTR = p.sem("tr")
        qf = p.sb("qf", [128, T], BF16, pst)
        kf = p.sb("kf", [128, T], BF16, pst)
        vf = p.sb("vf", [128, T], BF16, pst)
        vt = p.sb("vt", [128, T], BF16, pst)
        ogs = p.sb("ogs", [128, T], BF16, pst)
        pt = [p.sb("pt", [128, TT], BF16, pst) for _ in range(2)]
        ptm = p.sb("ptm", [128, TT], BF16, pst)
        zbf = p.sb("zbf", [128, TT], BF16, pst)
        srow = p.sb("srow", [1, TT], F32, pst)
        rrow = p.sb("rrow", [1, TT], F32, pst)
        rbsb = p.sb("rbsb", [128, TT], F32, pst)
        ps_s = [p.ps("sps", [128, TT], F32, pst) for _ in range(2)]
        ps_o = p.ps("ops", [128, TT], F32, pst)
        ps_sg = p.ps("sgps", [1, TT], F32, pst)
        ps_rb = p.ps("rbps", [128, TT], F32, pst)
        ps_tr = p.ps("trps", [128, 128], BF16, pst)
        step(steps, "vector", lambda e: e.memset(zbf[:, :], 0.0))
        for hl in range(MYH):
            dep_re = [(sMM, sMM.n), (sDVE, sDVE.n)]
            dstep(steps, "sync", lambda e, hl=hl: e.dma_start(
                out=qf[:, :], in_=qkvreg[hl * 128:(hl + 1) * 128, :]),
                deps=dep_re, inc=sL)
            dstep(steps, "sync", lambda e, hl=hl: e.dma_start(
                out=kf[:, :], in_=qkvreg[MYC + hl * 128:MYC + (hl + 1) * 128, :]),
                inc=sL)
            dstep(steps, "sync", lambda e, hl=hl: e.dma_start(
                out=vf[:, :],
                in_=qkvreg[2 * MYC + hl * 128:2 * MYC + (hl + 1) * 128, :]),
                inc=sL)
            lv = sL.n
            for w in range(T // 128):
                step(steps, "tensor",
                     lambda e, w=w: e.transpose(ps_tr[:, :],
                                                vf[:, w * 128:(w + 1) * 128],
                                                cn.ident[:, :]),
                     deps=([(sL, lv)] if w == 0 else []) +
                          [(sDVE, sDVE.n)],
                     inc=sTR)
                tv = sTR.n
                step(steps, "vector",
                     lambda e, w=w: e.tensor_copy(vt[:, w * 128:(w + 1) * 128],
                                                  ps_tr[:, :]),
                     deps=[(sTR, tv)], inc=sDVE)
            vtv = sDVE.n
            for qt in range(T // TT):
                kcmax = 4 * qt + 3
                for kc in range(kcmax + 1):
                    pb = kc % 2
                    diag = (kc >= 4 * qt)
                    dep = [(sACT, max(0, sACT.n - 1))]
                    if kc == 0:
                        dep.append((sL, lv))
                    step(steps, "tensor",
                         lambda e, kc=kc, qt=qt, pb=pb: e.matmul(
                             ps_s[pb][:, :],
                             kf[:, kc * 128:(kc + 1) * 128],
                             qf[:, qt * TT:(qt + 1) * TT],
                             start=True, stop=True),
                         deps=dep, inc=sMM)
                    mmv = sMM.n
                    step(steps, "scalar",
                         lambda e, kc=kc, pb=pb: e.activation(
                             pt[pb][:, :], ps_s[pb][:, :], AF.Exp,
                             scale=SQS, bias=cn.negc0_col[:, 0:1]),
                         deps=[(sMM, mmv), (sDVE, sDVE.n),
                               (sTR, sTR.n)], inc=sACT)
                    av = sACT.n
                    buf = pt[pb]
                    if diag:
                        j = kc - 4 * qt
                        step(steps, "vector",
                             lambda e, j=j, pb=pb: e.select(
                                 ptm[:, :],
                                 cn.mcaus[:, j * 512:(j + 1) * 512],
                                 pt[pb][:, :], zbf[:, :]),
                             deps=[(sACT, av), (sMM, sMM.n)], inc=sDVE)
                        buf = ptm
                    bdep = [(sACT, av)] if not diag else [(sDVE, sDVE.n)]
                    if kc == 0:
                        bdep = bdep + [(sACT, sACT.n), (sDVE, sDVE.n)]
                    step(steps, "tensor",
                         lambda e, buf=buf, kc=kc, qt=qt: e.matmul(
                             ps_sg[0:1, :], cn.ones_col, buf[:, :],
                             start=(kc == 0), stop=(kc == kcmax)),
                         deps=bdep)
                    step(steps, "tensor",
                         lambda e, buf=buf, kc=kc, qt=qt: e.matmul(
                             ps_o[:, :], vt[:, kc * 128:(kc + 1) * 128],
                             buf[:, :],
                             start=(kc == 0), stop=(kc == kcmax)),
                         inc=sMM)
                mmv = sMM.n
                step(steps, "scalar",
                     lambda e: e.copy(srow[:, :], ps_sg[0:1, :]),
                     deps=[(sMM, mmv)])
                step(steps, "scalar",
                     lambda e: e.activation(rrow[:, :], srow[:, :], AF.Ln))
                step(steps, "scalar",
                     lambda e: e.activation(rrow[:, :], rrow[:, :], AF.Exp,
                                            scale=-1.0), inc=sACT)
                av = sACT.n
                step(steps, "tensor",
                     lambda e: e.matmul(ps_rb[:, :], cn.one_row_f32[0:1, :],
                                        rrow[0:1, :], start=True, stop=True),
                     deps=[(sACT, av), (sDVE, sDVE.n)], inc=sMM)
                mmv2 = sMM.n
                step(steps, "scalar",
                     lambda e: e.copy(rbsb[:, :], ps_rb[:, :]),
                     deps=[(sMM, mmv2)], inc=sACT)
                av2 = sACT.n
                step(steps, "vector",
                     lambda e, qt=qt: e.tensor_mul(
                         ogs[:, qt * TT:(qt + 1) * TT], ps_o[:, :],
                         rbsb[:, :]),
                     deps=[(sACT, av2), (sST, sST.n)], inc=sDVE)
            dv = sDVE.n
            dstep(steps, "sync", lambda e, hl=hl: e.dma_start(
                out=ogreg[hl * 128:(hl + 1) * 128, :], in_=ogs[:, :]),
                deps=[(sDVE, dv)], inc=sST)
        wstep(steps, "sync", sST)
        return steps

    def ph_oexch(pst):
        steps = []
        sCC = p.usem("cc")
        sCP = p.sem("cp")
        step(steps, "gpsimd", lambda e: e.collective_compute(
            "AllGather", ALU.bypass, replica_groups=GROUPS,
            ins=[ogreg.ap().opt()], outs=[og_gath.ap().opt()]),
            inc=sCC)
        steps.append(Step("gpsimd", lambda e: e.nop(), [(sCC, 1)], None, 0))
        def ldreg(e):
            e.reg_load(blk["r_sy"], blk["t0s"][0:1, 0:1])
            return e.nop()
        steps.append(Step("sync", ldreg, [(sCC, 1)], None, 0))
        for i in range(8):
            def cp(e, i=i):
                e.reg_mov(blk["r_sy2"], i * 256 * T)
                e.reg_add(blk["r_sy2"], blk["r_sy2"], blk["r_sy"])
                return e.dma_start(
                    out=og_loc[i * 256:(i + 1) * 256, :],
                    in_=bass.AP(og_gath, blk["r_sy2"], [[T, 256], [1, TPC]]))
            dstep(steps, "sync", cp, inc=sCP)
        wstep(steps, "sync", sCP)
        return steps

    def ph_out(pst):
        steps = []
        sW = p.sem("w")
        sX = p.sem("x")
        sMM = p.sem("mm")
        sACT = p.sem("act")
        sLD = p.sem("lx")
        sST = p.sem("sx")
        TR = 256
        wo = p.sb("wo", [128, NCH * C], BF16, pst)
        ogt = p.sb("ogt", [128, NCH * TR], BF16, pst)
        y_sb = p.sb("yat", [128, NCH * TR], F32, pst)
        xf = p.sb("xo", [128, NCH * TR], F32, pst)
        stage = p.sb("stg", [128, NCH * TR], F32, pst)
        ln = LNCtx(p, pst, TR)
        yps = [p.ps("yps", [128, TR], F32, pst) for _ in range(2)]
        for k in range(NCH):
            dstep(steps, "gpsimd", lambda e, k=k: e.dma_start(
                out=wo[:, k * C:(k + 1) * C],
                in_=wod[k * 128:(k + 1) * 128, :]), inc=sW)
        wv = sW.n
        for tt in range(TPC // TR):
            t0 = tt * TR
            reuse = [(sMM, sMM.n)]
            for k in range(NCH):
                dstep(steps, "sync", lambda e, k=k, t0=t0: e.dma_start(
                    out=ogt[:, k * TR:(k + 1) * TR],
                    in_=og_loc[k * 128:(k + 1) * 128, t0:t0 + TR]),
                    deps=reuse if k == 0 else (), inc=sX)
            xv = sX.n
            reuse2 = [(ln.sV, ln.sV.n)]
            for k in range(NCH):
                dstep(steps, "sync", lambda e, k=k, t0=t0: e.dma_start(
                    out=xf[:, k * TR:(k + 1) * TR],
                    in_=xin[k * 128:(k + 1) * 128, t0:t0 + TR]),
                    deps=reuse2 if k == 0 else (), inc=sLD)
            ldv = sLD.n
            for oc in range(NCH):
                pb = oc % 2
                for k in range(NCH):
                    dep = []
                    if k == 0:
                        dep = [(sX, xv), (sACT, max(0, sACT.n - 1))]
                        if tt == 0 and oc == 0:
                            dep.append((sW, wv))
                    step(steps, "tensor",
                         lambda e, k=k, oc=oc, pb=pb: e.matmul(
                             yps[pb][:, :],
                             wo[:, k * C + oc * 128:k * C + (oc + 1) * 128],
                             ogt[:, k * TR:(k + 1) * TR],
                             start=(k == 0), stop=(k == NCH - 1)),
                         deps=dep, inc=sMM if k == NCH - 1 else None)
                mmv = sMM.n
                step(steps, "vector",
                     lambda e, oc=oc, pb=pb: e.tensor_scalar_add(
                         y_sb[:, oc * TR:(oc + 1) * TR], yps[pb][:, :],
                         blk["bq"][:, 24 + oc:24 + oc + 1]),
                     deps=[(sMM, mmv), (ln.sV, ln.sV.n)], inc=sACT)
            av = sACT.n
            emit_lnres_store(
                steps, p, cn, ln,
                lambda k: y_sb[:, k * TR:(k + 1) * TR],
                lambda k: xf[:, k * TR:(k + 1) * TR],
                [(sLD, ldv)], xout, t0, sST, stage,
                dep_y=[(sACT, av)])
        wstep(steps, "sync", sST)
        return steps

    p.phase(pfx + "_lng", ph_lng)
    p.phase(pfx + "_qkv", ph_qkv)
    p.phase(pfx + "_att", ph_attn)
    p.phase(pfx + "_oex", ph_oexch)
    p.phase(pfx + "_out", ph_out)
    p.block_end()

    def host_inputs(inputs, core):
        half = core % 2
        wq = np.asarray(inputs["L1_attn_w_qkv"], dtype=np.float32)
        cols = np.concatenate([
            np.arange(half * MYC, (half + 1) * MYC),
            np.arange(C + half * MYC, C + (half + 1) * MYC),
            np.arange(2 * C + half * MYC, 2 * C + (half + 1) * MYC)])
        wqs = np.ascontiguousarray(wq[:, cols])
        bq = np.asarray(inputs["L1_attn_b_qkv"], dtype=np.float32)[cols]
        bqs = np.ascontiguousarray(bq.reshape(24, 128).T)
        wo = np.ascontiguousarray(np.asarray(inputs["L1_attn_w_out"],
                                             dtype=np.float32))
        bo = np.ascontiguousarray(np.asarray(
            inputs["L1_attn_b_out"], np.float32).reshape(NCH, 128).T)
        return {pfx + "_wqkv": wqs, pfx + "_bqkv": bqs, pfx + "_wo": wo,
                pfx + "_bo": bo,
                pfx + "_t0": np.array([[half * TPC]], dtype=np.uint32)}

    return host_inputs


def build_prog_l1attn():
    p = Prog()
    cn = ConstSB(p, need_caus=True)
    cn.alloc()
    xin = p.dram("xin", [C, TPC], F32, "ExternalInput")
    xout = p.dram("xout", [C, TPC], F32, "ExternalOutput")
    hin = phases_l1attn(p, cn, xin, xout, first=True)
    nc = p.build()
    return nc, hin


def run_l1attn(x_shards, inputs):
    nc, hin = _get_prog("l1attn", build_prog_l1attn)
    cmap = ConstSB.host_inputs(need_caus=True)
    in_maps = []
    for c in range(8):
        m = {"xin": x_shards[c]}
        m.update(hin(inputs, c))
        m.update(cmap)
        in_maps.append(m)
    res = run_bass_kernel_spmd(nc, in_maps, core_ids=list(range(8)))
    return [res.results[c]["xout"] for c in range(8)]




def phases_l0attn(p, cn, xin, xout, pfx="l0t", first=False):
    wrd = p.dram(pfx + "_wr", [C, MYC], F32, "ExternalInput")
    wkd = p.dram(pfx + "_wk", [C, MYC], F32, "ExternalInput")
    wvd = p.dram(pfx + "_wv", [C, MYC], F32, "ExternalInput")
    wgd = p.dram(pfx + "_wg", [C, MYC], F32, "ExternalInput")
    wod = p.dram(pfx + "_wo", [C, C], F32, "ExternalInput")
    tmd = p.dram(pfx + "_tm", [128, 4 * NCH], F32, "ExternalInput")
    lnwd = p.dram(pfx + "_lnw", [128, 2 * NCH], F32, "ExternalInput")
    tdd = p.dram(pfx + "_td", [128, MYH], F32, "ExternalInput")
    tfd = p.dram(pfx + "_tf", [128, MYH], F32, "ExternalInput")
    t0d = p.dram(pfx + "_t0", [1, 1], U32, "ExternalInput")
    ag_in = p.dram(pfx + "_agi", [C, TPC], BF16)
    ag_out = p.dram(pfx + "_ago", [2 * C, TPC], BF16)
    regs = {nm: p.dram(pfx + "_" + nm, [MYC, T], BF16)
            for nm in ("rreg", "kreg", "vreg", "greg", "ogreg")}
    og_gath = p.dram(pfx + "_ogg", [2 * MYC, T], BF16)
    og_loc = p.dram(pfx + "_ogl", [C, TPC], BF16)

    blk = {}

    def balloc(bst):
        blk["tm"] = p.sb(pfx + "tm", [128, 4 * NCH], F32, bst)
        blk["lnwb"] = p.sb(pfx + "lw", [128, 2 * NCH], F32, bst)
        blk["lam"] = p.sb(pfx + "lam", [128, MYH], F32, bst)
        blk["ucol"] = p.sb(pfx + "uc", [128, MYH], F32, bst)
        blk["msk"] = p.sb(pfx + "msk", [128, 3 * MYH], F32, bst)
        blk["t0s"] = p.sb(pfx + "t0s", [1, 1], U32, bst)
    p.block_begin(balloc)
    blk["r_t0"] = p.stack.enter_context(p.nc.gpsimd.register(pfx + "rt0"))
    blk["r_tmp"] = p.stack.enter_context(p.nc.gpsimd.register(pfx + "rtm"))
    blk["r_sy"] = p.stack.enter_context(p.nc.sync.register(pfx + "rsyC"))
    blk["r_sy2"] = p.stack.enter_context(p.nc.sync.register(pfx + "rsyD"))

    def ph_lng(pst):
        steps = []
        sLD = p.sem("ld")
        sST = p.sem("st")
        sCC = p.usem("cc")
        sACT = p.sem("ac")
        if first:
            cn.load_steps(steps, sLD)
        dstep(steps, "sync", lambda e: e.dma_start(
            out=blk["tm"][:, :], in_=tmd[:, :]), inc=sLD)
        dstep(steps, "sync", lambda e: e.dma_start(
            out=blk["lnwb"][:, :], in_=lnwd[:, :]), inc=sLD)
        dstep(steps, "sync", lambda e: e.dma_start(
            out=blk["lam"][:, :], in_=tdd[:, :]), inc=sLD)
        dstep(steps, "sync", lambda e: e.dma_start(
            out=blk["ucol"][:, :], in_=tfd[:, :]), inc=sLD)
        dstep(steps, "gpsimd", lambda e: e.dma_start(
            out=blk["t0s"][:, :], in_=t0d[:, :]), inc=sLD)
        ldt = sLD.n
        step(steps, "gpsimd",
             lambda e: e.reg_load(blk["r_t0"], blk["t0s"][0:1, 0:1]),
             deps=[(sLD, ldt)])
        step(steps, "scalar", lambda e: e.activation(
            blk["lam"][:, :], blk["lam"][:, :], AF.Exp),
            deps=[(sLD, ldt)], inc=sACT)
        av = sACT.n
        step(steps, "vector", lambda e: e.tensor_single_scalar(
            blk["msk"][:, 0:MYH], blk["lam"][:, :], 26.0, ALU.is_le),
            deps=[(sACT, av)])
        step(steps, "vector", lambda e: e.tensor_single_scalar(
            blk["msk"][:, MYH:2 * MYH], blk["lam"][:, :], 1.25, ALU.is_le))
        step(steps, "vector", lambda e: e.tensor_single_scalar(
            blk["msk"][:, 2 * MYH:3 * MYH], blk["lam"][:, :], 26.0, ALU.is_gt))
        TL = 256
        ln = LNCtx(p, pst, TL)
        xf = p.sb("xf", [128, NCH * TL], F32, pst)
        xbstage = p.sb("xbs", [128, NCH * TL], BF16, pst)
        xf2 = [xf, p.sb("xfb", [128, NCH * TL], F32, pst)]
        xs2 = [xbstage, p.sb("xbsb", [128, NCH * TL], BF16, pst)]
        lastC = {0: (0, 0, 0), 1: (0, 0, 0)}
        for ti, t0 in enumerate(range(0, TPC, TL)):
            par = ti % 2
            xb, xs = xf2[par], xs2[par]
            reuse = [(ln.sV, lastC[par][0]), (ln.sS, lastC[par][1])]
            for k in range(NCH):
                dstep(steps, "sync", lambda e, k=k, t0=t0, xb=xb: e.dma_start(
                    out=xb[:, k * TL:(k + 1) * TL],
                    in_=xin[k * 128:(k + 1) * 128, t0:t0 + TL]),
                    deps=reuse if k == 0 else (), inc=sLD)
            dep = [(sLD, sLD.n)]
            tS = emit_ln_stats(steps, cn, ln,
                               lambda k, xb=xb: xb[:, k * TL:(k + 1) * TL], dep)
            for k in range(NCH):
                kk = k
                step(steps, "vector",
                     lambda e, kk=kk, xb=xb: e.tensor_sub(
                         ln.tmpn[:, :], xb[:, kk * TL:(kk + 1) * TL],
                         ln.ps_bc[:, 0:TL]),
                     deps=([(ln.sT, tS), (sST, lastC[par][2])]
                           if kk == 0 else ()))
                step(steps, "vector",
                     lambda e, kk=kk, xs=xs: e.tensor_mul(
                         xs[:, kk * TL:(kk + 1) * TL],
                         ln.tmpn[:, :], ln.ps_bc[:, TL:2 * TL]),
                     inc=ln.sV)
                vv = ln.sV.n
                dstep(steps, "sync",
                      lambda e, kk=kk, t0=t0, xs=xs: e.dma_start(
                          out=ag_in[kk * 128:(kk + 1) * 128, t0:t0 + TL],
                          in_=xs[:, kk * TL:(kk + 1) * TL]),
                      deps=[(ln.sV, vv)], inc=sST)
            lastC[par] = (ln.sV.n, ln.sS.n, sST.n)
        wstep(steps, "sync", sST)
        step(steps, "gpsimd", lambda e: e.collective_compute(
            "AllGather", ALU.bypass, replica_groups=GROUPS,
            ins=[ag_in.ap().opt()], outs=[ag_out.ap().opt()]),
            deps=[(sST, sST.n)], inc=sCC)
        steps.append(Step("gpsimd", lambda e: e.nop(), [(sCC, 1)], None, 0))
        return steps

    def ph_mixproj(pst):
        steps = []
        sW = p.sem("w")
        sX = p.sem("x")
        sMM = p.sem("mm")
        sACT = p.sem("act")
        sDVE = p.sem("dve")
        sST = p.sem("st")
        TL = 256
        wsb = {}
        for nm, dh in (("wk", wkd), ("wv", wvd), ("wr", wrd), ("wg", wgd)):
            wsb[nm] = p.sb(nm, [128, NCH * MYC], BF16, pst)
            for k in range(NCH):
                dstep(steps, "gpsimd", lambda e, k=k, dh=dh, nm=nm: e.dma_start(
                    out=wsb[nm][:, k * MYC:(k + 1) * MYC],
                    in_=dh[k * 128:(k + 1) * 128, :]), inc=sW)
        wv_ = sW.n
        xg = p.sb("xg", [128, NCH * TL], BF16, pst)
        sh = p.sb("sh", [128, NCH * TL], BF16, pst)
        dd = p.sb("dd", [128, NCH * TL], BF16, pst)
        mix = p.sb("mix", [128, NCH * TL], BF16, pst)
        st1 = p.sb("st1", [128, TL], BF16, pst)
        st2 = p.sb("st2", [128, TL], BF16, pst)
        pss = [p.ps("mps", [128, TL], F32, pst) for _ in range(2)]
        stages = [st1, st2]
        worder = [("wk", 0, "kreg", False), ("wv", 1, "vreg", False),
                  ("wr", 2, "rreg", False), ("wg", 3, "greg", True)]
        for tt in range(T // TL):
            t0g = tt * TL
            slab, tl = t0g // TPC, t0g % TPC
            dep_x = [(sDVE, sDVE.n)]
            for k in range(NCH):
                dstep(steps, "sync", lambda e, k=k, slab=slab, tl=tl: e.dma_start(
                    out=xg[:, k * TL:(k + 1) * TL],
                    in_=ag_out[slab * C + k * 128:slab * C + (k + 1) * 128,
                               tl:tl + TL]),
                    deps=dep_x if k == 0 else (), inc=sX)
                if t0g == 0:
                    dstep(steps, "sync", lambda e, k=k: e.dma_start(
                        out=sh[:, k * TL + 1:(k + 1) * TL],
                        in_=ag_out[k * 128:(k + 1) * 128, 0:TL - 1]), inc=sX)
                    dstep(steps, "sync", lambda e, k=k: e.dma_start(
                        out=sh[:, k * TL:k * TL + 1],
                        in_=ag_out[k * 128:(k + 1) * 128, 1:2],
                        allow_slow_non_contiguous=True), inc=sX)
                elif tl == 0:
                    dstep(steps, "sync", lambda e, k=k, slab=slab: e.dma_start(
                        out=sh[:, k * TL + 1:(k + 1) * TL],
                        in_=ag_out[slab * C + k * 128:slab * C + (k + 1) * 128,
                                   0:TL - 1]), inc=sX)
                    dstep(steps, "sync", lambda e, k=k, slab=slab: e.dma_start(
                        out=sh[:, k * TL:k * TL + 1],
                        in_=ag_out[(slab - 1) * C + k * 128:
                                   (slab - 1) * C + (k + 1) * 128,
                                   TPC - 1:TPC],
                        allow_slow_non_contiguous=True), inc=sX)
                else:
                    dstep(steps, "sync", lambda e, k=k, slab=slab, tl=tl:
                          e.dma_start(
                              out=sh[:, k * TL:(k + 1) * TL],
                              in_=ag_out[slab * C + k * 128:
                                         slab * C + (k + 1) * 128,
                                         tl - 1:tl + TL - 1]), inc=sX)
            xv = sX.n
            for k in range(NCH):
                step(steps, "vector",
                     lambda e, k=k: e.tensor_sub(dd[:, k * TL:(k + 1) * TL],
                                                 sh[:, k * TL:(k + 1) * TL],
                                                 xg[:, k * TL:(k + 1) * TL]),
                     deps=[(sX, xv)] if k == 0 else (), inc=sDVE)
            for nm, tmi, reg, sig in worder:
                for k in range(NCH):
                    step(steps, "vector",
                         lambda e, k=k, tmi=tmi: e.scalar_tensor_tensor(
                             mix[:, k * TL:(k + 1) * TL],
                             dd[:, k * TL:(k + 1) * TL],
                             blk["tm"][:, tmi * NCH + k:tmi * NCH + k + 1],
                             xg[:, k * TL:(k + 1) * TL],
                             ALU.mult, ALU.add),
                         deps=[(sMM, sMM.n)] if k == 0 else (), inc=sDVE)
                dv = sDVE.n
                for oc in range(MYH):
                    pb = oc % 2
                    for k in range(NCH):
                        dep = []
                        if k == 0:
                            dep = [(sDVE, dv), (sACT, max(0, sACT.n - 1))]
                            if tt == 0 and tmi == 0 and oc == 0:
                                dep.append((sW, wv_))
                        step(steps, "tensor",
                             lambda e, k=k, oc=oc, pb=pb, nm=nm: e.matmul(
                                 pss[pb][:, :],
                                 wsb[nm][:, k * MYC + oc * 128:
                                         k * MYC + (oc + 1) * 128],
                                 mix[:, k * TL:(k + 1) * TL],
                                 start=(k == 0), stop=(k == NCH - 1)),
                             deps=dep, inc=sMM if k == NCH - 1 else None)
                    mmv = sMM.n
                    step(steps, "scalar",
                         lambda e, pb=pb, sig=sig: e.activation(
                             stages[pb][:, :], pss[pb][:, :],
                             AF.Sigmoid if sig else AF.Copy),
                         deps=[(sMM, mmv), (sST, max(0, sST.n - 16))],
                         inc=sACT)
                    av = sACT.n
                    dstep(steps, "sync",
                          lambda e, oc=oc, tt=tt, pb=pb, reg=reg: e.dma_start(
                              out=regs[reg][oc * 128:(oc + 1) * 128,
                                             tt * TL:(tt + 1) * TL],
                              in_=stages[pb][:, :]),
                          deps=[(sACT, av)], inc=sST)
        wstep(steps, "sync", sST)
        return steps

    def ph_wkv(pst):
        steps = []
        sL = p.sem("l")
        sMM = p.sem("mm")
        sACT = p.sem("act")
        sDVE = p.sem("dve")
        sST = p.sem("st")
        sTR = p.sem("tr")
        rf = p.sb("rf", [128, T], BF16, pst)
        kf = p.sb("kf", [128, T], BF16, pst)
        vf = p.sb("vf", [128, T], BF16, pst)
        gf = p.sb("gf", [128, T], BF16, pst)
        vt = p.sb("vt", [128, T], BF16, pst)
        ogs = p.sb("ogs", [128, T], BF16, pst)
        ktf = p.sb("ktf", [128, T], BF16, pst)
        sc_names = ("r8c", "rn8", "r64", "rt", "rDo",
                    "k8c", "kn8", "k64", "kDe", "ku")
        sct = {nm: p.sb("sc" + nm, [128, TT], BF16, pst) for nm in sc_names}
        tabs = p.sb("tabs", [128, 8 * 128], BF16, pst)
        tpar = p.sb("tpar", [128, 2 * 128], BF16, pst)
        fcol = p.sb("fcol", [128, 8], F32, pst)
        ATc = p.sb("ATc", [128, 128], BF16, pst)
        ATp = p.sb("ATp", [128, 128], BF16, pst)
        pp = [p.ps("pp", [128, 128], F32, pst) for _ in range(2)]
        ps_o = p.ps("po", [128, 128], F32, pst)
        ps_tr = p.ps("ptr", [128, 128], BF16, pst)

        TAB = {"r8c": 0, "k8c": 1, "rn8": 2, "kn8": 3,
               "r64": 4, "k64": 5, "rt": 6, "kt": 7}

        for hl in range(MYH):
            dep_pre = [(sACT, sACT.n), (sMM, sMM.n)]
            step(steps, "vector", lambda e: e.memset(fcol[:, 0:1], 0.0),
                 deps=dep_pre)
            step(steps, "vector", lambda e, hl=hl: e.tensor_scalar(
                fcol[:, 1:2], blk["lam"][:, hl:hl + 1],
                blk["msk"][:, hl:hl + 1], -1.0, ALU.mult, ALU.mult))
            step(steps, "vector", lambda e, hl=hl: e.tensor_scalar(
                fcol[:, 2:3], blk["lam"][:, hl:hl + 1],
                blk["msk"][:, MYH + hl:MYH + hl + 1], -1.0,
                ALU.mult, ALU.mult))
            step(steps, "vector", lambda e, hl=hl: e.tensor_scalar_mul(
                fcol[:, 3:4], blk["lam"][:, hl:hl + 1], -1.0))
            step(steps, "vector", lambda e, hl=hl: e.tensor_scalar(
                fcol[:, 4:5], blk["msk"][:, hl:hl + 1], -1.0, 200.0,
                ALU.add, ALU.mult))
            step(steps, "vector", lambda e, hl=hl: e.tensor_scalar(
                fcol[:, 5:6], blk["msk"][:, MYH + hl:MYH + hl + 1],
                -1.0, 200.0, ALU.add, ALU.mult), inc=sDVE)
            fv = sDVE.n
            tspec = [("r8c", 1, 4), ("k8c", 1, 4), ("rn8", 3, 0), ("kn8", 3, 0),
                     ("r64", 2, 5), ("k64", 2, 5), ("rt", 3, 0), ("kt", 3, 0)]
            for nm, sci, bi in tspec:
                j = TAB[nm]
                step(steps, "scalar",
                     lambda e, j=j, sci=sci, bi=bi: e.activation(
                         tabs[:, j * 128:(j + 1) * 128],
                         cn.wkvp[:, j * 128:(j + 1) * 128], AF.Exp,
                         scale=fcol[:, sci:sci + 1], bias=fcol[:, bi:bi + 1]),
                     deps=[(sDVE, fv)], inc=sACT)
            step(steps, "vector", lambda e, hl=hl: e.tensor_scalar_mul(
                tpar[:, 0:128], cn.par[:, 0:128],
                blk["msk"][:, 2 * MYH + hl:2 * MYH + hl + 1]),
                deps=[(sACT, sACT.n)])
            step(steps, "vector", lambda e, hl=hl: e.tensor_scalar_mul(
                tpar[:, 128:256], cn.par[:, 128:256],
                blk["msk"][:, 2 * MYH + hl:2 * MYH + hl + 1]), inc=sDVE)
            tabv = sACT.n
            dep_re = [(sMM, sMM.n), (sDVE, sDVE.n)]
            for nm, reg in (("rf", "rreg"), ("kf", "kreg"),
                            ("vf", "vreg"), ("gf", "greg")):
                tgt = {"rf": rf, "kf": kf, "vf": vf, "gf": gf}[nm]
                dstep(steps, "sync", lambda e, tgt=tgt, reg=reg, hl=hl:
                      e.dma_start(out=tgt[:, :],
                                  in_=regs[reg][hl * 128:(hl + 1) * 128, :]),
                      deps=dep_re if nm == "rf" else (), inc=sL)
            lv = sL.n
            step(steps, "vector", lambda e: e.tensor_mul(
                ktf[:, :], kf[:, :],
                bass.AP(tabs, TAB["kt"] * 128,
                        [[8 * 128, 128], [0, T // 128], [1, 128]])),
                deps=[(sL, lv), (sACT, tabv), (sMM, sMM.n)], inc=sDVE)
            for w in range(T // 128):
                step(steps, "tensor",
                     lambda e, w=w: e.transpose(ps_tr[:, :],
                                                vf[:, w * 128:(w + 1) * 128],
                                                cn.ident[:, :]),
                     deps=([(sL, lv)] if w == 0 else []) + [(sDVE, sDVE.n)],
                     inc=sTR)
                tv = sTR.n
                step(steps, "vector",
                     lambda e, w=w: e.tensor_copy(vt[:, w * 128:(w + 1) * 128],
                                                  ps_tr[:, :]),
                     deps=[(sTR, tv)], inc=sDVE)
            for ts4 in range(T // TT):
                tb = ts4 * TT
                sspec = [("r8c", rf, "r8c"), ("rn8", rf, "rn8"),
                         ("r64", rf, "r64"), ("rt", rf, "rt"),
                         ("k8c", kf, "k8c"), ("kn8", kf, "kn8"),
                         ("k64", kf, "k64")]
                dep_sc = [(sMM, sMM.n)]
                for nm, srcb, tb_nm in sspec:
                    j = TAB[tb_nm]
                    step(steps, "vector",
                         lambda e, nm=nm, srcb=srcb, j=j, tb=tb: e.tensor_mul(
                             sct[nm][:, :], srcb[:, tb:tb + TT],
                             bass.AP(tabs, j * 128,
                                     [[8 * 128, 128], [0, 4], [1, 128]])),
                         deps=dep_sc if nm == "r8c" else ())
                step(steps, "vector",
                     lambda e, tb=tb: e.tensor_mul(
                         sct["rDo"][:, :], rf[:, tb:tb + TT],
                         bass.AP(tpar, 0, [[2 * 128, 128], [0, 4], [1, 128]])))
                step(steps, "vector",
                     lambda e, tb=tb: e.tensor_mul(
                         sct["kDe"][:, :], kf[:, tb:tb + TT],
                         bass.AP(tpar, 128, [[2 * 128, 128], [0, 4], [1, 128]])))
                step(steps, "vector",
                     lambda e, tb=tb, hl=hl: e.tensor_scalar_mul(
                         sct["ku"][:, :], kf[:, tb:tb + TT],
                         blk["ucol"][:, hl:hl + 1]), inc=sDVE)
                scv = sDVE.n
                for wl in range(4):
                    w = ts4 * 4 + wl
                    c0, c1 = wl * 128, (wl + 1) * 128
                    step(steps, "vector", lambda e: e.memset(ATc[:, :], 0.0),
                         deps=[(sMM, sMM.n)], inc=sDVE)
                    step(steps, "tensor",
                         lambda e, c0=c0, c1=c1: e.matmul(
                             pp[0][:, :], sct["k8c"][:, c0:c1],
                             sct["r8c"][:, c0:c1], start=True, stop=True),
                         deps=[(sDVE, sDVE.n)], inc=sMM)
                    mv = sMM.n
                    step(steps, "vector", lambda e: e.select(
                        ATc[:, :], cn.wmask[:, 0:128], pp[0][:, :], ATc[:, :]),
                        deps=[(sMM, mv)], inc=sDVE)
                    dv = sDVE.n
                    step(steps, "tensor",
                         lambda e, c0=c0, c1=c1: e.matmul(
                             pp[0][:, :], sct["kDe"][:, c0:c1],
                             sct["rDo"][:, c0:c1], start=False, stop=True),
                         deps=[(sDVE, dv)], inc=sMM)
                    mv = sMM.n
                    step(steps, "vector", lambda e: e.select(
                        ATc[:, :], cn.wmask[:, 128:256], pp[0][:, :], ATc[:, :]),
                        deps=[(sMM, mv)], inc=sDVE)
                    step(steps, "tensor",
                         lambda e, c0=c0, c1=c1: e.matmul(
                             pp[1][:, :], sct["kn8"][:, c0:c1],
                             sct["rn8"][:, c0:c1], start=True, stop=True),
                         deps=[(sDVE, sDVE.n)], inc=sMM)
                    mv = sMM.n
                    step(steps, "vector", lambda e: e.select(
                        ATc[:, :], cn.wmask[:, 256:384], pp[1][:, :], ATc[:, :]),
                        deps=[(sMM, mv)], inc=sDVE)
                    step(steps, "tensor",
                         lambda e, c0=c0, c1=c1: e.matmul(
                             pp[0][:, :], sct["k64"][:, c0:c1],
                             sct["r64"][:, c0:c1], start=True, stop=True),
                         deps=[(sDVE, sDVE.n)], inc=sMM)
                    mv = sMM.n
                    step(steps, "vector", lambda e: e.select(
                        ATc[:, :], cn.wmask[:, 384:512], pp[0][:, :], ATc[:, :]),
                        deps=[(sMM, mv)], inc=sDVE)
                    step(steps, "tensor",
                         lambda e, c0=c0, c1=c1, tb=tb: e.matmul(
                             pp[1][:, :], sct["ku"][:, c0:c1],
                             rf[:, tb + c0:tb + c1], start=True, stop=True),
                         deps=[(sDVE, sDVE.n)], inc=sMM)
                    mv = sMM.n
                    step(steps, "vector", lambda e: e.select(
                        ATc[:, :], cn.wmask[:, 512:640], pp[1][:, :], ATc[:, :]),
                        deps=[(sMM, mv)], inc=sDVE)
                    atv = sDVE.n
                    if w > 0:
                        step(steps, "tensor",
                             lambda e, w=w, c0=c0, c1=c1: e.matmul(
                                 pp[0][:, :],
                                 ktf[:, (w - 1) * 128:w * 128],
                                 sct["rt"][:, c0:c1], start=True, stop=True),
                             deps=[(sDVE, sDVE.n)], inc=sMM)
                        mv = sMM.n
                        step(steps, "vector", lambda e: e.tensor_copy(
                            ATp[:, :], pp[0][:, :]),
                            deps=[(sMM, mv)], inc=sDVE)
                        apv = sDVE.n
                    step(steps, "tensor",
                         lambda e, w=w: e.matmul(
                             ps_o[:, :], vt[:, w * 128:(w + 1) * 128],
                             ATc[:, :], start=True, stop=(w == 0)),
                         deps=[(sDVE, sDVE.n)],
                         inc=sMM if w == 0 else None)
                    if w > 0:
                        step(steps, "tensor",
                             lambda e, w=w: e.matmul(
                                 ps_o[:, :], vt[:, (w - 1) * 128:w * 128],
                                 ATp[:, :], start=False, stop=True),
                             inc=sMM)
                    mv = sMM.n
                    step(steps, "vector",
                         lambda e, w=w, tb=tb, c0=c0, c1=c1: e.tensor_mul(
                             ogs[:, w * 128:(w + 1) * 128], ps_o[:, :],
                             gf[:, tb + c0:tb + c1]),
                         deps=[(sMM, mv), (sST, sST.n)], inc=sDVE)
            dv = sDVE.n
            dstep(steps, "sync", lambda e, hl=hl: e.dma_start(
                out=regs["ogreg"][hl * 128:(hl + 1) * 128, :], in_=ogs[:, :]),
                deps=[(sDVE, dv)], inc=sST)
        wstep(steps, "sync", sST)
        return steps

    def ph_oexch(pst):
        steps = []
        sCC = p.usem("cc")
        sCP = p.sem("cp")
        step(steps, "gpsimd", lambda e: e.collective_compute(
            "AllGather", ALU.bypass, replica_groups=GROUPS,
            ins=[regs["ogreg"].ap().opt()], outs=[og_gath.ap().opt()]),
            inc=sCC)
        steps.append(Step("gpsimd", lambda e: e.nop(), [(sCC, 1)], None, 0))
        def ldreg(e):
            e.reg_load(blk["r_sy"], blk["t0s"][0:1, 0:1])
            return e.nop()
        steps.append(Step("sync", ldreg, [(sCC, 1)], None, 0))
        for i in range(8):
            def cp(e, i=i):
                e.reg_mov(blk["r_sy2"], i * 256 * T)
                e.reg_add(blk["r_sy2"], blk["r_sy2"], blk["r_sy"])
                return e.dma_start(
                    out=og_loc[i * 256:(i + 1) * 256, :],
                    in_=bass.AP(og_gath, blk["r_sy2"], [[T, 256], [1, TPC]]))
            dstep(steps, "sync", cp, inc=sCP)
        wstep(steps, "sync", sCP)
        return steps

    def ph_out(pst):
        steps = []
        sW = p.sem("w")
        sX = p.sem("x")
        sMM = p.sem("mm")
        sACT = p.sem("act")
        sLD = p.sem("lx")
        sST = p.sem("sx")
        sZ = p.sem("z")
        TR = 256
        wo = p.sb("wo", [128, NCH * C], BF16, pst)
        ogt = p.sb("ogt", [128, NCH * TR], BF16, pst)
        y_sb = p.sb("yat", [128, NCH * TR], F32, pst)
        z_sb = p.sb("zat", [128, NCH * TR], F32, pst)
        xf = p.sb("xo", [128, NCH * TR], F32, pst)
        stage = p.sb("stg", [128, NCH * TR], F32, pst)
        ln = LNCtx(p, pst, TR)
        yps = [p.ps("yps", [128, TR], F32, pst) for _ in range(2)]
        for k in range(NCH):
            dstep(steps, "gpsimd", lambda e, k=k: e.dma_start(
                out=wo[:, k * C:(k + 1) * C],
                in_=wod[k * 128:(k + 1) * 128, :]), inc=sW)
        wv = sW.n
        for tt in range(TPC // TR):
            t0 = tt * TR
            reuse = [(sMM, sMM.n)]
            for k in range(NCH):
                dstep(steps, "sync", lambda e, k=k, t0=t0: e.dma_start(
                    out=ogt[:, k * TR:(k + 1) * TR],
                    in_=og_loc[k * 128:(k + 1) * 128, t0:t0 + TR]),
                    deps=reuse if k == 0 else (), inc=sX)
            xv = sX.n
            reuse2 = [(ln.sV, ln.sV.n)]
            for k in range(NCH):
                dstep(steps, "sync", lambda e, k=k, t0=t0: e.dma_start(
                    out=xf[:, k * TR:(k + 1) * TR],
                    in_=xin[k * 128:(k + 1) * 128, t0:t0 + TR]),
                    deps=reuse2 if k == 0 else (), inc=sLD)
            ldv = sLD.n
            for oc in range(NCH):
                pb = oc % 2
                for k in range(NCH):
                    dep = []
                    if k == 0:
                        dep = [(sX, xv), (sACT, max(0, sACT.n - 1))]
                        if tt == 0 and oc == 0:
                            dep.append((sW, wv))
                    step(steps, "tensor",
                         lambda e, k=k, oc=oc, pb=pb: e.matmul(
                             yps[pb][:, :],
                             wo[:, k * C + oc * 128:k * C + (oc + 1) * 128],
                             ogt[:, k * TR:(k + 1) * TR],
                             start=(k == 0), stop=(k == NCH - 1)),
                         deps=dep, inc=sMM if k == NCH - 1 else None)
                mmv = sMM.n
                step(steps, "scalar",
                     lambda e, oc=oc, pb=pb: e.copy(
                         y_sb[:, oc * TR:(oc + 1) * TR], yps[pb][:, :]),
                     deps=[(sMM, mmv), (sZ, sZ.n)], inc=sACT)
            av = sACT.n
            tS = emit_ln_stats(steps, cn, ln,
                               lambda k: y_sb[:, k * TR:(k + 1) * TR],
                               [(sACT, av)])
            for k in range(NCH):
                kk = k
                step(steps, "vector",
                     lambda e, kk=kk: e.tensor_sub(ln.tmpn[:, :],
                                                   y_sb[:, kk * TR:(kk + 1) * TR],
                                                   ln.ps_bc[:, 0:TR]),
                     deps=[(ln.sT, tS)] if kk == 0 else ())
                step(steps, "vector",
                     lambda e, kk=kk: e.tensor_mul(ln.tmpn[:, :], ln.tmpn[:, :],
                                                   ln.ps_bc[:, TR:2 * TR]))
                step(steps, "vector",
                     lambda e, kk=kk: e.tensor_scalar(
                         z_sb[:, kk * TR:(kk + 1) * TR], ln.tmpn[:, :],
                         blk["lnwb"][:, kk:kk + 1],
                         blk["lnwb"][:, NCH + kk:NCH + kk + 1],
                         ALU.mult, ALU.add),
                     inc=sZ)
            zv = sZ.n
            emit_lnres_store(
                steps, p, cn, ln,
                lambda k: z_sb[:, k * TR:(k + 1) * TR],
                lambda k: xf[:, k * TR:(k + 1) * TR],
                [(sLD, ldv)], xout, t0, sST, stage,
                dep_y=[(sZ, zv)])
        wstep(steps, "sync", sST)
        return steps

    p.phase(pfx + "_lng", ph_lng)
    p.phase(pfx + "_mix", ph_mixproj)
    p.phase(pfx + "_wkv", ph_wkv)
    p.phase(pfx + "_oex", ph_oexch)
    p.phase(pfx + "_out", ph_out)
    p.block_end()

    def host_inputs(inputs, core):
        half = core % 2
        cols = np.arange(half * MYC, (half + 1) * MYC)
        out = {}
        for nm, key in (("_wr", "L0_attn_Wr"), ("_wk", "L0_attn_Wk"),
                        ("_wv", "L0_attn_Wv"), ("_wg", "L0_attn_Wg")):
            w = np.asarray(inputs[key], dtype=np.float32)
            out[pfx + nm] = np.ascontiguousarray(w.T[:, cols])
        out[pfx + "_wo"] = np.ascontiguousarray(
            np.asarray(inputs["L0_attn_Wo"], dtype=np.float32).T)
        tm = [np.asarray(inputs[k], np.float32).reshape(NCH, 128).T
              for k in ("L0_attn_tm_k", "L0_attn_tm_v",
                        "L0_attn_tm_r", "L0_attn_tm_g")]
        out[pfx + "_tm"] = np.ascontiguousarray(np.concatenate(tm, axis=1))
        lnw = np.asarray(inputs["L0_attn_ln_w"], np.float32).reshape(NCH, 128).T
        lnb = np.asarray(inputs["L0_attn_ln_b"], np.float32).reshape(NCH, 128).T
        out[pfx + "_lnw"] = np.ascontiguousarray(
            np.concatenate([lnw, lnb], axis=1))
        td = np.asarray(inputs["L0_attn_time_decay"], np.float32)
        tf = np.asarray(inputs["L0_attn_time_first"], np.float32)
        hs = slice(half * MYH, (half + 1) * MYH)
        out[pfx + "_td"] = np.ascontiguousarray(td[hs].T)
        out[pfx + "_tf"] = np.ascontiguousarray(tf[hs].T)
        out[pfx + "_t0"] = np.array([[half * TPC]], dtype=np.uint32)
        return out

    return host_inputs


def build_prog_l0attn():
    p = Prog()
    cn = ConstSB(p, need_wkv=True)
    cn.alloc()
    xin = p.dram("xin", [C, TPC], F32, "ExternalInput")
    xout = p.dram("xout", [C, TPC], F32, "ExternalOutput")
    hin = phases_l0attn(p, cn, xin, xout, first=True)
    nc = p.build()
    return nc, hin


def run_l0attn(x_shards, inputs):
    nc, hin = _get_prog("l0attn", build_prog_l0attn)
    cmap = ConstSB.host_inputs(need_wkv=True)
    in_maps = []
    for c in range(8):
        m = {"xin": x_shards[c]}
        m.update(hin(inputs, c))
        m.update(cmap)
        in_maps.append(m)
    res = run_bass_kernel_spmd(nc, in_maps, core_ids=list(range(8)))
    return [res.results[c]["xout"] for c in range(8)]




def build_prog_full():
    p = Prog()
    cn = ConstSB(p, need_wkv=True, need_caus=True)
    cn.alloc()
    x0 = p.dram("xin", [C, TPC], F32, "ExternalInput")
    x1 = p.dram("x1", [C, TPC], F32)
    x2 = p.dram("x2", [C, TPC], F32)
    x3 = p.dram("x3", [C, TPC], F32)
    x4 = p.dram("xout", [C, TPC], F32, "ExternalOutput")
    h0 = phases_l0attn(p, cn, x0, x1, first=True)
    h1 = phases_l0ffn(p, cn, x1, x2)
    h2 = phases_l1attn(p, cn, x2, x3)
    h3 = phases_l1ffn(p, cn, x3, x4)
    nc = p.build()

    def host_inputs(inputs, core):
        m = {}
        m.update(h0(inputs, core))
        m.update(h1(inputs))
        m.update(h2(inputs, core))
        m.update(h3(inputs))
        return m
    return nc, host_inputs


FUSED = True


def kernel(**inputs):
    x = [_fm_shard(np.asarray(inputs["L0_attn_x"]), c) for c in range(8)]
    if FUSED:
        nc, hin = _get_prog("full", build_prog_full)
        cmap = ConstSB.host_inputs(need_wkv=True, need_caus=True)
        in_maps = []
        for c in range(8):
            m = {"xin": x[c]}
            m.update(hin(inputs, c))
            m.update(cmap)
            in_maps.append(m)
        res = run_bass_kernel_spmd(nc, in_maps, core_ids=list(range(8)))
        x = [res.results[c]["xout"] for c in range(8)]
    else:
        x = run_l0attn(x, inputs)
        x = run_l0ffn(x, inputs)
        x = run_l1attn(x, inputs)
        x = run_l1ffn(x, inputs)
    return _unshard(x)
